# version 15
# baseline (speedup 1.0000x reference)
import numpy as np
import ml_dtypes
from contextlib import ExitStack

import concourse.bass as bass
import concourse.mybir as mybir
from concourse.bass_utils import run_bass_kernel_spmd

F32 = mybir.dt.float32
BF16 = mybir.dt.bfloat16
I32 = mybir.dt.int32
Alu = mybir.AluOpType
Act = mybir.ActivationFunctionType
AX = mybir.AxisListType.X

B, T, D = 4, 4096, 1024
N16 = 16
DFF1 = 4096
DFF2 = 2816
NCORE = 8
TOK = 2048
STRIPE = 512
NSTR = 4
DB = 8
MC1 = 32
NC2 = 22
LNEPS = 1e-5
RMSEPS = 1e-6

OFF_W1 = 0
SZ_W1 = 1024 * 512
OFF_W2 = OFF_W1 + SZ_W1
SZ_W2 = 4096 * 128
OFF_WG = OFF_W2 + SZ_W2
SZ_WG = 1024 * 384
OFF_WU = OFF_WG + SZ_WG
SZ_WU = 1024 * 384
OFF_WD = OFF_WU + SZ_WU
SZ_WD = 2816 * 128
WSH = OFF_WD + SZ_WD

STAGE = 3


def build_nc():
    nc = bass.Bass("TRN2", target_bir_lowering=False)

    d_x = nc.dram_tensor("x_own", [TOK, D], F32, kind="ExternalInput")
    d_A = nc.dram_tensor("A_sl", [128, N16], F32, kind="ExternalInput")
    d_B = nc.dram_tensor("B_sl", [128, N16], F32, kind="ExternalInput")
    d_C = nc.dram_tensor("C_sl", [128, N16], F32, kind="ExternalInput")
    d_dt = nc.dram_tensor("dt_mat", [128, N16], F32, kind="ExternalInput")
    d_Dfm = nc.dram_tensor("D_fm", [128, DB], F32, kind="ExternalInput")
    d_s1 = nc.dram_tensor("s1_fm", [128, DB], F32, kind="ExternalInput")
    d_s2 = nc.dram_tensor("s2_fm", [128, DB], F32, kind="ExternalInput")
    d_b2 = nc.dram_tensor("b2_fm", [128, DB], F32, kind="ExternalInput")
    d_b1 = nc.dram_tensor("b1_fm", [128, MC1], F32, kind="ExternalInput")
    d_cT = nc.dram_tensor("cT_fm", [128, DB, B], F32, kind="ExternalInput")
    d_wmod = nc.dram_tensor("wmod_sl", [D, 768], F32, kind="ExternalInput")
    d_sel = nc.dram_tensor("sel", [128, B], F32, kind="ExternalInput")
    d_w1s = nc.dram_tensor("w1_sh", [D, 512], F32, kind="ExternalInput")
    d_w2s = nc.dram_tensor("w2_sh", [DFF1, 128], F32, kind="ExternalInput")
    d_wgs = nc.dram_tensor("wg_sh", [384, D], F32, kind="ExternalInput")
    d_wus = nc.dram_tensor("wu_sh", [384, D], F32, kind="ExternalInput")
    d_wds = nc.dram_tensor("wd_sh", [128, DFF2], F32, kind="ExternalInput")
    d_id32 = nc.dram_tensor("ident", [128, 128], F32, kind="ExternalInput")
    d_idbf = nc.dram_tensor("identb", [128, 128], BF16, kind="ExternalInput")
    d_onc = nc.dram_tensor("ones_col", [128, 1], F32, kind="ExternalInput")
    d_onr = nc.dram_tensor("ones_row", [1, 128], F32, kind="ExternalInput")
    d_offs = nc.dram_tensor("offs", [1, 8], I32, kind="ExternalInput")
    d_out = nc.dram_tensor("out_own", [TOK, D], F32, kind="ExternalOutput")

    h_xfm = nc.dram_tensor("h_xfm", [DB, 128, TOK], F32, kind="Internal")
    h_p = nc.dram_tensor("h_p", [DB, 128, TOK], BF16, kind="Internal")
    ag_u_in = nc.dram_tensor("ag_u_in", [NSTR, DB, 128, STRIPE], BF16, kind="Internal")
    ag_u_out = nc.dram_tensor("ag_u_out", [NSTR, NCORE, DB, 128, STRIPE], BF16, kind="Internal")
    ag_y_in = nc.dram_tensor("ag_y_in", [NSTR, 128, B, 1024], BF16, kind="Internal")
    ag_y_out = nc.dram_tensor("ag_y_out", [NSTR, NCORE, 128, B, 1024], BF16, kind="Internal")
    ag_w_in = nc.dram_tensor("ag_w_in", [WSH], BF16, kind="Internal")
    ag_w_out = nc.dram_tensor("ag_w_out", [NCORE, WSH], BF16, kind="Internal")
    ag_m_in = nc.dram_tensor("ag_m_in", [6, 128, B], F32, kind="Internal")
    ag_m_out = nc.dram_tensor("ag_m_out", [NCORE, 6, 128, B], F32, kind="Internal")

    RG = [list(range(NCORE))]
    es = ExitStack()

    def sbuf(name, shape, dtype):
        return es.enter_context(nc.sbuf_tensor(name, shape, dtype))

    C, SEM = {}, {}
    for n in ["sQ", "sQS", "sU", "sY", "sOa", "sOb", "sQ2", "sQ2b", "sQ3", "sQW",
              "sQP", "sQXa", "sQXb",
              "sW1a", "sW1b", "sW2a", "sW2b", "sW2c", "sW2d",
              "sGUa", "sGUb", "sWDa", "sWDb", "sXC", "sG", "sYb",
              "sPE", "sV", "sS", "sAGu", "sAGy", "sAGw", "sAGm"]:
        SEM[n] = es.enter_context(nc.semaphore(n))
        C[n] = 0

    def inc(instr, name, k=1):
        instr.then_inc(SEM[name], k)
        C[name] += k
        return C[name]

    def dmx(eng, name, out_ap, in_ap):
        ins = eng.dma_start(out=out_ap, in_=in_ap)
        ins.then_inc(SEM[name], 16)
        C[name] += 16
        return C[name]

    V = {}

    s_id32 = sbuf("s_id32", [128, 128], F32)
    s_idbf = sbuf("s_idbf", [128, 128], BF16)
    s_onc = sbuf("s_onc", [128, 1], F32)
    s_onr = sbuf("s_onr", [1, 128], F32)
    s_oncb = sbuf("s_oncb", [128, 1], BF16)
    s_A = sbuf("s_A", [128, N16], F32)
    s_Bm = sbuf("s_Bm", [128, N16], F32)
    s_Cm = sbuf("s_Cm", [128, N16], F32)
    s_dt = sbuf("s_dt", [128, N16], F32)
    s_dA = sbuf("s_dA", [128, N16], F32)
    s_CdB = sbuf("s_CdB", [128, N16], F32)
    s_t16a = sbuf("s_t16a", [128, N16], F32)
    s_t16b = sbuf("s_t16b", [128, N16], F32)
    s_Dfm = sbuf("s_Dfm", [128, DB], F32)
    s_s1 = sbuf("s_s1", [128, DB], F32)
    s_s2 = sbuf("s_s2", [128, DB], F32)
    s_b2 = sbuf("s_b2", [128, DB], F32)
    s_b1 = sbuf("s_b1", [128, MC1], F32)
    s_sel = sbuf("s_sel", [128, B], F32)
    s_cT = sbuf("s_cT", [128, DB, B], F32)
    s_cTb = sbuf("s_cTb", [128, DB, B], F32)
    s_modsl = sbuf("s_modsl", [128, 6, B], F32)
    s_modal = sbuf("s_modal", [128, 48, B], F32)
    s_mod = sbuf("s_mod", [128, 48], F32)
    s_ca1 = sbuf("s_ca1", [128, DB], F32)
    s_sa = sbuf("s_sa", [128, DB], F32)
    s_ga = sbuf("s_ga", [128, DB], F32)
    s_cb1 = sbuf("s_cb1", [128, DB], F32)
    s_sbv = sbuf("s_sbv", [128, DB], F32)
    s_gb = sbuf("s_gb", [128, DB], F32)
    s_gtmp = sbuf("s_gtmp", [128, DB], F32)
    s_gtmp2 = sbuf("s_gtmp2", [128, DB], F32)
    s_offs = sbuf("s_offs", [1, 8], I32)
    s_state = sbuf("s_state", [128, B * N16], F32)

    a1 = sbuf("a1", [128, 32, 512], BF16)
    a2 = sbuf("a2", [128, 16, 512], BF16)
    F1 = sbuf("F1", [128, 4096], F32)
    F2 = sbuf("F2", [128, 4096], F32)
    F3 = sbuf("F3", [128, 3072], F32)


    sh_w = F1[:]
    sh_b = a1[:, 0:8, :].rearrange("p a b -> p (a b)")
    sh_f1 = F2[:, 0:3072]
    sh_f2 = F3[:, 0:3072]
    sh_i1 = a1[:, 16:22, :].rearrange("p a b -> p (a b)")
    sh_t = a1[:, 24:30, :].rearrange("p a b -> p (a b)").rearrange(
        "p (i m) -> p i m", m=128)

    s_xtm = [F3[:, 0:1024], F3[:, 1024:2048]]
    s_sq = sbuf("s_sq", [128, DB, STRIPE], BF16)
    s_t1 = s_sq
    s_xfm = sbuf("s_xfm", [128, DB, STRIPE], F32)
    sh_m = F3[:, 2560:2816]
    sh_i = F3[:, 2816:3072]
    s_h1 = a2[:, 0:8, :]
    s_u = a2[:, 8:16, :]
    s_pch = a1[:, 8:16, :]
    s_row = sbuf("s_row", [1, STRIPE], F32)
    s_row2 = sbuf("s_row2", [1, STRIPE], F32)

    s_ug2 = sbuf("s_ug2", [128, 2, DB, STRIPE], BF16)
    s_ug = [s_ug2[:, 0], s_ug2[:, 1]]
    s_z = [F2[:, 2048:3072], F2[:, 3072:4096]]
    s_yac2 = sbuf("s_yac2", [128, 2, 1024], F32)
    s_yst2 = sbuf("s_yst2", [128, 2048], BF16)
    s_zc = sbuf("s_zc", [128, 2, 1024], BF16)

    s_yin = F3[:, 0:2048].bitcast(BF16).rearrange("p (a b) -> p a b", b=STRIPE)
    s_pin = a1[:, 22:30, :]
    s_xp = sbuf("s_xp", [128, DB, STRIPE], BF16)
    s_h2 = a2[:, 0:8, :]
    s_g = a1[:]
    s_xch = s_xfm
    s_x2 = a2[:, 8:16, :]
    s_h3 = a1[:, 22:30, :]
    s_xq = a1[:, 22:30, :]
    s_hq = a1[:, 0:22, :]
    s_x3 = F1[:].rearrange("p (a b) -> p a b", b=STRIPE)
    s_otm = [F2[:, 0:1024], F2[:, 1024:2048]]
    s_sg2 = sbuf("s_sg2", [128, 2 * STRIPE], BF16)

    s_w1t = [sbuf(f"s_w1t{i}", [128, DB, 128], BF16) for i in range(2)]
    s_w2t = [sbuf(f"s_w2t{i}", [128, 4, 128], BF16) for i in range(4)]
    s_wgt = [sbuf(f"s_wgt{i}", [128, DB, 128], BF16) for i in range(2)]
    s_wut = [sbuf(f"s_wut{i}", [128, DB, 128], BF16) for i in range(2)]
    s_wdt = [sbuf(f"s_wdt{i}", [128, NC2, 128], BF16) for i in range(2)]

    QW = 3072
    q_vt = sbuf("q_vt", [128, QW], BF16)
    q_sq = sbuf("q_sq", [128, QW], BF16)
    q_q1 = s_xp[:, 0:8, :].rearrange("p a b -> p (a b)")
    q_h5 = s_sq[:, 0:8, :].rearrange("p a b -> p (a b)")
    q_u = a2[:, 0:8, :].rearrange("p a b -> p (a b)")
    q_m = F3[:, 2048:2048 + QW // 16]
    q_in = F3[:, 2304:2304 + QW // 16]

    PG1 = es.enter_context(nc.psum_tensor("pg1", [128, 2048], F32))
    PG2 = es.enter_context(nc.psum_tensor("pg2", [128, 2048], F32))
    PA = PG1[:, 0:512]
    PX = PG1[:, 512:1024]
    PBK = PG1[:, 1024:1536]
    PY = PG1[:, 1536:2048]
    PACC = [PG2[:, 512 * i:512 * i + 512] for i in range(4)]

    blk = es.enter_context(nc.Block())

    r_u = [es.enter_context(nc.sync.register(f"r_u{k}")) for k in range(NSTR)]
    r_y = [es.enter_context(nc.sync.register(f"r_y{k}")) for k in range(NSTR)]

    def vnop(v, name="sV"):
        return inc(v.drain(), name)

    @blk.sync
    def _(sy):
        for d, s in [(d_id32, s_id32), (d_idbf, s_idbf), (d_onc, s_onc),
                     (d_onr, s_onr), (d_A, s_A), (d_B, s_Bm), (d_C, s_Cm),
                     (d_dt, s_dt), (d_Dfm, s_Dfm), (d_s1, s_s1), (d_s2, s_s2),
                     (d_b2, s_b2), (d_b1, s_b1), (d_sel, s_sel), (d_cT, s_cT),
                     (d_offs, s_offs)]:
            dmx(sy, "sQ", s[:], d[:])
        sy.wait_ge(SEM["sQ"], C["sQ"])
        for k in range(NSTR):
            sy.reg_load(r_u[k], s_offs[0:1, k:k + 1])
            sy.reg_load(r_y[k], s_offs[0:1, 4 + k:5 + k])
    V["q_small"] = C["sQ"]

    @blk.scalar
    def _(s):
        s.wait_ge(SEM["sQ"], V["q_small"])
        inc(s.activation(s_t16a[:], s_A[:], Act.Exp), "sS")
    V["s_expA"] = C["sS"]

    @blk.vector
    def _(v):
        v.wait_ge(SEM["sS"], V["s_expA"])
        v.tensor_tensor(s_t16b[:], s_t16a[:], s_dt[:], Alu.mult)
        inc(v.reciprocal(s_dA[:], s_t16a[:]), "sV")
    V["v_rec"] = C["sV"]

    @blk.scalar
    def _(s):
        s.wait_ge(SEM["sV"], V["v_rec"])
        inc(s.activation(s_t16a[:], s_t16b[:], Act.Exp, scale=-1.0), "sS")
    V["s_dA"] = C["sS"]

    @blk.vector
    def _(v):
        v.wait_ge(SEM["sS"], V["s_dA"])
        v.tensor_scalar(s_t16b[:], s_t16a[:], -1.0, 1.0, Alu.mult, Alu.add)
        v.drain()
        v.tensor_tensor(s_t16b[:], s_t16b[:], s_Bm[:], Alu.mult)
        v.drain()
        v.tensor_tensor(s_t16b[:], s_t16b[:], s_dA[:], Alu.mult)
        v.drain()
        v.tensor_tensor(s_CdB[:], s_t16b[:], s_Cm[:], Alu.mult)
        v.tensor_copy(s_dA[:], s_t16a[:])
        v.tensor_copy(s_oncb[:], s_onc[:])
        v.memset(s_state[:], 0.0)
        vnop(v)
    V["v_tabs"] = C["sV"]

    for h in range(2):
        @blk.gpsimd
        def _(g, h=h):
            if h == 1:
                g.wait_ge(SEM["sPE"], V["pe_mod2"])
            dmx(g, "sQ3", sh_w[:, 0:3072].rearrange("p (db m) -> p db m", db=DB), d_wmod[:, 384 * h:384 * h + 384].rearrange(
                    "(db p) m -> p db m", p=128))
        V[f"q3_wmod{h}"] = C["sQ3"]

        @blk.scalar
        def _(s, h=h):
            if h == 0:
                s.wait_ge(SEM["sQ"], V["q_small"])
                inc(s.activation(s_cTb[:], s_cT[:], Act.Silu), "sS")
        if h == 0:
            V["s_siluc"] = C["sS"]

        for jl in range(3):
            j = 3 * h + jl

            @blk.tensor
            def _(t, h=h, jl=jl, j=j):
                if j == 0:
                    t.wait_ge(SEM["sS"], V["s_siluc"])
                if jl == 0:
                    t.wait_ge(SEM["sQ3"], V[f"q3_wmod{h}"])
                if j > 0:
                    t.wait_ge(SEM["sV"], V[f"v_modev{j-1}"])
                wv = sh_w[:, 0:3072].rearrange("p (db j m) -> p db j m",
                                               db=DB, j=3)
                last = None
                for kc in range(DB):
                    last = t.matmul(PX[:, 0:B], wv[:, kc, jl, :],
                                    s_cTb[:, kc, :],
                                    start=(kc == 0), stop=(kc == DB - 1))
                inc(last, "sPE")
            V[f"pe_mod{j}"] = C["sPE"]

            @blk.vector
            def _(v, j=j):
                v.wait_ge(SEM["sPE"], V[f"pe_mod{j}"])
                inc(v.tensor_copy(s_modsl[:, j, :], PX[:, 0:B]), "sV")
            V[f"v_modev{j}"] = C["sV"]

    @blk.sync
    def _(sy):
        sy.wait_ge(SEM["sV"], V["v_modev5"])
        dmx(sy, "sQS", ag_m_in[:].transpose([1, 0, 2]), s_modsl[:])
    V["q_modin"] = C["sQS"]

    @blk.gpsimd
    def _(g):
        g.wait_ge(SEM["sQS"], V["q_modin"])
        inc(g.collective_compute("AllGather", Alu.bypass, replica_groups=RG,
                                 ins=[ag_m_in[:]], outs=[ag_m_out[:]]), "sAGm")
    V["g_mod"] = C["sAGm"]

    @blk.sync
    def _(sy):
        sy.wait_ge(SEM["sAGm"], V["g_mod"])
        dmx(sy, "sQ", s_modal[:], ag_m_out[:].rearrange("r s p b -> p (r s) b"))
    V["q_modal"] = C["sQ"]

    @blk.vector
    def _(v):
        v.wait_ge(SEM["sQ"], V["q_modal"])
        v.scalar_tensor_tensor(s_mod[:], s_modal[:, :, 0], s_sel[:, 0:1],
                               s_modal[:, :, 0], Alu.mult, Alu.bypass)
        for b in range(1, B):
            v.drain()
            v.scalar_tensor_tensor(s_mod[:], s_modal[:, :, b], s_sel[:, b:b + 1],
                                   s_mod[:], Alu.mult, Alu.add)
        v.drain()
        mv = s_mod[:].rearrange("p (v db) -> p v db", v=6)
        v.tensor_scalar(s_ca1[:], mv[:, 1, :], 1.0, None, Alu.add)
        v.tensor_copy(s_sa[:], mv[:, 0, :])
        v.tensor_scalar(s_cb1[:], mv[:, 4, :], 1.0, None, Alu.add)
        v.tensor_copy(s_sbv[:], mv[:, 3, :])
        for src_i, dst in [(2, s_ga), (5, s_gb)]:
            v.tensor_scalar(s_gtmp[:], mv[:, src_i, :], 0.0, 2.0,
                            Alu.is_ge, Alu.mult)
            v.drain()
            v.tensor_scalar(s_gtmp[:], s_gtmp[:], -1.0, None, Alu.add)
            v.tensor_scalar(s_gtmp2[:], mv[:, src_i, :], -1.0, None, Alu.mult)
            v.drain()
            v.tensor_tensor(s_gtmp2[:], s_gtmp2[:], mv[:, src_i, :], Alu.max)
            v.drain()
            v.tensor_scalar(s_gtmp2[:], s_gtmp2[:], 0.25, None, Alu.max)
            v.drain()
            v.tensor_tensor(dst[:], s_gtmp[:], s_gtmp2[:], Alu.mult)
            v.drain()
        vnop(v)
    V["v_mod"] = C["sV"]

    def emit_stats_rows(src, tag, want_mean, wait_s_v=None, wait_pe_v=None,
                        src_f32=False):
        @blk.scalar
        def _(s):
            if wait_s_v is not None:
                s.wait_ge(SEM["sV"], wait_s_v)
            inc(s.activation(s_sq[:], src[:], Act.Square), "sS")
        V[f"sq_{tag}"] = C["sS"]

        @blk.tensor
        def _(t):
            t.wait_ge(SEM["sS"], V[f"sq_{tag}"])
            if wait_pe_v is not None:
                t.wait_ge(SEM["sV"], wait_pe_v)
            last = None
            for db in range(DB):
                last = t.matmul(PY[0:1, 0:512], s_oncb[:], s_sq[:, db, :],
                                start=(db == 0), stop=(db == DB - 1))
            if want_mean:
                ones = s_onc[:] if src_f32 else s_oncb[:]
                for db in range(DB):
                    last = t.matmul(PX[0:1, 0:512], ones, src[:, db, :],
                                    start=(db == 0), stop=(db == DB - 1))
            inc(last, "sPE")
        V[f"st_{tag}"] = C["sPE"]

        @blk.vector
        def _(v):
            v.wait_ge(SEM["sPE"], V[f"st_{tag}"])
            if want_mean:
                v.tensor_scalar(s_row2[:], PX[0:1, 0:512], 1.0 / D, None, Alu.mult)
                v.drain()
                v.tensor_tensor(s_row[:], s_row2[:], s_row2[:], Alu.mult)
                v.drain()
                v.scalar_tensor_tensor(s_row[:], PY[0:1, 0:512], 1.0 / D,
                                       s_row[:], Alu.mult, Alu.subtract)
                v.drain()
                v.tensor_scalar(s_row[:], s_row[:], LNEPS, None, Alu.add)
            else:
                v.tensor_scalar(s_row[:], PY[0:1, 0:512], 1.0 / D, RMSEPS,
                                Alu.mult, Alu.add)
            vnop(v)
        V[f"stv_{tag}"] = C["sV"]

        @blk.scalar
        def _(s):
            s.wait_ge(SEM["sV"], V[f"stv_{tag}"])
            s.activation(s_row[:], s_row[:], Act.Ln)
            s.drain()
            inc(s.activation(s_row[:], s_row[:], Act.Exp, scale=-0.5), "sS")
        V[f"str_{tag}"] = C["sS"]

        @blk.vector
        def _(v):
            v.wait_ge(SEM["sS"], V[f"str_{tag}"])
            if want_mean:
                v.tensor_tensor(s_row2[:], s_row2[:], s_row[:], Alu.mult)
            vnop(v)
        V[f"strows_{tag}"] = C["sV"]

        @blk.tensor
        def _(t):
            t.wait_ge(SEM["sV"], V[f"strows_{tag}"])
            i1 = t.matmul(PX[:, 0:512], s_onr[:], s_row[:], start=True, stop=True)
            if want_mean:
                i1 = t.matmul(PY[:, 0:512], s_onr[:], s_row2[:],
                              start=True, stop=True)
            inc(i1, "sPE")
        V[f"stexp_{tag}"] = C["sPE"]

    def quant_steps(src_slc, dst_slc, F, tag, wait_v=None):
        nbm = F // 16
        mb = q_m[:, 0:nbm].rearrange("p (a b) -> p a b", b=2) \
            .unsqueeze(3).broadcast_to([128, F // 32, 2, 16])
        ib = q_in[:, 0:nbm].rearrange("p (a b) -> p a b", b=2) \
            .unsqueeze(3).broadcast_to([128, F // 32, 2, 16])

        def g4(ap):
            return ap.rearrange("p (a b k) -> p a b k", b=2, k=16)

        def mk(fn, first=False, last=False):
            def emit(fn=fn, first=first, last=last):
                @blk.vector
                def _(v):
                    if not first:
                        v.drain()
                    r = fn(v)
                    if last:
                        inc(r, "sV")
                        V[f"q_{tag}"] = C["sV"]
            return emit

        def s1(v):
            if wait_v is not None:
                v.wait_ge(SEM["sV"], wait_v)
            v.transpose(q_vt[:, 0:F], src_slc)

        def s2(v):
            v.tensor_reduce(q_m[:, 0:nbm], g4(q_vt[:, 0:F]), AX, Alu.max,
                            apply_absolute_value=True)

        def s3(v):
            v.tensor_scalar(q_m[:, 0:nbm], q_m[:, 0:nbm], 1.0 / 6.0, 1e-12,
                            Alu.mult, Alu.max)

        def s4(v):
            v.reciprocal(q_in[:, 0:nbm], q_m[:, 0:nbm])

        def s5(v):
            v.tensor_tensor(g4(q_vt[:, 0:F]), g4(q_vt[:, 0:F]), ib, Alu.mult)

        def s6(v):
            v.tensor_tensor(q_sq[:, 0:F], q_vt[:, 0:F], q_vt[:, 0:F], Alu.mult)
            v.tensor_scalar(q_u[:, 0:F], q_vt[:, 0:F], 0.0, 2.0,
                            Alu.is_gt, Alu.mult)

        def s7(v):
            v.tensor_scalar(q_q1[:, 0:F], q_sq[:, 0:F], 0.0625, 0.5,
                            Alu.is_gt, Alu.mult)
            v.tensor_scalar(q_h5[:, 0:F], q_sq[:, 0:F], 0.5625, 0.5,
                            Alu.is_gt, Alu.mult)
            v.tensor_scalar(q_vt[:, 0:F], q_sq[:, 0:F], 1.5625, 0.5,
                            Alu.is_gt, Alu.mult)
            v.tensor_scalar(q_u[:, 0:F], q_u[:, 0:F], -1.0, None, Alu.add)

        def s8(v):
            v.tensor_tensor(q_q1[:, 0:F], q_q1[:, 0:F], q_h5[:, 0:F], Alu.add)

        def s9(v):
            v.tensor_scalar(q_h5[:, 0:F], q_sq[:, 0:F], 3.0625, 0.5,
                            Alu.is_gt, Alu.mult)

        def s10(v):
            v.tensor_tensor(q_q1[:, 0:F], q_q1[:, 0:F], q_vt[:, 0:F], Alu.add)

        def s11(v):
            v.tensor_tensor(q_q1[:, 0:F], q_q1[:, 0:F], q_h5[:, 0:F], Alu.add)
            v.tensor_scalar(q_vt[:, 0:F], q_sq[:, 0:F], 6.25, None, Alu.is_gt)

        def s12(v):
            v.tensor_tensor(q_q1[:, 0:F], q_q1[:, 0:F], q_vt[:, 0:F], Alu.add)
            v.tensor_scalar(q_h5[:, 0:F], q_sq[:, 0:F], 12.25, None, Alu.is_gt)

        def s13(v):
            v.tensor_tensor(q_q1[:, 0:F], q_q1[:, 0:F], q_h5[:, 0:F], Alu.add)
            v.tensor_scalar(q_vt[:, 0:F], q_sq[:, 0:F], 25.0, 2.0,
                            Alu.is_gt, Alu.mult)

        def s14(v):
            v.tensor_tensor(q_q1[:, 0:F], q_q1[:, 0:F], q_vt[:, 0:F], Alu.add)

        def s15(v):
            v.tensor_tensor(q_q1[:, 0:F], q_q1[:, 0:F], q_u[:, 0:F], Alu.mult)

        def s16(v):
            v.tensor_tensor(g4(q_q1[:, 0:F]), g4(q_q1[:, 0:F]), mb, Alu.mult)

        def s17(v):
            return v.transpose(dst_slc, q_q1[:, 0:F])

        fns = [s1, s2, s3, s4, s5, s6, s7, s8, s9, s10, s11, s12, s13,
               s14, s15, s16, s17]
        return [mk(f, first=(i == 0), last=(i == len(fns) - 1))
                for i, f in enumerate(fns)]

    def emit_quant(src_slc, dst_slc, F, tag, wait_v=None):
        for st in quant_steps(src_slc, dst_slc, F, tag, wait_v):
            st()

    def emit_ug_load(k):
        @blk.sync
        def _(sy, k=k):
            sy.wait_ge(SEM["sAGu"], V[f"agu_{k}"])
            if k >= 2:
                sy.wait_ge(SEM["sV"], V[f"scn_{k-2}_{B*N16-1}"])
            src = bass.AP(ag_u_out, r_u[k],
                          [[STRIPE, 128], [DB * 128 * STRIPE, NCORE], [1, STRIPE]])
            qn = "sQ2" if k % 2 == 0 else "sQ2b"
            dmx(sy, qn, s_ug[k % 2][:], src)
        V[f"q_ug_{k}"] = (("sQ2" if k % 2 == 0 else "sQ2b"),
                          C["sQ2" if k % 2 == 0 else "sQ2b"])

    def emit_scan_iter(k, i):
        b, n = divmod(i, N16)
        par = i % 2

        @blk.vector
        def _(v, k=k, b=b, n=n, i=i, par=par):
            if i == 0:
                qs, qv = V[f"q_ug_{k}"]
                v.wait_ge(SEM[qs], qv)
            gi = k * B * N16 + i
            if gi >= 2:
                v.wait_ge(SEM["sG"], V[f"g_{gi-2}"])
            st = b * N16 + n
            init = 0.0 if k == 0 else s_state[:, st:st + 1]
            uview = s_ug[k % 2][:].rearrange("p s t -> p (s t)")
            v.tensor_tensor_scan(s_z[par][:],
                                 s_dA[:, n:n + 1].broadcast_to([128, 1024]),
                                 uview[:, 1024 * b:1024 * b + 1024],
                                 init, Alu.mult, Alu.add)
            v.drain()
            v.tensor_copy(s_state[:, st:st + 1], s_z[par][:, 1023:1024])
            inc(v.tensor_scalar(s_zc[:, par, :], s_z[par][:],
                                s_CdB[:, n:n + 1], None, Alu.mult), "sV")
        V[f"scn_{k}_{i}"] = C["sV"]

        @blk.gpsimd
        def _(g, k=k, b=b, n=n, i=i, par=par):
            g.wait_ge(SEM["sV"], V[f"scn_{k}_{i}"])
            g.drain()
            if n == 0:
                pk, pb = (k, b - 2) if b >= 2 else (k - 1, b + 2)
                if pk >= 0:
                    g.wait_ge(SEM["sV"], V[f"yfl_{pk}_{pb}"])
                ins = g.tensor_copy(s_yac2[:, b % 2, :], s_zc[:, par, :])
            else:
                ins = g.tensor_tensor(s_yac2[:, b % 2, :], s_yac2[:, b % 2, :],
                                      s_zc[:, par, :], Alu.add)
            ins.then_inc(SEM["sG"], 1)
            C["sG"] += 1
        V[f"g_{k * B * N16 + i}"] = C["sG"]

    def emit_scan_flush(k, b):
        @blk.vector
        def _(v, k=k, b=b):
            v.wait_ge(SEM["sG"], V[f"g_{k * B * N16 + b * N16 + N16 - 1}"])
            pk, pb = (k, b - 2) if b >= 2 else (k - 1, b + 2)
            if pk >= 0:
                qs, qv = V[f"q_yst_{pk}_{pb}"]
                v.wait_ge(SEM[qs], qv)
            v.drain()
            inc(v.tensor_copy(s_yst2[:, 1024 * (b % 2):1024 * (b % 2) + 1024],
                              s_yac2[:, b % 2, :]), "sV")
        V[f"yfl_{k}_{b}"] = C["sV"]

        @blk.gpsimd
        def _(g, k=k, b=b):
            g.wait_ge(SEM["sV"], V[f"yfl_{k}_{b}"])
            dmx(g, "sY" if b % 2 == 0 else "sYb", ag_y_in[k][:, b, :],
                s_yst2[:, 1024 * (b % 2):1024 * (b % 2) + 1024])
        qn_ = "sY" if b % 2 == 0 else "sYb"
        V[f"q_yst_{k}_{b}"] = (qn_, C[qn_])

        if b == B - 1:
            @blk.gpsimd
            def _(g, k=k, _ye=C["sY"] + 16, _yo=C["sYb"] + 16):
                pass

            @blk.gpsimd
            def _(g, k=k):
                g.wait_ge(SEM["sY"], C["sY"])
                g.wait_ge(SEM["sYb"], C["sYb"])
                inc(g.collective_compute("AllGather", Alu.bypass,
                                         replica_groups=RG,
                                         ins=[ag_y_in[k]],
                                         outs=[ag_y_out[k]]), "sAGy")
            V[f"agy_{k}"] = C["sAGy"]

    scan_cur = {k: 0 for k in range(NSTR)}

    def weave_scan(k, upto):
        if k >= NSTR:
            return
        while scan_cur[k] < min(upto, B * N16):
            i = scan_cur[k]
            emit_scan_iter(k, i)
            scan_cur[k] += 1
            if i % N16 == N16 - 1:
                emit_scan_flush(k, i // N16)

    def emit_p1_stripe(k):
        for r in range(4):
            idx = 4 * k + r

            @blk.sync
            def _(sy, k=k, r=r, idx=idx):
                if idx == 0:
                    pass
                if idx >= 2:
                    pk, pr = divmod(idx - 2, 4)
                    sy.wait_ge(SEM["sPE"], V[f"xT_{pk}_{pr}"])
                qn = "sQXa" if idx % 2 == 0 else "sQXb"
                dmx(sy, qn, s_xtm[idx % 2][:],
                    d_x[512 * k + 128 * r:512 * k + 128 * r + 128, :])
            V[f"q_x_{k}_{r}"] = ("sQXa" if (4 * k + r) % 2 == 0 else "sQXb",
                                 C["sQXa" if (4 * k + r) % 2 == 0 else "sQXb"])

            for db in range(DB):
                @blk.tensor
                def _(t, k=k, r=r, db=db, idx=idx):
                    if db == 0:
                        qs, qv = V[f"q_x_{k}_{r}"]
                        t.wait_ge(SEM[qs], qv)
                        if idx == 0:
                            t.wait_ge(SEM["sV"], V["v_mod"])
                        elif r == 0 and k > 0:
                            t.wait_ge(SEM["sV"], V[f"up_{k-1}"])
                            if k == 1:
                                t.wait_ge(SEM["sV"], V["shTev_wd_21"])
                    if db >= 2:
                        t.wait_ge(SEM["sV"], V[f"xev_{k}_{r}_{db-2}"])
                    elif idx > 0:
                        pk, pr = divmod(idx - 1, 4)
                        t.wait_ge(SEM["sV"], V[f"xev_{pk}_{pr}_{DB-2+db}"])
                    inc(t.transpose((PX if db % 2 == 0 else PY)[:, 0:128],
                                    s_xtm[idx % 2][:, 128 * db:128 * db + 128],
                                    s_id32[:]), "sPE")
                V[f"xTp_{k}_{r}_{db}"] = C["sPE"]

                @blk.vector
                def _(v, k=k, r=r, db=db):
                    v.wait_ge(SEM["sPE"], V[f"xTp_{k}_{r}_{db}"])
                    if k > 0 and r == 0 and db == 0:
                        v.wait_ge(SEM["sQP"], V[f"q_xsp_{k-1}"])
                    ins = v.tensor_copy(s_xfm[:, db, 128 * r:128 * r + 128],
                                        (PX if db % 2 == 0 else PY)[:, 0:128])
                    inc(ins, "sV")
                V[f"xev_{k}_{r}_{db}"] = C["sV"]
            V[f"xT_{k}_{r}"] = C["sPE"]
        V[f"xev_last_{k}"] = C["sV"]

        @blk.sync
        def _(sy, k=k):
            sy.wait_ge(SEM["sV"], V[f"xev_last_{k}"])
            dmx(sy, "sQP", h_xfm[:, :, 512 * k:512 * k + 512].transpose([1, 0, 2]), s_xfm[:])
        V[f"q_xsp_{k}"] = C["sQP"]

        emit_stats_rows(s_xfm, f"ln1_{k}", True,
                        wait_s_v=V[f"xev_last_{k}"],
                        wait_pe_v=(V[f"up_{k-1}"] if k > 0 else None),
                        src_f32=True)

        @blk.vector
        def _(v, k=k):
            v.wait_ge(SEM["sPE"], V[f"stexp_ln1_{k}"])
            pxb = PX[:, 0:512].unsqueeze(1).broadcast_to([128, DB, 512])
            pyb = PY[:, 0:512].unsqueeze(1).broadcast_to([128, DB, 512])
            v.tensor_tensor(s_t1[:], s_xfm[:], pxb, Alu.mult)
            v.drain()
            v.tensor_tensor(s_t1[:], s_t1[:], pyb, Alu.subtract)
            v.drain()
            for db in range(DB):
                v.tensor_scalar(s_h1[:, db, :], s_t1[:, db, :],
                                s_ca1[:, db:db + 1], s_sa[:, db:db + 1],
                                Alu.mult, Alu.add)
            vnop(v)
        V[f"h1_{k}"] = C["sV"]

        emit_stats_rows(s_h1, f"rms1_{k}", False,
                        wait_s_v=V[f"h1_{k}"], wait_pe_v=V[f"h1_{k}"])

        @blk.vector
        def _(v, k=k):
            v.wait_ge(SEM["sPE"], V[f"stexp_rms1_{k}"])
            if k > 0:
                v.wait_ge(SEM["sU"], V[f"q_ust_{k-1}"])
            pxb = PX[:, 0:512].unsqueeze(1).broadcast_to([128, DB, 512])
            v.tensor_tensor(s_t1[:], s_h1[:], pxb, Alu.mult)
            v.drain()
            for db in range(DB):
                v.tensor_scalar(s_u[:, db, :], s_t1[:, db, :],
                                s_s1[:, db:db + 1], None, Alu.mult)
            v.drain()
            for db in range(DB):
                v.scalar_tensor_tensor(s_pch[:, db, :], s_u[:, db, :],
                                       s_Dfm[:, db:db + 1], s_h1[:, db, :],
                                       Alu.mult, Alu.add)
            vnop(v)
        V[f"up_{k}"] = C["sV"]

        @blk.sync
        def _(sy, k=k):
            sy.wait_ge(SEM["sV"], V[f"up_{k}"])
            dmx(sy, "sU", ag_u_in[k].transpose([1, 0, 2]), s_u[:])
            dmx(sy, "sU", h_p[:, :, 512 * k:512 * k + 512].transpose([1, 0, 2]), s_pch[:])
        V[f"q_ust_{k}"] = C["sU"]

        @blk.gpsimd
        def _(g, k=k):
            g.wait_ge(SEM["sU"], V[f"q_ust_{k}"])
            inc(g.collective_compute("AllGather", Alu.bypass, replica_groups=RG,
                                     ins=[ag_u_in[k]], outs=[ag_u_out[k]]), "sAGu")
        V[f"agu_{k}"] = C["sAGu"]

        if k == 1:
            emit_ug_load(0)
            weave_scan(0, 24)
        elif k == 2:
            emit_ug_load(1)
            weave_scan(0, 48)
        elif k == 3:
            weave_scan(0, 64)

    emit_p1_stripe(0)

    def emit_shard_quant(n, tag, wait_pe=None):
        nb16 = n // 16
        mv = sh_m[:, 0:nb16]
        iv = sh_i[:, 0:nb16]
        mb = mv.unsqueeze(2).broadcast_to([128, nb16, 16])
        ib = iv.unsqueeze(2).broadcast_to([128, nb16, 16])

        def g3(ap):
            return ap.rearrange("p (g k) -> p g k", k=16)

        sqv = sh_w[:, 0:n]
        qb = sh_b[:, 0:n]
        tmp = sh_i1[:, 0:n]
        xn = sh_f1[:, 0:n]

        @blk.vector
        def _(v):
            v.wait_ge(SEM["sQ3"], V[f"q3_{tag}"])
            bsem, bval = V[f"shbfree_{tag}"]
            v.wait_ge(SEM[bsem], bval)
            if wait_pe is not None:
                v.wait_ge(SEM["sPE"], wait_pe)
            v.tensor_reduce(mv, g3(sh_w[:, 0:n]), AX, Alu.max,
                            apply_absolute_value=True)
            v.drain()
            v.tensor_scalar(mv, mv, 1.0 / 6.0, 1e-12, Alu.mult, Alu.max)
            v.drain()
            v.reciprocal(iv, mv)
            v.drain()
            v.tensor_tensor(g3(xn), g3(sh_w[:, 0:n]), ib, Alu.mult)
            v.drain()
            v.tensor_tensor(sqv, xn, xn, Alu.mult)
            v.drain()
            v.tensor_scalar(qb, sqv, 0.0625, 0.5, Alu.is_gt, Alu.mult)
            v.tensor_scalar(tmp, sqv, 0.5625, 0.5, Alu.is_gt, Alu.mult)
            v.tensor_scalar(xn, xn, 0.0, 2.0, Alu.is_gt, Alu.mult)
            v.drain()
            v.tensor_tensor(qb, qb, tmp, Alu.add)
            v.tensor_scalar(xn, xn, -1.0, None, Alu.add)
            v.drain()
            for m2, w in [(1.5625, 0.5), (3.0625, 0.5), (6.25, None),
                          (12.25, None), (25.0, 2.0)]:
                if w is None:
                    v.tensor_scalar(tmp, sqv, m2, None, Alu.is_gt)
                else:
                    v.tensor_scalar(tmp, sqv, m2, w, Alu.is_gt, Alu.mult)
                v.drain()
                v.tensor_tensor(qb, qb, tmp, Alu.add)
                v.drain()
            v.tensor_tensor(xn, qb, xn, Alu.mult)
            v.drain()
            inc(v.tensor_tensor(g3(xn), g3(xn), mb, Alu.mult), "sV")
        V[f"shq_{tag}"] = C["sV"]

    @blk.gpsimd
    def _(g):
        g.wait_ge(SEM["sPE"], V["pe_mod5"])
        dmx(g, "sQ3", sh_w[:].rearrange("p (kc m) -> p kc m", kc=DB), d_w1s[:].rearrange("(kc p) m -> p kc m", p=128))
    V["q3_w1s"] = C["sQ3"]

    @blk.vector
    def _(v):
        v.wait_ge(SEM["sQ3"], V["q3_w1s"])
        v.wait_ge(SEM["sPE"], V["pe_mod5"])
        inc(v.tensor_copy(sh_b[:], sh_w[:]), "sV")
    V["v_w1b"] = C["sV"]

    @blk.sync
    def _(sy):
        sy.wait_ge(SEM["sV"], V["v_w1b"])
        dmx(sy, "sQW", ag_w_in[OFF_W1:OFF_W1 + SZ_W1].rearrange(
                "(kc p m) -> p kc m", p=128, m=512), sh_b[:].rearrange("p (kc m) -> p kc m", kc=DB))
    V["q_w1in"] = C["sQW"]

    @blk.gpsimd
    def _(g):
        g.wait_ge(SEM["sV"], V["v_w1b"])
        dmx(g, "sQ3", sh_w[:].rearrange("p (kc m) -> p kc m", kc=MC1), d_w2s[:].rearrange("(kc p) m -> p kc m", p=128))
    V["q3_w2s"] = C["sQ3"]

    @blk.vector
    def _(v):
        v.wait_ge(SEM["sQ3"], V["q3_w2s"])
        v.wait_ge(SEM["sQW"], V["q_w1in"])
        inc(v.tensor_copy(sh_b[:], sh_w[:]), "sV")
    V["v_w2b"] = C["sV"]

    @blk.sync
    def _(sy):
        sy.wait_ge(SEM["sV"], V["v_w2b"])
        dmx(sy, "sQW", ag_w_in[OFF_W2:OFF_W2 + SZ_W2].rearrange(
                "(kc p m) -> p kc m", p=128, m=128), sh_b[:].rearrange("p (kc m) -> p kc m", kc=MC1))
    V["q_w2in"] = C["sQW"]

    def emit_ffn_shard(d_src, off, nblk, kdim, tag, prev_w_free, prev_b_free,
                       prev_pe=None):
        n = nblk * kdim
        n_kc = kdim // 128

        @blk.gpsimd
        def _(g):
            g.wait_ge(SEM["sV"], prev_w_free)
            dmx(g, "sQ3", sh_w[:, 0:n].rearrange("p (nb k) -> p nb k", nb=nblk), d_src[:].rearrange("(nb p) k -> p nb k", p=128))
        V[f"q3_{tag}"] = C["sQ3"]
        V[f"shbfree_{tag}"] = prev_b_free
        emit_shard_quant(n, tag, wait_pe=prev_pe)

        ntile = nblk * n_kc
        for idx in range(ntile):
            nb, kc = idx // n_kc, idx % n_kc

            @blk.tensor
            def _(t, nb=nb, kc=kc, idx=idx):
                if idx == 0:
                    t.wait_ge(SEM["sV"], V[f"shq_{tag}"])
                if idx >= 2:
                    t.wait_ge(SEM["sV"], V[f"shTev_{tag}_{idx-2}"])
                src = sh_f1[:, 0:n].rearrange("p (nb k) -> p nb k", nb=nblk)
                inc(t.transpose((PX if idx % 2 == 0 else PY)[:, 0:128],
                                src[:, nb, 128 * kc:128 * kc + 128], s_id32[:]),
                    "sPE")
            V[f"shTpe_{tag}_{idx}"] = C["sPE"]

            @blk.vector
            def _(v, idx=idx):
                v.wait_ge(SEM["sPE"], V[f"shTpe_{tag}_{idx}"])
                inc(v.tensor_copy(sh_t[:, idx, :],
                                  (PX if idx % 2 == 0 else PY)[:, 0:128]), "sV")
            V[f"shTev_{tag}_{idx}"] = C["sV"]

        @blk.sync
        def _(sy):
            sy.wait_ge(SEM["sV"], V[f"shTev_{tag}_{ntile-1}"])
            shp = sh_t[:, 0:ntile, :].rearrange("p (nb kc) m -> p kc nb m",
                                                nb=nblk)
            hv = ag_w_in[off:off + n_kc * 128 * nblk * 128].rearrange(
                "(kc p nb m) -> p kc nb m", p=128, nb=nblk, m=128)
            if nblk == 1:
                dmx(sy, "sQW", hv[:, :, 0, :], shp[:, :, 0, :])
            else:
                for kc in range(n_kc):
                    dmx(sy, "sQW", hv[:, kc, :, :], shp[:, kc, :, :])
        V[f"q_{tag}in"] = C["sQW"]
        return V[f"shq_{tag}"], V[f"shq_{tag}"]

    wfree, bfree = emit_ffn_shard(d_wgs, OFF_WG, 3, 1024, "wg",
                                  V["v_w2b"], ("sQW", V["q_w2in"]))
    wfree, bfree = emit_ffn_shard(d_wus, OFF_WU, 3, 1024, "wu", wfree,
                                  ("sQW", V["q_wgin"]),
                                  prev_pe=V["shTpe_wg_23"])
    wfree, bfree = emit_ffn_shard(d_wds, OFF_WD, 1, 2816, "wd", wfree,
                                  ("sQW", V["q_wuin"]),
                                  prev_pe=V["shTpe_wu_23"])

    @blk.gpsimd
    def _(g):
        g.wait_ge(SEM["sQW"], V["q_wdin"])
        inc(g.collective_compute("AllGather", Alu.bypass, replica_groups=RG,
                                 ins=[ag_w_in[:]], outs=[ag_w_out[:]]), "sAGw")
    V["g_w"] = C["sAGw"]



    for _k1 in range(1, NSTR):
        emit_p1_stripe(_k1)

    def emit_pass_front(k):
        @blk.sync
        def _(sy, k=k):
            sy.wait_ge(SEM["sAGy"], V[f"agy_{k}"])
            if k == 0:
                sy.wait_ge(SEM["sU"], V[f"q_ust_{NSTR-1}"])
                sy.wait_ge(SEM["sPE"], V[f"xT_{NSTR-1}_3"])
                sy.wait_ge(SEM["sQW"], V["q_wdin"])
            else:
                sy.wait_ge(SEM["sPE"], V[f"gu_{k-1}_{NC2-1}"])
            src = bass.AP(ag_y_out, r_y[k],
                          [[B * 1024, 128], [128 * B * 1024, DB], [1, STRIPE]])
            dmx(sy, "sQ", s_yin[:], src)
            dmx(sy, "sQ", s_pin[:], h_p[:, :, 512 * k:512 * k + 512].transpose([1, 0, 2]))
        V[f"q_yin_{k}"] = C["sQ"]

        @blk.sync
        def _(sy, k=k):
            if k == 0:
                sy.wait_ge(SEM["sQP"], V[f"q_xsp_{NSTR-1}"])
            else:
                sy.wait_ge(SEM["sV"], V[f"ep_{k-1}_1"])
            dmx(sy, "sXC", s_xch[:], h_xfm[:, :, 512 * k:512 * k + 512].transpose([1, 0, 2]))
        V[f"q_xch_{k}"] = C["sXC"]

        @blk.vector
        def _(v, k=k):
            v.wait_ge(SEM["sQ"], V[f"q_yin_{k}"])
            v.drain()
            inc(v.tensor_tensor(s_xp[:], s_pin[:], s_yin[:], Alu.add), "sV")
        V[f"xp_{k}"] = C["sV"]

        if STAGE >= 2:
            emit_stats_rows(s_xp, f"rms2_{k}", False,
                            wait_s_v=V[f"xp_{k}"], wait_pe_v=V[f"xp_{k}"])

            @blk.vector
            def _(v, k=k):
                v.wait_ge(SEM["sPE"], V[f"stexp_rms2_{k}"])
                pxb = PX[:, 0:512].unsqueeze(1).broadcast_to([128, DB, 512])
                v.tensor_tensor(s_t1[:], s_xp[:], pxb, Alu.mult)
                v.drain()
                for db in range(DB):
                    v.tensor_scalar(s_h2[:, db, :], s_t1[:, db, :],
                                    s_s2[:, db:db + 1], None, Alu.mult)
                vnop(v)
            V[f"h2_{k}"] = C["sV"]

        if k + 2 < NSTR:
            emit_ug_load(k + 2)

    for k in range(NSTR):
        if k == 0:
            emit_pass_front(0)

        if STAGE == 1:
            @blk.vector
            def _(v, k=k):
                v.wait_ge(SEM["sV"], V[f"xp_{k}"])
                v.tensor_copy(s_x3[:], s_xp[:])
                vnop(v)
            V[f"x3r_{k}"] = C["sV"]
        else:
            w1_scan_base = scan_cur.get(k + 1, 0) if k + 1 < NSTR else 0
            for mc in range(MC1):
                @blk.sync
                def _(s, k=k, mc=mc):
                    if k == 0 and mc == 0:
                        s.wait_ge(SEM["sAGw"], V["g_w"])
                    if mc >= 2:
                        s.wait_ge(SEM["sPE"], V[f"w1mm_{k}_{mc-2}"])
                    elif k > 0:
                        s.wait_ge(SEM["sPE"], V[f"w1mm_{k-1}_{MC1-1}"])
                    rank, col = mc // 4, mc % 4
                    off = rank * WSH + OFF_W1 + col * 128
                    src = bass.AP(ag_w_out, off,
                                  [[512, 128], [128 * 512, DB], [1, 128]])
                    dmx(s, "sW1a" if mc % 2 == 0 else "sW1b", s_w1t[mc % 2][:], src)
                V[f"q_w1t_{k}_{mc}"] = ("sW1a" if mc % 2 == 0 else "sW1b",
                                        C["sW1a" if mc % 2 == 0 else "sW1b"])

                @blk.tensor
                def _(t, k=k, mc=mc):
                    qs, qv = V[f"q_w1t_{k}_{mc}"]
                    t.wait_ge(SEM[qs], qv)
                    if mc == 0:
                        t.wait_ge(SEM["sV"], V[f"h2_{k}"])
                        if k > 0 and STAGE >= 3:
                            t.wait_ge(SEM["sS"], V[f"sil_{k-1}_{NC2//2-1}"])
                            t.wait_ge(SEM["sV"], V[f"hmul_{k-1}_{NC2//2-1}"])
                    if mc >= 4:
                        t.wait_ge(SEM["sS"], V[f"gel_{k}_{mc-4}"])
                    pdst = [PA, PX, PBK, PY][mc % 4][:, 0:512]
                    last = None
                    for kc in range(DB):
                        last = t.matmul(pdst, s_w1t[mc % 2][:, kc, :],
                                        s_h2[:, kc, :], start=(kc == 0),
                                        stop=(kc == DB - 1))
                    inc(last, "sPE")
                V[f"w1mm_{k}_{mc}"] = C["sPE"]

                @blk.scalar
                def _(s, k=k, mc=mc):
                    s.wait_ge(SEM["sPE"], V[f"w1mm_{k}_{mc}"])
                    inc(s.activation(s_g[:, mc, :],
                                     [PA, PX, PBK, PY][mc % 4][:, 0:512],
                                     Act.Gelu, bias=s_b1[:, mc:mc + 1]), "sS")
                V[f"gel_{k}_{mc}"] = C["sS"]

                if k + 1 < NSTR and w1_scan_base < 44:
                    base = w1_scan_base
                    weave_scan(k + 1, base + ((44 - base) * (mc + 1)) // MC1)

            for half in range(2):
                for mc in range(MC1):
                    gidx = half * MC1 + mc

                    @blk.sync
                    def _(s, k=k, half=half, mc=mc, gidx=gidx):
                        if gidx >= 4:
                            ph, pm = divmod(gidx - 4, MC1)
                            s.wait_ge(SEM["sPE"], V[f"w2mm_{k}_{ph}_{pm}"])
                        elif k > 0 and gidx < 4:
                            s.wait_ge(SEM["sPE"], V[f"w2mm_{k-1}_{1}_{MC1-4+gidx}"])
                        qn = ["sW2a", "sW2b", "sW2c", "sW2d"][gidx % 4]
                        off = (half * 4) * WSH + OFF_W2 + mc * 128 * 128
                        src = bass.AP(ag_w_out, off,
                                      [[128, 128], [WSH, 4], [1, 128]])
                        dmx(s, qn, s_w2t[gidx % 4][:], src)
                    _q2 = ["sW2a", "sW2b", "sW2c", "sW2d"][gidx % 4]
                    V[f"q_w2t_{k}_{half}_{mc}"] = (_q2, C[_q2])

                    @blk.tensor
                    def _(t, k=k, half=half, mc=mc):
                        qs, qv = V[f"q_w2t_{k}_{half}_{mc}"]
                        t.wait_ge(SEM[qs], qv)
                        t.wait_ge(SEM["sS"], V[f"gel_{k}_{mc}"])
                        if mc == 0:
                            if half == 1:
                                t.wait_ge(SEM["sV"], V[f"ep_{k}_0"])
                            elif k > 0:
                                t.wait_ge(SEM["sV"],
                                          V[f"x3ep_{k-1}_1"] if STAGE >= 3
                                          else V[f"ep_{k-1}_1"])
                        last = None
                        gidx = half * MC1 + mc
                        for dcl in range(4):
                            last = t.matmul(PACC[dcl][:, 0:512],
                                            s_w2t[gidx % 4][:, dcl, :],
                                            s_g[:, mc, :], start=(mc == 0),
                                            stop=(mc == MC1 - 1))
                        inc(last, "sPE")
                    V[f"w2mm_{k}_{half}_{mc}"] = C["sPE"]

                @blk.vector
                def _(v, k=k, half=half):
                    v.wait_ge(SEM["sPE"], V[f"w2mm_{k}_{half}_{MC1-1}"])
                    if half == 0:
                        v.wait_ge(SEM["sXC"], V[f"q_xch_{k}"])
                    for dcl in range(4):
                        dc = half * 4 + dcl
                        v.scalar_tensor_tensor(s_t1[:, dc, :], PACC[dcl][:, 0:512],
                                               s_b2[:, dc:dc + 1], s_xp[:, dc, :],
                                               Alu.add, Alu.add)
                        v.drain()
                        v.scalar_tensor_tensor(s_x2[:, dc, :], s_t1[:, dc, :],
                                               s_ga[:, dc:dc + 1], s_xch[:, dc, :],
                                               Alu.mult, Alu.add)
                        v.drain()
                    inc(v.drain(), "sV")
                V[f"ep_{k}_{half}"] = C["sV"]

                if half == 0:
                    weave_scan(k + 1, 64)

            if STAGE == 2:
                @blk.vector
                def _(v, k=k):
                    v.wait_ge(SEM["sV"], V[f"ep_{k}_1"])
                    v.tensor_copy(s_x3[:], s_x2[:])
                    vnop(v)
                V[f"x3r_{k}"] = C["sV"]
            else:
                emit_stats_rows(s_x2, f"ln2_{k}", True,
                                wait_s_v=V[f"ep_{k}_1"], wait_pe_v=V[f"ep_{k}_1"])

                @blk.vector
                def _(v, k=k):
                    v.wait_ge(SEM["sPE"], V[f"stexp_ln2_{k}"])
                    pxb = PX[:, 0:512].unsqueeze(1).broadcast_to([128, DB, 512])
                    pyb = PY[:, 0:512].unsqueeze(1).broadcast_to([128, DB, 512])
                    v.tensor_tensor(s_t1[:], s_x2[:], pxb, Alu.mult)
                    v.drain()
                    v.tensor_tensor(s_t1[:], s_t1[:], pyb, Alu.subtract)
                    v.drain()
                    for db in range(DB):
                        v.tensor_scalar(s_h3[:, db, :], s_t1[:, db, :],
                                        s_cb1[:, db:db + 1], s_sbv[:, db:db + 1],
                                        Alu.mult, Alu.add)
                    vnop(v)
                V[f"h3_{k}"] = C["sV"]

                prev = V[f"h3_{k}"]
                for ch in range(4):
                    emit_quant(s_h3[:, 2 * ch:2 * ch + 2, :]
                               .rearrange("p a b -> p (a b)"),
                               s_xq[:, 2 * ch:2 * ch + 2, :]
                               .rearrange("p a b -> p (a b)"),
                               2 * STRIPE, f"xq_{k}_{ch}", wait_v=prev)
                    prev = V[f"q_xq_{k}_{ch}"]
                V[f"xq_{k}"] = prev

                hq_queue = []
                for nc_ in range(NC2):
                    @blk.sync
                    def _(s, k=k, nc_=nc_):
                        if nc_ >= 2:
                            s.wait_ge(SEM["sPE"], V[f"gu_{k}_{nc_-2}"])
                        rank, col = nc_ // 3, nc_ % 3
                        offg = rank * WSH + OFF_WG + col * 128
                        offu = rank * WSH + OFF_WU + col * 128
                        srcg = bass.AP(ag_w_out, offg,
                                       [[384, 128], [128 * 384, DB], [1, 128]])
                        srcu = bass.AP(ag_w_out, offu,
                                       [[384, 128], [128 * 384, DB], [1, 128]])
                        qn = "sGUa" if nc_ % 2 == 0 else "sGUb"
                        dmx(s, qn, s_wgt[nc_ % 2][:], srcg)
                        dmx(s, qn, s_wut[nc_ % 2][:], srcu)
                    _qg = "sGUa" if nc_ % 2 == 0 else "sGUb"
                    V[f"q_gu_{k}_{nc_}"] = (_qg, C[_qg])

                    @blk.tensor
                    def _(t, k=k, nc_=nc_):
                        qs, qv = V[f"q_gu_{k}_{nc_}"]
                        t.wait_ge(SEM[qs], qv)
                        p = nc_ // 2
                        if nc_ == 0:
                            t.wait_ge(SEM["sV"], V[f"q_xq_{k}_0"])
                            t.wait_ge(SEM["sS"], V[f"gel_{k}_{MC1-1}"])
                        if nc_ % 2 == 0 and p >= 2:
                            t.wait_ge(SEM["sV"], V[f"hmul_{k}_{p-2}"])
                            t.wait_ge(SEM["sS"], V[f"sil_{k}_{p-2}"])
                        grp = PG1 if p % 2 == 0 else PG2
                        go = 512 * (nc_ % 2)
                        last = None
                        for kc in range(DB):
                            if nc_ == 0 and kc in (2, 4, 6):
                                t.wait_ge(SEM["sV"], V[f"q_xq_{k}_{kc//2}"])
                            last = t.matmul(grp[:, go:go + 512],
                                            s_wgt[nc_ % 2][:, kc, :],
                                            s_xq[:, kc, :], start=(kc == 0),
                                            stop=(kc == DB - 1))
                        for kc in range(DB):
                            last = t.matmul(grp[:, 1024 + go:1536 + go],
                                            s_wut[nc_ % 2][:, kc, :],
                                            s_xq[:, kc, :], start=(kc == 0),
                                            stop=(kc == DB - 1))
                        inc(last, "sPE")
                    V[f"gu_{k}_{nc_}"] = C["sPE"]

                    if nc_ % 2 == 1:
                        @blk.scalar
                        def _(s, k=k, nc_=nc_):
                            s.wait_ge(SEM["sPE"], V[f"gu_{k}_{nc_}"])
                            p = nc_ // 2
                            if p >= 1:
                                s.wait_ge(SEM["sV"], V[f"hmul_{k}_{p-1}"])
                            elif k > 0:
                                s.wait_ge(SEM["sV"], V[f"hmul_{k-1}_{NC2//2-1}"])
                            grp = PG1 if (nc_ // 2) % 2 == 0 else PG2
                            inc(s.activation(s_sg2[:], grp[:, 0:1024],
                                             Act.Silu), "sS")
                        V[f"sil_{k}_{nc_//2}"] = C["sS"]

                        @blk.vector
                        def _(v, k=k, nc_=nc_):
                            v.wait_ge(SEM["sS"], V[f"sil_{k}_{nc_//2}"])
                            grp = PG1 if (nc_ // 2) % 2 == 0 else PG2
                            inc(v.tensor_tensor(s_hq[:, nc_ - 1:nc_ + 1, :]
                                                .rearrange("p a b -> p (a b)"),
                                                s_sg2[:], grp[:, 1024:2048],
                                                Alu.mult), "sV")
                        V[f"hmul_{k}_{nc_//2}"] = C["sV"]

                        p = nc_ // 2
                        if p == 3:
                            hs = s_hq[:, 0:6, :].rearrange("p a b -> p (a b)")
                            hq_queue.extend(quant_steps(
                                hs, hs, 6 * STRIPE, f"hq_{k}_0",
                                wait_v=V[f"hmul_{k}_{2}"]))
                        elif p == 6:
                            hs = s_hq[:, 6:12, :].rearrange("p a b -> p (a b)")
                            hq_queue.extend(quant_steps(
                                hs, hs, 6 * STRIPE, f"hq_{k}_1",
                                wait_v=V[f"hmul_{k}_{5}"]))
                        elif p == 9:
                            hs = s_hq[:, 12:17, :].rearrange("p a b -> p (a b)")
                            hq_queue.extend(quant_steps(
                                hs, hs, 5 * STRIPE, f"hq_{k}_2",
                                wait_v=V[f"hmul_{k}_{8}"]))
                        if p >= 4:
                            for _ in range(5):
                                if hq_queue:
                                    hq_queue.pop(0)()

                while hq_queue:
                    hq_queue.pop(0)()
                V[f"hq01_{k}"] = V[f"q_hq_{k}_2"]
                hs = s_hq[:, 17:22, :].rearrange("p a b -> p (a b)")
                for st in quant_steps(hs, hs, 5 * STRIPE, f"hq_{k}_3",
                                      wait_v=V[f"hmul_{k}_{NC2//2-1}"]):
                    st()
                V[f"hq_{k}"] = V[f"q_hq_{k}_3"]

                if k + 1 < NSTR:
                    emit_pass_front(k + 1)

                for half in range(2):
                    for dcl in range(4):
                        dc = half * 4 + dcl

                        @blk.sync
                        def _(s, k=k, dc=dc):
                            if dc >= 2:
                                s.wait_ge(SEM["sPE"], V[f"dn_{k}_{dc-2}"])
                            off = dc * WSH + OFF_WD
                            src = bass.AP(ag_w_out, off,
                                          [[128, 128], [128 * 128, NC2], [1, 128]])
                            qn = "sWDa" if dc % 2 == 0 else "sWDb"
                            dmx(s, qn, s_wdt[dc % 2][:], src)
                        _qd = "sWDa" if dc % 2 == 0 else "sWDb"
                        V[f"q_wd_{k}_{dc}"] = (_qd, C[_qd])

                        @blk.tensor
                        def _(t, k=k, dc=dc, dcl=dcl, half=half):
                            qs, qv = V[f"q_wd_{k}_{dc}"]
                            t.wait_ge(SEM[qs], qv)
                            if dc == 0:
                                t.wait_ge(SEM["sV"], V[f"hq01_{k}"])
                            if half == 1 and dcl == 0:
                                t.wait_ge(SEM["sV"], V[f"x3ep_{k}_0"])
                            last = None
                            for kc in range(NC2):
                                if dc == 0 and kc == 17:
                                    t.wait_ge(SEM["sV"], V[f"hq_{k}"])
                                last = t.matmul(PACC[dcl][:, 0:512],
                                                s_wdt[dc % 2][:, kc, :],
                                                s_hq[:, kc, :], start=(kc == 0),
                                                stop=(kc == NC2 - 1))
                            inc(last, "sPE")
                        V[f"dn_{k}_{dc}"] = C["sPE"]

                        weave_scan(k + 2, 2 * (dc + 1))

                    @blk.vector
                    def _(v, k=k, half=half):
                        v.wait_ge(SEM["sPE"], V[f"dn_{k}_{half*4+3}"])
                        for dcl in range(4):
                            dc = half * 4 + dcl
                            v.scalar_tensor_tensor(s_x3[:, dc, :],
                                                   PACC[dcl][:, 0:512],
                                                   s_gb[:, dc:dc + 1],
                                                   s_x2[:, dc, :],
                                                   Alu.mult, Alu.add)
                            v.drain()
                        inc(v.drain(), "sV")
                    V[f"x3ep_{k}_{half}"] = C["sV"]

                @blk.vector
                def _(v, k=k):
                    v.wait_ge(SEM["sV"], V[f"x3ep_{k}_1"])
                    vnop(v)
                V[f"x3r_{k}"] = C["sV"]

        for r in range(4):
            for db in range(DB):
                @blk.tensor
                def _(t, k=k, r=r, db=db):
                    if r == 0 and db == 0:
                        t.wait_ge(SEM["sV"], V[f"x3r_{k}"])
                    oidx = r * DB + db
                    if oidx >= 2:
                        pr, pdb = divmod(oidx - 2, DB)
                        t.wait_ge(SEM["sV"], V[f"oev_{k}_{pr}_{pdb}"])
                    inc(t.transpose((PX if db % 2 == 0 else PY)[:, 0:128],
                                    s_x3[:, db, 128 * r:128 * r + 128], s_id32[:]),
                        "sPE")
                V[f"oT_{k}_{r}_{db}"] = C["sPE"]

                @blk.vector
                def _(v, k=k, r=r, db=db):
                    v.wait_ge(SEM["sPE"], V[f"oT_{k}_{r}_{db}"])
                    gidx = 4 * k + r
                    if db == 0 and gidx >= 2:
                        pk, pr = divmod(gidx - 2, 4)
                        qs, qv = V[f"q_out_{pk}_{pr}"]
                        v.wait_ge(SEM[qs], qv)
                    inc(v.tensor_copy(s_otm[r % 2][:, 128 * db:128 * db + 128],
                                      (PX if db % 2 == 0 else PY)[:, 0:128]), "sV")
                V[f"oev_{k}_{r}_{db}"] = C["sV"]

            weave_scan(k + 2, 16 + (r + 1))

            @blk.sync
            def _(sy, k=k, r=r):
                sy.wait_ge(SEM["sV"], V[f"oev_{k}_{r}_{DB-1}"])
                dmx(sy, "sOa" if r % 2 == 0 else "sOb",
                    d_out[512 * k + 128 * r:512 * k + 128 * r + 128, :],
                    s_otm[r % 2][:])
            _qo = "sOa" if r % 2 == 0 else "sOb"
            V[f"q_out_{k}_{r}"] = (_qo, C[_qo])

        @blk.vector
        def _(v, k=k):
            vnop(v)
        V[f"pass_done_{k}"] = C["sV"]

    @blk.sync
    def _(sy):
        for n in ["sQ", "sQS", "sU", "sY", "sOa", "sOb", "sQ2", "sQ2b", "sQ3", "sQW",
              "sQP", "sQXa", "sQXb",
                  "sW1a", "sW1b", "sW2a", "sW2b", "sW2c", "sW2d",
                  "sGUa", "sGUb", "sWDa", "sWDb", "sXC", "sYb"]:
            if C[n]:
                sy.wait_ge(SEM[n], C[n])

    es.close()
    return nc


_NC_CACHE = None


def _get_nc():
    global _NC_CACHE
    if _NC_CACHE is None:
        _NC_CACHE = build_nc()
    return _NC_CACHE


def make_in_maps(inputs):
    f32 = np.float32
    x = np.asarray(inputs["L0_attn_x"], f32)
    A = np.asarray(inputs["L0_attn_A_ssm"], f32)
    Bm = np.asarray(inputs["L0_attn_B_ssm"], f32)
    Cm = np.asarray(inputs["L0_attn_C_ssm"], f32)
    Dv = np.asarray(inputs["L0_attn_D_ssm"], f32)
    s1 = np.asarray(inputs["L0_attn_scale1"], f32)
    s2 = np.asarray(inputs["L0_attn_scale2"], f32)
    w1 = np.asarray(inputs["L0_attn_w1"], f32)
    b1 = np.asarray(inputs["L0_attn_b1"], f32)
    w2 = np.asarray(inputs["L0_attn_w2"], f32)
    b2 = np.asarray(inputs["L0_attn_b2"], f32)
    wg = np.asarray(inputs["L0_ffn_w_gate"], f32)
    wu = np.asarray(inputs["L0_ffn_w_up"], f32)
    wd = np.asarray(inputs["L0_ffn_w_down"], f32)
    cc = np.asarray(inputs["L0_c"], f32)
    wmod = np.asarray(inputs["L0_wmod"], f32)

    dt = np.exp(np.linspace(np.log(0.001), np.log(0.1), N16)).astype(f32)
    dt_mat = np.ascontiguousarray(np.broadcast_to(dt[None, :], (128, N16)))

    def fmvec(v):
        return np.ascontiguousarray(v.reshape(DB, 128).T)

    wg_pad = np.zeros((384 * NCORE, D), f32)
    wg_pad[:DFF2] = wg
    wu_pad = np.zeros((384 * NCORE, D), f32)
    wu_pad[:DFF2] = wu

    ident = np.eye(128, dtype=f32)
    identb = np.eye(128, dtype=ml_dtypes.bfloat16)
    ones_col = np.ones((128, 1), f32)
    ones_row = np.ones((1, 128), f32)

    in_maps = []
    for c in range(NCORE):
        b = c // 2
        half = c % 2
        rows = np.concatenate([np.arange(1024 * kk + 512 * half,
                                         1024 * kk + 512 * half + 512)
                               for kk in range(NSTR)])
        sel = np.zeros((128, B), f32)
        sel[:, b] = 1.0
        offs = np.zeros((1, 8), np.int32)
        for kk in range(NSTR):
            offs[0, kk] = kk * (NCORE * DB * 128 * STRIPE) + c * (128 * STRIPE)
            offs[0, 4 + kk] = kk * (NCORE * 128 * B * 1024) + b * 1024 + half * 512
        in_maps.append({
            "x_own": np.ascontiguousarray(x[b][rows]),
            "A_sl": np.ascontiguousarray(A[128 * c:128 * c + 128]),
            "B_sl": np.ascontiguousarray(Bm[128 * c:128 * c + 128]),
            "C_sl": np.ascontiguousarray(Cm[128 * c:128 * c + 128]),
            "dt_mat": dt_mat,
            "D_fm": fmvec(Dv), "s1_fm": fmvec(s1), "s2_fm": fmvec(s2),
            "b2_fm": fmvec(b2),
            "b1_fm": np.ascontiguousarray(b1.reshape(MC1, 128).T),
            "cT_fm": np.ascontiguousarray(
                cc.T.reshape(DB, 128, B).transpose(1, 0, 2)),
            "wmod_sl": np.ascontiguousarray(wmod[:, 768 * c:768 * c + 768]),
            "sel": sel,
            "w1_sh": np.ascontiguousarray(w1[:, 512 * c:512 * c + 512]),
            "w2_sh": np.ascontiguousarray(w2[:, 128 * c:128 * c + 128]),
            "wg_sh": np.ascontiguousarray(wg_pad[384 * c:384 * c + 384]),
            "wu_sh": np.ascontiguousarray(wu_pad[384 * c:384 * c + 384]),
            "wd_sh": np.ascontiguousarray(wd[128 * c:128 * c + 128]),
            "ident": ident, "identb": identb,
            "ones_col": ones_col, "ones_row": ones_row,
            "offs": offs,
        })
    return in_maps


def assemble(results):
    out = np.zeros((B, T, D), np.float32)
    for c in range(NCORE):
        b = c // 2
        half = c % 2
        rows = np.concatenate([np.arange(1024 * kk + 512 * half,
                                         1024 * kk + 512 * half + 512)
                               for kk in range(NSTR)])
        out[b][rows] = np.asarray(results[c]["out_own"], np.float32)
    return out


def kernel(**inputs):
    nc = _get_nc()
    in_maps = make_in_maps(inputs)
    res = run_bass_kernel_spmd(nc, in_maps, core_ids=list(range(NCORE)))
    return assemble(res.results)


# version 16
# speedup vs baseline: 1.1245x; 1.1245x over previous
import numpy as np
import ml_dtypes
from contextlib import ExitStack

import concourse.bass as bass
import concourse.mybir as mybir
from concourse.bass_utils import run_bass_kernel_spmd

F32 = mybir.dt.float32
BF16 = mybir.dt.bfloat16
I32 = mybir.dt.int32
Alu = mybir.AluOpType
Act = mybir.ActivationFunctionType
AX = mybir.AxisListType.X

B, T, D = 4, 4096, 1024
N16 = 16
DFF1 = 4096
DFF2 = 2816
NCORE = 8
TOK = 2048
STRIPE = 512
NSTR = 4
DB = 8
MC1 = 32
NC2 = 22
LNEPS = 1e-5
RMSEPS = 1e-6

OFF_W1 = 0
SZ_W1 = 1024 * 512
OFF_W2 = OFF_W1 + SZ_W1
SZ_W2 = 4096 * 128
OFF_WG = OFF_W2 + SZ_W2
SZ_WG = 1024 * 384
OFF_WU = OFF_WG + SZ_WG
SZ_WU = 1024 * 384
OFF_WD = OFF_WU + SZ_WU
SZ_WD = 2816 * 128
WSH = OFF_WD + SZ_WD

STAGE = 3


def build_nc():
    nc = bass.Bass("TRN2", target_bir_lowering=False)

    d_x = nc.dram_tensor("x_own", [TOK, D], F32, kind="ExternalInput")
    d_A = nc.dram_tensor("A_sl", [128, N16], F32, kind="ExternalInput")
    d_B = nc.dram_tensor("B_sl", [128, N16], F32, kind="ExternalInput")
    d_C = nc.dram_tensor("C_sl", [128, N16], F32, kind="ExternalInput")
    d_dt = nc.dram_tensor("dt_mat", [128, N16], F32, kind="ExternalInput")
    d_Dfm = nc.dram_tensor("D_fm", [128, DB], F32, kind="ExternalInput")
    d_s1 = nc.dram_tensor("s1_fm", [128, DB], F32, kind="ExternalInput")
    d_s2 = nc.dram_tensor("s2_fm", [128, DB], F32, kind="ExternalInput")
    d_b2 = nc.dram_tensor("b2_fm", [128, DB], F32, kind="ExternalInput")
    d_b1 = nc.dram_tensor("b1_fm", [128, MC1], F32, kind="ExternalInput")
    d_cT = nc.dram_tensor("cT_fm", [128, DB, B], F32, kind="ExternalInput")
    d_wmod = nc.dram_tensor("wmod_sl", [D, 768], F32, kind="ExternalInput")
    d_sel = nc.dram_tensor("sel", [128, B], F32, kind="ExternalInput")
    d_w1s = nc.dram_tensor("w1_sh", [D, 512], F32, kind="ExternalInput")
    d_w2s = nc.dram_tensor("w2_sh", [DFF1, 128], F32, kind="ExternalInput")
    d_wgs = nc.dram_tensor("wg_sh", [384, D], F32, kind="ExternalInput")
    d_wus = nc.dram_tensor("wu_sh", [384, D], F32, kind="ExternalInput")
    d_wds = nc.dram_tensor("wd_sh", [128, DFF2], F32, kind="ExternalInput")
    d_id32 = nc.dram_tensor("ident", [128, 128], F32, kind="ExternalInput")
    d_idbf = nc.dram_tensor("identb", [128, 128], BF16, kind="ExternalInput")
    d_onc = nc.dram_tensor("ones_col", [128, 1], F32, kind="ExternalInput")
    d_onr = nc.dram_tensor("ones_row", [1, 128], F32, kind="ExternalInput")
    d_offs = nc.dram_tensor("offs", [1, 8], I32, kind="ExternalInput")
    d_out = nc.dram_tensor("out_own", [TOK, D], F32, kind="ExternalOutput")

    h_xfm = nc.dram_tensor("h_xfm", [DB, 128, TOK], F32, kind="Internal")
    h_p = nc.dram_tensor("h_p", [DB, 128, TOK], BF16, kind="Internal")
    ag_u_in = nc.dram_tensor("ag_u_in", [NSTR, DB, 128, STRIPE], BF16, kind="Internal")
    ag_u_out = nc.dram_tensor("ag_u_out", [NSTR, NCORE, DB, 128, STRIPE], BF16, kind="Internal", addr_space="Shared")
    ag_y_in = nc.dram_tensor("ag_y_in", [NSTR, 128, B, 1024], BF16, kind="Internal")
    ag_y_out = nc.dram_tensor("ag_y_out", [NSTR, NCORE, 128, B, 1024], BF16, kind="Internal", addr_space="Shared")
    ag_w_in = nc.dram_tensor("ag_w_in", [WSH], BF16, kind="Internal")
    ag_w_out = nc.dram_tensor("ag_w_out", [NCORE, WSH], BF16, kind="Internal", addr_space="Shared")
    ag_m_in = nc.dram_tensor("ag_m_in", [6, 128, B], F32, kind="Internal")
    ag_m_out = nc.dram_tensor("ag_m_out", [NCORE, 6, 128, B], F32, kind="Internal", addr_space="Shared")

    RG = [list(range(NCORE))]
    es = ExitStack()

    def sbuf(name, shape, dtype):
        return es.enter_context(nc.sbuf_tensor(name, shape, dtype))

    C, SEM = {}, {}
    for n in ["sQ", "sQS", "sU", "sY", "sOa", "sOb", "sQ2", "sQ2b", "sQ3", "sQW",
              "sQP", "sQXa", "sQXb",
              "sW1a", "sW1b", "sW2a", "sW2b", "sW2c", "sW2d",
              "sGUa", "sGUb", "sWDa", "sWDb", "sXC",
              "sPE", "sV", "sS", "sAGu", "sAGy", "sAGw", "sAGm"]:
        SEM[n] = es.enter_context(nc.semaphore(n))
        C[n] = 0

    def inc(instr, name, k=1):
        instr.then_inc(SEM[name], k)
        C[name] += k
        return C[name]

    def dmx(eng, name, out_ap, in_ap):
        ins = eng.dma_start(out=out_ap, in_=in_ap)
        ins.then_inc(SEM[name], 16)
        C[name] += 16
        return C[name]

    V = {}

    s_id32 = sbuf("s_id32", [128, 128], F32)
    s_idbf = sbuf("s_idbf", [128, 128], BF16)
    s_onc = sbuf("s_onc", [128, 1], F32)
    s_onr = sbuf("s_onr", [1, 128], F32)
    s_oncb = sbuf("s_oncb", [128, 1], BF16)
    s_A = sbuf("s_A", [128, N16], F32)
    s_Bm = sbuf("s_Bm", [128, N16], F32)
    s_Cm = sbuf("s_Cm", [128, N16], F32)
    s_dt = sbuf("s_dt", [128, N16], F32)
    s_dA = sbuf("s_dA", [128, N16], F32)
    s_CdB = sbuf("s_CdB", [128, N16], F32)
    s_t16a = sbuf("s_t16a", [128, N16], F32)
    s_t16b = sbuf("s_t16b", [128, N16], F32)
    s_Dfm = sbuf("s_Dfm", [128, DB], F32)
    s_s1 = sbuf("s_s1", [128, DB], F32)
    s_s2 = sbuf("s_s2", [128, DB], F32)
    s_b2 = sbuf("s_b2", [128, DB], F32)
    s_b1 = sbuf("s_b1", [128, MC1], F32)
    s_sel = sbuf("s_sel", [128, B], F32)
    s_cT = sbuf("s_cT", [128, DB, B], F32)
    s_cTb = sbuf("s_cTb", [128, DB, B], F32)
    s_modsl = sbuf("s_modsl", [128, 6, B], F32)
    s_modal = sbuf("s_modal", [128, 48, B], F32)
    s_mod = sbuf("s_mod", [128, 48], F32)
    s_ca1 = sbuf("s_ca1", [128, DB], F32)
    s_sa = sbuf("s_sa", [128, DB], F32)
    s_ga = sbuf("s_ga", [128, DB], F32)
    s_cb1 = sbuf("s_cb1", [128, DB], F32)
    s_sbv = sbuf("s_sbv", [128, DB], F32)
    s_gb = sbuf("s_gb", [128, DB], F32)
    s_gtmp = sbuf("s_gtmp", [128, DB], F32)
    s_gtmp2 = sbuf("s_gtmp2", [128, DB], F32)
    s_offs = sbuf("s_offs", [1, 8], I32)
    s_state = sbuf("s_state", [128, B * N16], F32)

    a1 = sbuf("a1", [128, 32, 512], BF16)
    a2 = sbuf("a2", [128, 16, 512], BF16)
    F1 = sbuf("F1", [128, 4096], F32)
    F2 = sbuf("F2", [128, 4096], F32)
    F3 = sbuf("F3", [128, 3072], F32)


    sh_w = F1[:]
    sh_b = a1[:, 0:8, :].rearrange("p a b -> p (a b)")
    sh_f1 = F2[:, 0:3072]
    sh_f2 = F3[:, 0:3072]
    sh_i1 = a1[:, 16:22, :].rearrange("p a b -> p (a b)")
    sh_t = a1[:, 24:30, :].rearrange("p a b -> p (a b)").rearrange(
        "p (i m) -> p i m", m=128)

    s_xtm = [F3[:, 0:1024], F3[:, 1024:2048]]
    s_sq = sbuf("s_sq", [128, DB, STRIPE], BF16)
    s_t1 = s_sq
    s_xfm = sbuf("s_xfm", [128, DB, STRIPE], F32)
    sh_m = F3[:, 2560:2816]
    sh_i = F3[:, 2816:3072]
    s_h1 = a2[:, 0:8, :]
    s_u = a2[:, 8:16, :]
    s_pch = a1[:, 8:16, :]
    s_row = sbuf("s_row", [1, STRIPE], F32)
    s_row2 = sbuf("s_row2", [1, STRIPE], F32)

    s_ug2 = sbuf("s_ug2", [128, 2, DB, STRIPE], BF16)
    s_ug = [s_ug2[:, 0], s_ug2[:, 1]]
    s_z = [F2[:, 2048:3072], F2[:, 3072:4096]]
    s_yac2 = sbuf("s_yac2", [128, 2, 1024], F32)
    s_yst2 = sbuf("s_yst2", [128, 4096], BF16)

    s_yin = F3[:, 0:2048].bitcast(BF16).rearrange("p (a b) -> p a b", b=STRIPE)
    s_pin = a1[:, 22:30, :]
    s_xp = sbuf("s_xp", [128, DB, STRIPE], BF16)
    s_h2 = a2[:, 0:8, :]
    s_g = a1[:]
    s_xch = s_xfm
    s_x2 = a2[:, 8:16, :]
    s_h3 = a1[:, 22:30, :]
    s_xq = a1[:, 22:30, :]
    s_hq = a1[:, 0:22, :]
    s_x3 = F1[:].rearrange("p (a b) -> p a b", b=STRIPE)
    s_otm = [F2[:, 0:1024], F2[:, 1024:2048]]
    s_sg2 = sbuf("s_sg2", [128, 2 * STRIPE], BF16)

    s_w1t = [sbuf(f"s_w1t{i}", [128, DB, 128], BF16) for i in range(2)]
    s_w2t = [sbuf(f"s_w2t{i}", [128, 4, 128], BF16) for i in range(4)]
    s_wgt = [sbuf(f"s_wgt{i}", [128, DB, 128], BF16) for i in range(2)]
    s_wut = [sbuf(f"s_wut{i}", [128, DB, 128], BF16) for i in range(2)]
    s_wdt = [sbuf(f"s_wdt{i}", [128, NC2, 128], BF16) for i in range(2)]

    QW = 3072
    q_vt = sbuf("q_vt", [128, QW], BF16)
    q_sq = sbuf("q_sq", [128, QW], BF16)
    q_q1 = s_xp[:, 0:8, :].rearrange("p a b -> p (a b)")
    q_h5 = s_sq[:, 0:8, :].rearrange("p a b -> p (a b)")
    q_u = a2[:, 0:8, :].rearrange("p a b -> p (a b)")
    q_m = F3[:, 2048:2048 + QW // 16]
    q_in = F3[:, 2304:2304 + QW // 16]

    PG1 = es.enter_context(nc.psum_tensor("pg1", [128, 2048], F32))
    PG2 = es.enter_context(nc.psum_tensor("pg2", [128, 2048], F32))
    PA = PG1[:, 0:512]
    PX = PG1[:, 512:1024]
    PBK = PG1[:, 1024:1536]
    PY = PG1[:, 1536:2048]
    PACC = [PG2[:, 512 * i:512 * i + 512] for i in range(4)]

    blk = es.enter_context(nc.Block())

    r_u = [es.enter_context(nc.sync.register(f"r_u{k}")) for k in range(NSTR)]
    r_y = [es.enter_context(nc.sync.register(f"r_y{k}")) for k in range(NSTR)]

    def vnop(v, name="sV"):
        return inc(v.drain(), name)

    @blk.sync
    def _(sy):
        for d, s in [(d_id32, s_id32), (d_idbf, s_idbf), (d_onc, s_onc),
                     (d_onr, s_onr), (d_A, s_A), (d_B, s_Bm), (d_C, s_Cm),
                     (d_dt, s_dt), (d_Dfm, s_Dfm), (d_s1, s_s1), (d_s2, s_s2),
                     (d_b2, s_b2), (d_b1, s_b1), (d_sel, s_sel), (d_cT, s_cT),
                     (d_offs, s_offs)]:
            dmx(sy, "sQ", s[:], d[:])
        sy.wait_ge(SEM["sQ"], C["sQ"])
        for k in range(NSTR):
            sy.reg_load(r_u[k], s_offs[0:1, k:k + 1])
            sy.reg_load(r_y[k], s_offs[0:1, 4 + k:5 + k])
    V["q_small"] = C["sQ"]

    @blk.scalar
    def _(s):
        s.wait_ge(SEM["sQ"], V["q_small"])
        inc(s.activation(s_t16a[:], s_A[:], Act.Exp), "sS")
    V["s_expA"] = C["sS"]

    @blk.vector
    def _(v):
        v.wait_ge(SEM["sS"], V["s_expA"])
        v.tensor_tensor(s_t16b[:], s_t16a[:], s_dt[:], Alu.mult)
        inc(v.reciprocal(s_dA[:], s_t16a[:]), "sV")
    V["v_rec"] = C["sV"]

    @blk.scalar
    def _(s):
        s.wait_ge(SEM["sV"], V["v_rec"])
        inc(s.activation(s_t16a[:], s_t16b[:], Act.Exp, scale=-1.0), "sS")
    V["s_dA"] = C["sS"]

    @blk.vector
    def _(v):
        v.wait_ge(SEM["sS"], V["s_dA"])
        v.tensor_scalar(s_t16b[:], s_t16a[:], -1.0, 1.0, Alu.mult, Alu.add)
        v.drain()
        v.tensor_tensor(s_t16b[:], s_t16b[:], s_Bm[:], Alu.mult)
        v.drain()
        v.tensor_tensor(s_t16b[:], s_t16b[:], s_dA[:], Alu.mult)
        v.drain()
        v.tensor_tensor(s_CdB[:], s_t16b[:], s_Cm[:], Alu.mult)
        v.tensor_copy(s_dA[:], s_t16a[:])
        v.tensor_copy(s_oncb[:], s_onc[:])
        v.memset(s_state[:], 0.0)
        vnop(v)
    V["v_tabs"] = C["sV"]

    for h in range(2):
        @blk.gpsimd
        def _(g, h=h):
            if h == 1:
                g.wait_ge(SEM["sPE"], V["pe_mod2"])
            dmx(g, "sQ3", sh_w[:, 0:3072].rearrange("p (db m) -> p db m", db=DB), d_wmod[:, 384 * h:384 * h + 384].rearrange(
                    "(db p) m -> p db m", p=128))
        V[f"q3_wmod{h}"] = C["sQ3"]

        @blk.scalar
        def _(s, h=h):
            if h == 0:
                s.wait_ge(SEM["sQ"], V["q_small"])
                inc(s.activation(s_cTb[:], s_cT[:], Act.Silu), "sS")
        if h == 0:
            V["s_siluc"] = C["sS"]

        for jl in range(3):
            j = 3 * h + jl

            @blk.tensor
            def _(t, h=h, jl=jl, j=j):
                if j == 0:
                    t.wait_ge(SEM["sS"], V["s_siluc"])
                if jl == 0:
                    t.wait_ge(SEM["sQ3"], V[f"q3_wmod{h}"])
                if j > 0:
                    t.wait_ge(SEM["sV"], V[f"v_modev{j-1}"])
                wv = sh_w[:, 0:3072].rearrange("p (db j m) -> p db j m",
                                               db=DB, j=3)
                last = None
                for kc in range(DB):
                    last = t.matmul(PX[:, 0:B], wv[:, kc, jl, :],
                                    s_cTb[:, kc, :],
                                    start=(kc == 0), stop=(kc == DB - 1))
                inc(last, "sPE")
            V[f"pe_mod{j}"] = C["sPE"]

            @blk.vector
            def _(v, j=j):
                v.wait_ge(SEM["sPE"], V[f"pe_mod{j}"])
                inc(v.tensor_copy(s_modsl[:, j, :], PX[:, 0:B]), "sV")
            V[f"v_modev{j}"] = C["sV"]

    @blk.sync
    def _(sy):
        sy.wait_ge(SEM["sV"], V["v_modev5"])
        dmx(sy, "sQS", ag_m_in[:].transpose([1, 0, 2]), s_modsl[:])
    V["q_modin"] = C["sQS"]

    @blk.gpsimd
    def _(g):
        g.wait_ge(SEM["sQS"], V["q_modin"])
        inc(g.collective_compute("AllGather", Alu.bypass, replica_groups=RG,
                                 ins=[ag_m_in[:]], outs=[ag_m_out[:]]), "sAGm")
    V["g_mod"] = C["sAGm"]

    @blk.sync
    def _(sy):
        sy.wait_ge(SEM["sAGm"], V["g_mod"])
        dmx(sy, "sQ", s_modal[:], ag_m_out[:].rearrange("r s p b -> p (r s) b"))
    V["q_modal"] = C["sQ"]

    @blk.vector
    def _(v):
        v.wait_ge(SEM["sQ"], V["q_modal"])
        v.scalar_tensor_tensor(s_mod[:], s_modal[:, :, 0], s_sel[:, 0:1],
                               s_modal[:, :, 0], Alu.mult, Alu.bypass)
        for b in range(1, B):
            v.drain()
            v.scalar_tensor_tensor(s_mod[:], s_modal[:, :, b], s_sel[:, b:b + 1],
                                   s_mod[:], Alu.mult, Alu.add)
        v.drain()
        mv = s_mod[:].rearrange("p (v db) -> p v db", v=6)
        v.tensor_scalar(s_ca1[:], mv[:, 1, :], 1.0, None, Alu.add)
        v.tensor_copy(s_sa[:], mv[:, 0, :])
        v.tensor_scalar(s_cb1[:], mv[:, 4, :], 1.0, None, Alu.add)
        v.tensor_copy(s_sbv[:], mv[:, 3, :])
        for src_i, dst in [(2, s_ga), (5, s_gb)]:
            v.tensor_scalar(s_gtmp[:], mv[:, src_i, :], 0.0, 2.0,
                            Alu.is_ge, Alu.mult)
            v.drain()
            v.tensor_scalar(s_gtmp[:], s_gtmp[:], -1.0, None, Alu.add)
            v.tensor_scalar(s_gtmp2[:], mv[:, src_i, :], -1.0, None, Alu.mult)
            v.drain()
            v.tensor_tensor(s_gtmp2[:], s_gtmp2[:], mv[:, src_i, :], Alu.max)
            v.drain()
            v.tensor_scalar(s_gtmp2[:], s_gtmp2[:], 0.25, None, Alu.max)
            v.drain()
            v.tensor_tensor(dst[:], s_gtmp[:], s_gtmp2[:], Alu.mult)
            v.drain()
        vnop(v)
    V["v_mod"] = C["sV"]

    def emit_stats_rows(src, tag, want_mean, wait_s_v=None, wait_pe_v=None,
                        src_f32=False):
        @blk.scalar
        def _(s):
            if wait_s_v is not None:
                s.wait_ge(SEM["sV"], wait_s_v)
            inc(s.activation(s_sq[:], src[:], Act.Square), "sS")
        V[f"sq_{tag}"] = C["sS"]

        @blk.tensor
        def _(t):
            t.wait_ge(SEM["sS"], V[f"sq_{tag}"])
            if wait_pe_v is not None:
                t.wait_ge(SEM["sV"], wait_pe_v)
            last = None
            for db in range(DB):
                last = t.matmul(PY[0:1, 0:512], s_oncb[:], s_sq[:, db, :],
                                start=(db == 0), stop=(db == DB - 1))
            if want_mean:
                ones = s_onc[:] if src_f32 else s_oncb[:]
                for db in range(DB):
                    last = t.matmul(PX[0:1, 0:512], ones, src[:, db, :],
                                    start=(db == 0), stop=(db == DB - 1))
            inc(last, "sPE")
        V[f"st_{tag}"] = C["sPE"]

        @blk.vector
        def _(v):
            v.wait_ge(SEM["sPE"], V[f"st_{tag}"])
            if want_mean:
                v.tensor_scalar(s_row2[:], PX[0:1, 0:512], 1.0 / D, None, Alu.mult)
                v.drain()
                v.tensor_tensor(s_row[:], s_row2[:], s_row2[:], Alu.mult)
                v.drain()
                v.scalar_tensor_tensor(s_row[:], PY[0:1, 0:512], 1.0 / D,
                                       s_row[:], Alu.mult, Alu.subtract)
                v.drain()
                v.tensor_scalar(s_row[:], s_row[:], LNEPS, None, Alu.add)
            else:
                v.tensor_scalar(s_row[:], PY[0:1, 0:512], 1.0 / D, RMSEPS,
                                Alu.mult, Alu.add)
            vnop(v)
        V[f"stv_{tag}"] = C["sV"]

        @blk.scalar
        def _(s):
            s.wait_ge(SEM["sV"], V[f"stv_{tag}"])
            s.activation(s_row[:], s_row[:], Act.Ln)
            s.drain()
            inc(s.activation(s_row[:], s_row[:], Act.Exp, scale=-0.5), "sS")
        V[f"str_{tag}"] = C["sS"]

        @blk.vector
        def _(v):
            v.wait_ge(SEM["sS"], V[f"str_{tag}"])
            if want_mean:
                v.tensor_tensor(s_row2[:], s_row2[:], s_row[:], Alu.mult)
            vnop(v)
        V[f"strows_{tag}"] = C["sV"]

        @blk.tensor
        def _(t):
            t.wait_ge(SEM["sV"], V[f"strows_{tag}"])
            i1 = t.matmul(PX[:, 0:512], s_onr[:], s_row[:], start=True, stop=True)
            if want_mean:
                i1 = t.matmul(PY[:, 0:512], s_onr[:], s_row2[:],
                              start=True, stop=True)
            inc(i1, "sPE")
        V[f"stexp_{tag}"] = C["sPE"]

    def quant_steps(src_slc, dst_slc, F, tag, wait_v=None):
        nbm = F // 16
        mb = q_m[:, 0:nbm].rearrange("p (a b) -> p a b", b=2) \
            .unsqueeze(3).broadcast_to([128, F // 32, 2, 16])
        ib = q_in[:, 0:nbm].rearrange("p (a b) -> p a b", b=2) \
            .unsqueeze(3).broadcast_to([128, F // 32, 2, 16])

        def g4(ap):
            return ap.rearrange("p (a b k) -> p a b k", b=2, k=16)

        def mk(fn, first=False, last=False):
            def emit(fn=fn, first=first, last=last):
                @blk.vector
                def _(v):
                    if not first:
                        v.drain()
                    r = fn(v)
                    if last:
                        inc(r, "sV")
                        V[f"q_{tag}"] = C["sV"]
            return emit

        def s1(v):
            if wait_v is not None:
                v.wait_ge(SEM["sV"], wait_v)
            v.transpose(q_vt[:, 0:F], src_slc)

        def s2(v):
            v.tensor_reduce(q_m[:, 0:nbm], g4(q_vt[:, 0:F]), AX, Alu.max,
                            apply_absolute_value=True)

        def s3(v):
            v.tensor_scalar(q_m[:, 0:nbm], q_m[:, 0:nbm], 1.0 / 6.0, 1e-12,
                            Alu.mult, Alu.max)

        def s4(v):
            v.reciprocal(q_in[:, 0:nbm], q_m[:, 0:nbm])

        def s5(v):
            v.tensor_tensor(g4(q_vt[:, 0:F]), g4(q_vt[:, 0:F]), ib, Alu.mult)

        def s6(v):
            v.tensor_tensor(q_sq[:, 0:F], q_vt[:, 0:F], q_vt[:, 0:F], Alu.mult)
            v.tensor_scalar(q_u[:, 0:F], q_vt[:, 0:F], 0.0, 2.0,
                            Alu.is_gt, Alu.mult)

        def s7(v):
            v.tensor_scalar(q_q1[:, 0:F], q_sq[:, 0:F], 0.0625, 0.5,
                            Alu.is_gt, Alu.mult)
            v.tensor_scalar(q_h5[:, 0:F], q_sq[:, 0:F], 0.5625, 0.5,
                            Alu.is_gt, Alu.mult)
            v.tensor_scalar(q_vt[:, 0:F], q_sq[:, 0:F], 1.5625, 0.5,
                            Alu.is_gt, Alu.mult)
            v.tensor_scalar(q_u[:, 0:F], q_u[:, 0:F], -1.0, None, Alu.add)

        def s8(v):
            v.tensor_tensor(q_q1[:, 0:F], q_q1[:, 0:F], q_h5[:, 0:F], Alu.add)

        def s9(v):
            v.tensor_scalar(q_h5[:, 0:F], q_sq[:, 0:F], 3.0625, 0.5,
                            Alu.is_gt, Alu.mult)

        def s10(v):
            v.tensor_tensor(q_q1[:, 0:F], q_q1[:, 0:F], q_vt[:, 0:F], Alu.add)

        def s11(v):
            v.tensor_tensor(q_q1[:, 0:F], q_q1[:, 0:F], q_h5[:, 0:F], Alu.add)
            v.tensor_scalar(q_vt[:, 0:F], q_sq[:, 0:F], 6.25, None, Alu.is_gt)

        def s12(v):
            v.tensor_tensor(q_q1[:, 0:F], q_q1[:, 0:F], q_vt[:, 0:F], Alu.add)
            v.tensor_scalar(q_h5[:, 0:F], q_sq[:, 0:F], 12.25, None, Alu.is_gt)

        def s13(v):
            v.tensor_tensor(q_q1[:, 0:F], q_q1[:, 0:F], q_h5[:, 0:F], Alu.add)
            v.tensor_scalar(q_vt[:, 0:F], q_sq[:, 0:F], 25.0, 2.0,
                            Alu.is_gt, Alu.mult)

        def s14(v):
            v.tensor_tensor(q_q1[:, 0:F], q_q1[:, 0:F], q_vt[:, 0:F], Alu.add)

        def s15(v):
            v.tensor_tensor(q_q1[:, 0:F], q_q1[:, 0:F], q_u[:, 0:F], Alu.mult)

        def s16(v):
            v.tensor_tensor(g4(q_q1[:, 0:F]), g4(q_q1[:, 0:F]), mb, Alu.mult)

        def s17(v):
            return v.transpose(dst_slc, q_q1[:, 0:F])

        fns = [s1, s2, s3, s4, s5, s6, s7, s8, s9, s10, s11, s12, s13,
               s14, s15, s16, s17]
        return [mk(f, first=(i == 0), last=(i == len(fns) - 1))
                for i, f in enumerate(fns)]

    def emit_quant(src_slc, dst_slc, F, tag, wait_v=None):
        for st in quant_steps(src_slc, dst_slc, F, tag, wait_v):
            st()

    def emit_ug_load(k):
        @blk.sync
        def _(sy, k=k):
            sy.wait_ge(SEM["sAGu"], V[f"agu_{k}"])
            if k >= 2:
                sy.wait_ge(SEM["sV"], V[f"scn_{k-2}_{B*N16-1}"])
            src = bass.AP(ag_u_out, r_u[k],
                          [[STRIPE, 128], [DB * 128 * STRIPE, NCORE], [1, STRIPE]])
            qn = "sQ2" if k % 2 == 0 else "sQ2b"
            dmx(sy, qn, s_ug[k % 2][:], src)
        V[f"q_ug_{k}"] = (("sQ2" if k % 2 == 0 else "sQ2b"),
                          C["sQ2" if k % 2 == 0 else "sQ2b"])

    def emit_scan_iter(k, i):
        b, n = divmod(i, N16)
        par = i % 2

        @blk.vector
        def _(v, k=k, b=b, n=n, i=i, par=par):
            if i == 0:
                qs, qv = V[f"q_ug_{k}"]
                v.wait_ge(SEM[qs], qv)
            st = b * N16 + n
            init = 0.0 if k == 0 else s_state[:, st:st + 1]
            uview = s_ug[k % 2][:].rearrange("p s t -> p (s t)")
            v.tensor_tensor_scan(s_z[par][:],
                                 s_dA[:, n:n + 1].broadcast_to([128, 1024]),
                                 uview[:, 1024 * b:1024 * b + 1024],
                                 init, Alu.mult, Alu.add)
            v.drain()
            v.tensor_copy(s_state[:, st:st + 1], s_z[par][:, 1023:1024])
            if n == 0:
                last = v.tensor_scalar(s_yac2[:, b % 2, :], s_z[par][:],
                                       s_CdB[:, n:n + 1], None, Alu.mult)
            else:
                last = v.scalar_tensor_tensor(s_yac2[:, b % 2, :], s_z[par][:],
                                              s_CdB[:, n:n + 1],
                                              s_yac2[:, b % 2, :],
                                              Alu.mult, Alu.add)
            inc(last, "sV")
        V[f"scn_{k}_{i}"] = C["sV"]

    def emit_scan_flush(k, half):
        @blk.vector
        def _(v, k=k, half=half):
            if half == 0 and k > 0:
                v.wait_ge(SEM["sY"], V[f"q_yst_{k-1}"])
            v.drain()
            v.tensor_copy(s_yst2[:, 2048 * half:2048 * half + 2048],
                          s_yac2[:].rearrange("p a b -> p (a b)"))
            vnop(v)
        V[f"yfl_{k}_{half}"] = C["sV"]

    def emit_scan_store(k):
        @blk.gpsimd
        def _(g, k=k):
            g.wait_ge(SEM["sV"], V[f"yfl_{k}_1"])
            dmx(g, "sY", ag_y_in[k], s_yst2[:])
        V[f"q_yst_{k}"] = C["sY"]

        @blk.gpsimd
        def _(g, k=k):
            g.wait_ge(SEM["sY"], V[f"q_yst_{k}"])
            inc(g.collective_compute("AllGather", Alu.bypass, replica_groups=RG,
                                     ins=[ag_y_in[k]], outs=[ag_y_out[k]]), "sAGy")
        V[f"agy_{k}"] = C["sAGy"]

    scan_cur = {k: 0 for k in range(NSTR)}

    def weave_scan(k, upto):
        if k >= NSTR:
            return
        while scan_cur[k] < min(upto, B * N16):
            i = scan_cur[k]
            emit_scan_iter(k, i)
            scan_cur[k] += 1
            if i == 31:
                emit_scan_flush(k, 0)
            elif i == 63:
                emit_scan_flush(k, 1)
                emit_scan_store(k)

    def emit_p1_stripe(k):
        for r in range(4):
            idx = 4 * k + r

            @blk.sync
            def _(sy, k=k, r=r, idx=idx):
                if idx == 0:
                    pass
                if idx >= 2:
                    pk, pr = divmod(idx - 2, 4)
                    sy.wait_ge(SEM["sPE"], V[f"xT_{pk}_{pr}"])
                qn = "sQXa" if idx % 2 == 0 else "sQXb"
                dmx(sy, qn, s_xtm[idx % 2][:],
                    d_x[512 * k + 128 * r:512 * k + 128 * r + 128, :])
            V[f"q_x_{k}_{r}"] = ("sQXa" if (4 * k + r) % 2 == 0 else "sQXb",
                                 C["sQXa" if (4 * k + r) % 2 == 0 else "sQXb"])

            for db in range(DB):
                @blk.tensor
                def _(t, k=k, r=r, db=db, idx=idx):
                    if db == 0:
                        qs, qv = V[f"q_x_{k}_{r}"]
                        t.wait_ge(SEM[qs], qv)
                        if idx == 0:
                            t.wait_ge(SEM["sV"], V["v_mod"])
                        elif r == 0 and k > 0:
                            t.wait_ge(SEM["sV"], V[f"up_{k-1}"])
                            if k == 1:
                                t.wait_ge(SEM["sV"], V["shTev_wd_21"])
                    if db >= 2:
                        t.wait_ge(SEM["sV"], V[f"xev_{k}_{r}_{db-2}"])
                    elif idx > 0:
                        pk, pr = divmod(idx - 1, 4)
                        t.wait_ge(SEM["sV"], V[f"xev_{pk}_{pr}_{DB-2+db}"])
                    inc(t.transpose((PX if db % 2 == 0 else PY)[:, 0:128],
                                    s_xtm[idx % 2][:, 128 * db:128 * db + 128],
                                    s_id32[:]), "sPE")
                V[f"xTp_{k}_{r}_{db}"] = C["sPE"]

                @blk.vector
                def _(v, k=k, r=r, db=db):
                    v.wait_ge(SEM["sPE"], V[f"xTp_{k}_{r}_{db}"])
                    if k > 0 and r == 0 and db == 0:
                        v.wait_ge(SEM["sQP"], V[f"q_xsp_{k-1}"])
                    ins = v.tensor_copy(s_xfm[:, db, 128 * r:128 * r + 128],
                                        (PX if db % 2 == 0 else PY)[:, 0:128])
                    inc(ins, "sV")
                V[f"xev_{k}_{r}_{db}"] = C["sV"]
            V[f"xT_{k}_{r}"] = C["sPE"]
        V[f"xev_last_{k}"] = C["sV"]

        @blk.sync
        def _(sy, k=k):
            sy.wait_ge(SEM["sV"], V[f"xev_last_{k}"])
            dmx(sy, "sQP", h_xfm[:, :, 512 * k:512 * k + 512].transpose([1, 0, 2]), s_xfm[:])
        V[f"q_xsp_{k}"] = C["sQP"]

        emit_stats_rows(s_xfm, f"ln1_{k}", True,
                        wait_s_v=V[f"xev_last_{k}"],
                        wait_pe_v=(V[f"up_{k-1}"] if k > 0 else None),
                        src_f32=True)

        @blk.vector
        def _(v, k=k):
            v.wait_ge(SEM["sPE"], V[f"stexp_ln1_{k}"])
            pxb = PX[:, 0:512].unsqueeze(1).broadcast_to([128, DB, 512])
            pyb = PY[:, 0:512].unsqueeze(1).broadcast_to([128, DB, 512])
            v.tensor_tensor(s_t1[:], s_xfm[:], pxb, Alu.mult)
            v.drain()
            v.tensor_tensor(s_t1[:], s_t1[:], pyb, Alu.subtract)
            v.drain()
            for db in range(DB):
                v.tensor_scalar(s_h1[:, db, :], s_t1[:, db, :],
                                s_ca1[:, db:db + 1], s_sa[:, db:db + 1],
                                Alu.mult, Alu.add)
            vnop(v)
        V[f"h1_{k}"] = C["sV"]

        emit_stats_rows(s_h1, f"rms1_{k}", False,
                        wait_s_v=V[f"h1_{k}"], wait_pe_v=V[f"h1_{k}"])

        @blk.vector
        def _(v, k=k):
            v.wait_ge(SEM["sPE"], V[f"stexp_rms1_{k}"])
            if k > 0:
                v.wait_ge(SEM["sU"], V[f"q_ust_{k-1}"])
            pxb = PX[:, 0:512].unsqueeze(1).broadcast_to([128, DB, 512])
            v.tensor_tensor(s_t1[:], s_h1[:], pxb, Alu.mult)
            v.drain()
            for db in range(DB):
                v.tensor_scalar(s_u[:, db, :], s_t1[:, db, :],
                                s_s1[:, db:db + 1], None, Alu.mult)
            v.drain()
            for db in range(DB):
                v.scalar_tensor_tensor(s_pch[:, db, :], s_u[:, db, :],
                                       s_Dfm[:, db:db + 1], s_h1[:, db, :],
                                       Alu.mult, Alu.add)
            vnop(v)
        V[f"up_{k}"] = C["sV"]

        @blk.sync
        def _(sy, k=k):
            sy.wait_ge(SEM["sV"], V[f"up_{k}"])
            dmx(sy, "sU", ag_u_in[k].transpose([1, 0, 2]), s_u[:])
            dmx(sy, "sU", h_p[:, :, 512 * k:512 * k + 512].transpose([1, 0, 2]), s_pch[:])
        V[f"q_ust_{k}"] = C["sU"]

        @blk.gpsimd
        def _(g, k=k):
            g.wait_ge(SEM["sU"], V[f"q_ust_{k}"])
            inc(g.collective_compute("AllGather", Alu.bypass, replica_groups=RG,
                                     ins=[ag_u_in[k]], outs=[ag_u_out[k]]), "sAGu")
        V[f"agu_{k}"] = C["sAGu"]

        if k == 1:
            emit_ug_load(0)
            weave_scan(0, 24)
        elif k == 2:
            emit_ug_load(1)
            weave_scan(0, 48)
        elif k == 3:
            weave_scan(0, 64)

    emit_p1_stripe(0)

    def emit_shard_quant(n, tag, wait_pe=None):
        nb16 = n // 16
        mv = sh_m[:, 0:nb16]
        iv = sh_i[:, 0:nb16]
        mb = mv.unsqueeze(2).broadcast_to([128, nb16, 16])
        ib = iv.unsqueeze(2).broadcast_to([128, nb16, 16])

        def g3(ap):
            return ap.rearrange("p (g k) -> p g k", k=16)

        sqv = sh_w[:, 0:n]
        qb = sh_b[:, 0:n]
        tmp = sh_i1[:, 0:n]
        xn = sh_f1[:, 0:n]

        @blk.vector
        def _(v):
            v.wait_ge(SEM["sQ3"], V[f"q3_{tag}"])
            bsem, bval = V[f"shbfree_{tag}"]
            v.wait_ge(SEM[bsem], bval)
            if wait_pe is not None:
                v.wait_ge(SEM["sPE"], wait_pe)
            v.tensor_reduce(mv, g3(sh_w[:, 0:n]), AX, Alu.max,
                            apply_absolute_value=True)
            v.drain()
            v.tensor_scalar(mv, mv, 1.0 / 6.0, 1e-12, Alu.mult, Alu.max)
            v.drain()
            v.reciprocal(iv, mv)
            v.drain()
            v.tensor_tensor(g3(xn), g3(sh_w[:, 0:n]), ib, Alu.mult)
            v.drain()
            v.tensor_tensor(sqv, xn, xn, Alu.mult)
            v.drain()
            v.tensor_scalar(qb, sqv, 0.0625, 0.5, Alu.is_gt, Alu.mult)
            v.tensor_scalar(tmp, sqv, 0.5625, 0.5, Alu.is_gt, Alu.mult)
            v.tensor_scalar(xn, xn, 0.0, 2.0, Alu.is_gt, Alu.mult)
            v.drain()
            v.tensor_tensor(qb, qb, tmp, Alu.add)
            v.tensor_scalar(xn, xn, -1.0, None, Alu.add)
            v.drain()
            for m2, w in [(1.5625, 0.5), (3.0625, 0.5), (6.25, None),
                          (12.25, None), (25.0, 2.0)]:
                if w is None:
                    v.tensor_scalar(tmp, sqv, m2, None, Alu.is_gt)
                else:
                    v.tensor_scalar(tmp, sqv, m2, w, Alu.is_gt, Alu.mult)
                v.drain()
                v.tensor_tensor(qb, qb, tmp, Alu.add)
                v.drain()
            v.tensor_tensor(xn, qb, xn, Alu.mult)
            v.drain()
            inc(v.tensor_tensor(g3(xn), g3(xn), mb, Alu.mult), "sV")
        V[f"shq_{tag}"] = C["sV"]

    @blk.gpsimd
    def _(g):
        g.wait_ge(SEM["sPE"], V["pe_mod5"])
        dmx(g, "sQ3", sh_w[:].rearrange("p (kc m) -> p kc m", kc=DB), d_w1s[:].rearrange("(kc p) m -> p kc m", p=128))
    V["q3_w1s"] = C["sQ3"]

    @blk.vector
    def _(v):
        v.wait_ge(SEM["sQ3"], V["q3_w1s"])
        v.wait_ge(SEM["sPE"], V["pe_mod5"])
        inc(v.tensor_copy(sh_b[:], sh_w[:]), "sV")
    V["v_w1b"] = C["sV"]

    @blk.sync
    def _(sy):
        sy.wait_ge(SEM["sV"], V["v_w1b"])
        dmx(sy, "sQW", ag_w_in[OFF_W1:OFF_W1 + SZ_W1].rearrange(
                "(kc p m) -> p kc m", p=128, m=512), sh_b[:].rearrange("p (kc m) -> p kc m", kc=DB))
    V["q_w1in"] = C["sQW"]

    @blk.gpsimd
    def _(g):
        g.wait_ge(SEM["sV"], V["v_w1b"])
        dmx(g, "sQ3", sh_w[:].rearrange("p (kc m) -> p kc m", kc=MC1), d_w2s[:].rearrange("(kc p) m -> p kc m", p=128))
    V["q3_w2s"] = C["sQ3"]

    @blk.vector
    def _(v):
        v.wait_ge(SEM["sQ3"], V["q3_w2s"])
        v.wait_ge(SEM["sQW"], V["q_w1in"])
        inc(v.tensor_copy(sh_b[:], sh_w[:]), "sV")
    V["v_w2b"] = C["sV"]

    @blk.sync
    def _(sy):
        sy.wait_ge(SEM["sV"], V["v_w2b"])
        dmx(sy, "sQW", ag_w_in[OFF_W2:OFF_W2 + SZ_W2].rearrange(
                "(kc p m) -> p kc m", p=128, m=128), sh_b[:].rearrange("p (kc m) -> p kc m", kc=MC1))
    V["q_w2in"] = C["sQW"]

    def emit_ffn_shard(d_src, off, nblk, kdim, tag, prev_w_free, prev_b_free,
                       prev_pe=None):
        n = nblk * kdim
        n_kc = kdim // 128

        @blk.gpsimd
        def _(g):
            g.wait_ge(SEM["sV"], prev_w_free)
            dmx(g, "sQ3", sh_w[:, 0:n].rearrange("p (nb k) -> p nb k", nb=nblk), d_src[:].rearrange("(nb p) k -> p nb k", p=128))
        V[f"q3_{tag}"] = C["sQ3"]
        V[f"shbfree_{tag}"] = prev_b_free
        emit_shard_quant(n, tag, wait_pe=prev_pe)

        ntile = nblk * n_kc
        for idx in range(ntile):
            nb, kc = idx // n_kc, idx % n_kc

            @blk.tensor
            def _(t, nb=nb, kc=kc, idx=idx):
                if idx == 0:
                    t.wait_ge(SEM["sV"], V[f"shq_{tag}"])
                if idx >= 2:
                    t.wait_ge(SEM["sV"], V[f"shTev_{tag}_{idx-2}"])
                src = sh_f1[:, 0:n].rearrange("p (nb k) -> p nb k", nb=nblk)
                inc(t.transpose((PX if idx % 2 == 0 else PY)[:, 0:128],
                                src[:, nb, 128 * kc:128 * kc + 128], s_id32[:]),
                    "sPE")
            V[f"shTpe_{tag}_{idx}"] = C["sPE"]

            @blk.vector
            def _(v, idx=idx):
                v.wait_ge(SEM["sPE"], V[f"shTpe_{tag}_{idx}"])
                inc(v.tensor_copy(sh_t[:, idx, :],
                                  (PX if idx % 2 == 0 else PY)[:, 0:128]), "sV")
            V[f"shTev_{tag}_{idx}"] = C["sV"]

        @blk.sync
        def _(sy):
            sy.wait_ge(SEM["sV"], V[f"shTev_{tag}_{ntile-1}"])
            shp = sh_t[:, 0:ntile, :].rearrange("p (nb kc) m -> p kc nb m",
                                                nb=nblk)
            hv = ag_w_in[off:off + n_kc * 128 * nblk * 128].rearrange(
                "(kc p nb m) -> p kc nb m", p=128, nb=nblk, m=128)
            if nblk == 1:
                dmx(sy, "sQW", hv[:, :, 0, :], shp[:, :, 0, :])
            else:
                for kc in range(n_kc):
                    dmx(sy, "sQW", hv[:, kc, :, :], shp[:, kc, :, :])
        V[f"q_{tag}in"] = C["sQW"]
        return V[f"shq_{tag}"], V[f"shq_{tag}"]

    wfree, bfree = emit_ffn_shard(d_wgs, OFF_WG, 3, 1024, "wg",
                                  V["v_w2b"], ("sQW", V["q_w2in"]))
    wfree, bfree = emit_ffn_shard(d_wus, OFF_WU, 3, 1024, "wu", wfree,
                                  ("sQW", V["q_wgin"]),
                                  prev_pe=V["shTpe_wg_23"])
    wfree, bfree = emit_ffn_shard(d_wds, OFF_WD, 1, 2816, "wd", wfree,
                                  ("sQW", V["q_wuin"]),
                                  prev_pe=V["shTpe_wu_23"])

    @blk.gpsimd
    def _(g):
        g.wait_ge(SEM["sQW"], V["q_wdin"])
        inc(g.collective_compute("AllGather", Alu.bypass, replica_groups=RG,
                                 ins=[ag_w_in[:]], outs=[ag_w_out[:]]), "sAGw")
    V["g_w"] = C["sAGw"]



    for _k1 in range(1, NSTR):
        emit_p1_stripe(_k1)

    def emit_pass_front(k):
        @blk.sync
        def _(sy, k=k):
            sy.wait_ge(SEM["sAGy"], V[f"agy_{k}"])
            if k == 0:
                sy.wait_ge(SEM["sU"], V[f"q_ust_{NSTR-1}"])
                sy.wait_ge(SEM["sPE"], V[f"xT_{NSTR-1}_3"])
                sy.wait_ge(SEM["sQW"], V["q_wdin"])
            else:
                sy.wait_ge(SEM["sPE"], V[f"gu_{k-1}_{NC2-1}"])
            src = bass.AP(ag_y_out, r_y[k],
                          [[B * 1024, 128], [128 * B * 1024, DB], [1, STRIPE]])
            dmx(sy, "sQ", s_yin[:], src)
            dmx(sy, "sQ", s_pin[:], h_p[:, :, 512 * k:512 * k + 512].transpose([1, 0, 2]))
        V[f"q_yin_{k}"] = C["sQ"]

        @blk.sync
        def _(sy, k=k):
            if k == 0:
                sy.wait_ge(SEM["sQP"], V[f"q_xsp_{NSTR-1}"])
            else:
                sy.wait_ge(SEM["sV"], V[f"ep_{k-1}_1"])
            dmx(sy, "sXC", s_xch[:], h_xfm[:, :, 512 * k:512 * k + 512].transpose([1, 0, 2]))
        V[f"q_xch_{k}"] = C["sXC"]

        @blk.vector
        def _(v, k=k):
            v.wait_ge(SEM["sQ"], V[f"q_yin_{k}"])
            v.drain()
            inc(v.tensor_tensor(s_xp[:], s_pin[:], s_yin[:], Alu.add), "sV")
        V[f"xp_{k}"] = C["sV"]

        if STAGE >= 2:
            emit_stats_rows(s_xp, f"rms2_{k}", False,
                            wait_s_v=V[f"xp_{k}"], wait_pe_v=V[f"xp_{k}"])

            @blk.vector
            def _(v, k=k):
                v.wait_ge(SEM["sPE"], V[f"stexp_rms2_{k}"])
                pxb = PX[:, 0:512].unsqueeze(1).broadcast_to([128, DB, 512])
                v.tensor_tensor(s_t1[:], s_xp[:], pxb, Alu.mult)
                v.drain()
                for db in range(DB):
                    v.tensor_scalar(s_h2[:, db, :], s_t1[:, db, :],
                                    s_s2[:, db:db + 1], None, Alu.mult)
                vnop(v)
            V[f"h2_{k}"] = C["sV"]

        if k + 2 < NSTR:
            emit_ug_load(k + 2)

    for k in range(NSTR):
        if k == 0:
            emit_pass_front(0)

        if STAGE == 1:
            @blk.vector
            def _(v, k=k):
                v.wait_ge(SEM["sV"], V[f"xp_{k}"])
                v.tensor_copy(s_x3[:], s_xp[:])
                vnop(v)
            V[f"x3r_{k}"] = C["sV"]
        else:
            w1_scan_base = scan_cur.get(k + 1, 0) if k + 1 < NSTR else 0
            for mc in range(MC1):
                @blk.sync
                def _(s, k=k, mc=mc):
                    if k == 0 and mc == 0:
                        s.wait_ge(SEM["sAGw"], V["g_w"])
                    if mc >= 2:
                        s.wait_ge(SEM["sPE"], V[f"w1mm_{k}_{mc-2}"])
                    elif k > 0:
                        s.wait_ge(SEM["sPE"], V[f"w1mm_{k-1}_{MC1-1}"])
                    rank, col = mc // 4, mc % 4
                    off = rank * WSH + OFF_W1 + col * 128
                    src = bass.AP(ag_w_out, off,
                                  [[512, 128], [128 * 512, DB], [1, 128]])
                    dmx(s, "sW1a" if mc % 2 == 0 else "sW1b", s_w1t[mc % 2][:], src)
                V[f"q_w1t_{k}_{mc}"] = ("sW1a" if mc % 2 == 0 else "sW1b",
                                        C["sW1a" if mc % 2 == 0 else "sW1b"])

                @blk.tensor
                def _(t, k=k, mc=mc):
                    qs, qv = V[f"q_w1t_{k}_{mc}"]
                    t.wait_ge(SEM[qs], qv)
                    if mc == 0:
                        t.wait_ge(SEM["sV"], V[f"h2_{k}"])
                        if k > 0 and STAGE >= 3:
                            t.wait_ge(SEM["sS"], V[f"sil_{k-1}_{NC2//2-1}"])
                            t.wait_ge(SEM["sV"], V[f"hmul_{k-1}_{NC2//2-1}"])
                    if mc >= 4:
                        t.wait_ge(SEM["sS"], V[f"gel_{k}_{mc-4}"])
                    pdst = [PA, PX, PBK, PY][mc % 4][:, 0:512]
                    last = None
                    for kc in range(DB):
                        last = t.matmul(pdst, s_w1t[mc % 2][:, kc, :],
                                        s_h2[:, kc, :], start=(kc == 0),
                                        stop=(kc == DB - 1))
                    inc(last, "sPE")
                V[f"w1mm_{k}_{mc}"] = C["sPE"]

                @blk.scalar
                def _(s, k=k, mc=mc):
                    s.wait_ge(SEM["sPE"], V[f"w1mm_{k}_{mc}"])
                    inc(s.activation(s_g[:, mc, :],
                                     [PA, PX, PBK, PY][mc % 4][:, 0:512],
                                     Act.Gelu, bias=s_b1[:, mc:mc + 1]), "sS")
                V[f"gel_{k}_{mc}"] = C["sS"]

                if k + 1 < NSTR and w1_scan_base < 56:
                    base = w1_scan_base
                    weave_scan(k + 1, base + ((56 - base) * (mc + 1)) // MC1)

            for half in range(2):
                for mc in range(MC1):
                    gidx = half * MC1 + mc

                    @blk.sync
                    def _(s, k=k, half=half, mc=mc, gidx=gidx):
                        if gidx >= 4:
                            ph, pm = divmod(gidx - 4, MC1)
                            s.wait_ge(SEM["sPE"], V[f"w2mm_{k}_{ph}_{pm}"])
                        elif k > 0 and gidx < 4:
                            s.wait_ge(SEM["sPE"], V[f"w2mm_{k-1}_{1}_{MC1-4+gidx}"])
                        qn = ["sW2a", "sW2b", "sW2c", "sW2d"][gidx % 4]
                        off = (half * 4) * WSH + OFF_W2 + mc * 128 * 128
                        src = bass.AP(ag_w_out, off,
                                      [[128, 128], [WSH, 4], [1, 128]])
                        dmx(s, qn, s_w2t[gidx % 4][:], src)
                    _q2 = ["sW2a", "sW2b", "sW2c", "sW2d"][gidx % 4]
                    V[f"q_w2t_{k}_{half}_{mc}"] = (_q2, C[_q2])

                    @blk.tensor
                    def _(t, k=k, half=half, mc=mc):
                        qs, qv = V[f"q_w2t_{k}_{half}_{mc}"]
                        t.wait_ge(SEM[qs], qv)
                        t.wait_ge(SEM["sS"], V[f"gel_{k}_{mc}"])
                        if mc == 0:
                            if half == 1:
                                t.wait_ge(SEM["sV"], V[f"ep_{k}_0"])
                            elif k > 0:
                                t.wait_ge(SEM["sV"],
                                          V[f"x3ep_{k-1}_1"] if STAGE >= 3
                                          else V[f"ep_{k-1}_1"])
                        last = None
                        gidx = half * MC1 + mc
                        for dcl in range(4):
                            last = t.matmul(PACC[dcl][:, 0:512],
                                            s_w2t[gidx % 4][:, dcl, :],
                                            s_g[:, mc, :], start=(mc == 0),
                                            stop=(mc == MC1 - 1))
                        inc(last, "sPE")
                    V[f"w2mm_{k}_{half}_{mc}"] = C["sPE"]

                @blk.vector
                def _(v, k=k, half=half):
                    v.wait_ge(SEM["sPE"], V[f"w2mm_{k}_{half}_{MC1-1}"])
                    if half == 0:
                        v.wait_ge(SEM["sXC"], V[f"q_xch_{k}"])
                    for dcl in range(4):
                        dc = half * 4 + dcl
                        v.scalar_tensor_tensor(s_t1[:, dc, :], PACC[dcl][:, 0:512],
                                               s_b2[:, dc:dc + 1], s_xp[:, dc, :],
                                               Alu.add, Alu.add)
                        v.drain()
                        v.scalar_tensor_tensor(s_x2[:, dc, :], s_t1[:, dc, :],
                                               s_ga[:, dc:dc + 1], s_xch[:, dc, :],
                                               Alu.mult, Alu.add)
                        v.drain()
                    inc(v.drain(), "sV")
                V[f"ep_{k}_{half}"] = C["sV"]

                if half == 0:
                    weave_scan(k + 1, 64)

            if STAGE == 2:
                @blk.vector
                def _(v, k=k):
                    v.wait_ge(SEM["sV"], V[f"ep_{k}_1"])
                    v.tensor_copy(s_x3[:], s_x2[:])
                    vnop(v)
                V[f"x3r_{k}"] = C["sV"]
            else:
                emit_stats_rows(s_x2, f"ln2_{k}", True,
                                wait_s_v=V[f"ep_{k}_1"], wait_pe_v=V[f"ep_{k}_1"])

                @blk.vector
                def _(v, k=k):
                    v.wait_ge(SEM["sPE"], V[f"stexp_ln2_{k}"])
                    pxb = PX[:, 0:512].unsqueeze(1).broadcast_to([128, DB, 512])
                    pyb = PY[:, 0:512].unsqueeze(1).broadcast_to([128, DB, 512])
                    v.tensor_tensor(s_t1[:], s_x2[:], pxb, Alu.mult)
                    v.drain()
                    v.tensor_tensor(s_t1[:], s_t1[:], pyb, Alu.subtract)
                    v.drain()
                    for db in range(DB):
                        v.tensor_scalar(s_h3[:, db, :], s_t1[:, db, :],
                                        s_cb1[:, db:db + 1], s_sbv[:, db:db + 1],
                                        Alu.mult, Alu.add)
                    vnop(v)
                V[f"h3_{k}"] = C["sV"]

                prev = V[f"h3_{k}"]
                for ch in range(4):
                    emit_quant(s_h3[:, 2 * ch:2 * ch + 2, :]
                               .rearrange("p a b -> p (a b)"),
                               s_xq[:, 2 * ch:2 * ch + 2, :]
                               .rearrange("p a b -> p (a b)"),
                               2 * STRIPE, f"xq_{k}_{ch}", wait_v=prev)
                    prev = V[f"q_xq_{k}_{ch}"]
                V[f"xq_{k}"] = prev

                hq_queue = []
                for nc_ in range(NC2):
                    @blk.sync
                    def _(s, k=k, nc_=nc_):
                        if nc_ >= 2:
                            s.wait_ge(SEM["sPE"], V[f"gu_{k}_{nc_-2}"])
                        rank, col = nc_ // 3, nc_ % 3
                        offg = rank * WSH + OFF_WG + col * 128
                        offu = rank * WSH + OFF_WU + col * 128
                        srcg = bass.AP(ag_w_out, offg,
                                       [[384, 128], [128 * 384, DB], [1, 128]])
                        srcu = bass.AP(ag_w_out, offu,
                                       [[384, 128], [128 * 384, DB], [1, 128]])
                        qn = "sGUa" if nc_ % 2 == 0 else "sGUb"
                        dmx(s, qn, s_wgt[nc_ % 2][:], srcg)
                        dmx(s, qn, s_wut[nc_ % 2][:], srcu)
                    _qg = "sGUa" if nc_ % 2 == 0 else "sGUb"
                    V[f"q_gu_{k}_{nc_}"] = (_qg, C[_qg])

                    @blk.tensor
                    def _(t, k=k, nc_=nc_):
                        qs, qv = V[f"q_gu_{k}_{nc_}"]
                        t.wait_ge(SEM[qs], qv)
                        p = nc_ // 2
                        if nc_ == 0:
                            t.wait_ge(SEM["sV"], V[f"q_xq_{k}_0"])
                            t.wait_ge(SEM["sS"], V[f"gel_{k}_{MC1-1}"])
                        if nc_ % 2 == 0 and p >= 2:
                            t.wait_ge(SEM["sV"], V[f"hmul_{k}_{p-2}"])
                            t.wait_ge(SEM["sS"], V[f"sil_{k}_{p-2}"])
                        grp = PG1 if p % 2 == 0 else PG2
                        go = 512 * (nc_ % 2)
                        last = None
                        for kc in range(DB):
                            if nc_ == 0 and kc in (2, 4, 6):
                                t.wait_ge(SEM["sV"], V[f"q_xq_{k}_{kc//2}"])
                            last = t.matmul(grp[:, go:go + 512],
                                            s_wgt[nc_ % 2][:, kc, :],
                                            s_xq[:, kc, :], start=(kc == 0),
                                            stop=(kc == DB - 1))
                        for kc in range(DB):
                            last = t.matmul(grp[:, 1024 + go:1536 + go],
                                            s_wut[nc_ % 2][:, kc, :],
                                            s_xq[:, kc, :], start=(kc == 0),
                                            stop=(kc == DB - 1))
                        inc(last, "sPE")
                    V[f"gu_{k}_{nc_}"] = C["sPE"]

                    if nc_ % 2 == 1:
                        @blk.scalar
                        def _(s, k=k, nc_=nc_):
                            s.wait_ge(SEM["sPE"], V[f"gu_{k}_{nc_}"])
                            p = nc_ // 2
                            if p >= 1:
                                s.wait_ge(SEM["sV"], V[f"hmul_{k}_{p-1}"])
                            elif k > 0:
                                s.wait_ge(SEM["sV"], V[f"hmul_{k-1}_{NC2//2-1}"])
                            grp = PG1 if (nc_ // 2) % 2 == 0 else PG2
                            inc(s.activation(s_sg2[:], grp[:, 0:1024],
                                             Act.Silu), "sS")
                        V[f"sil_{k}_{nc_//2}"] = C["sS"]

                        @blk.vector
                        def _(v, k=k, nc_=nc_):
                            v.wait_ge(SEM["sS"], V[f"sil_{k}_{nc_//2}"])
                            grp = PG1 if (nc_ // 2) % 2 == 0 else PG2
                            inc(v.tensor_tensor(s_hq[:, nc_ - 1:nc_ + 1, :]
                                                .rearrange("p a b -> p (a b)"),
                                                s_sg2[:], grp[:, 1024:2048],
                                                Alu.mult), "sV")
                        V[f"hmul_{k}_{nc_//2}"] = C["sV"]

                        p = nc_ // 2
                        if p == 3:
                            hs = s_hq[:, 0:6, :].rearrange("p a b -> p (a b)")
                            hq_queue.extend(quant_steps(
                                hs, hs, 6 * STRIPE, f"hq_{k}_0",
                                wait_v=V[f"hmul_{k}_{2}"]))
                        elif p == 6:
                            hs = s_hq[:, 6:12, :].rearrange("p a b -> p (a b)")
                            hq_queue.extend(quant_steps(
                                hs, hs, 6 * STRIPE, f"hq_{k}_1",
                                wait_v=V[f"hmul_{k}_{5}"]))
                        elif p == 9:
                            hs = s_hq[:, 12:17, :].rearrange("p a b -> p (a b)")
                            hq_queue.extend(quant_steps(
                                hs, hs, 5 * STRIPE, f"hq_{k}_2",
                                wait_v=V[f"hmul_{k}_{8}"]))
                        if p >= 4:
                            for _ in range(5):
                                if hq_queue:
                                    hq_queue.pop(0)()

                while hq_queue:
                    hq_queue.pop(0)()
                V[f"hq01_{k}"] = V[f"q_hq_{k}_2"]
                hs = s_hq[:, 17:22, :].rearrange("p a b -> p (a b)")
                for st in quant_steps(hs, hs, 5 * STRIPE, f"hq_{k}_3",
                                      wait_v=V[f"hmul_{k}_{NC2//2-1}"]):
                    st()
                V[f"hq_{k}"] = V[f"q_hq_{k}_3"]

                if k + 1 < NSTR:
                    emit_pass_front(k + 1)

                for half in range(2):
                    for dcl in range(4):
                        dc = half * 4 + dcl

                        @blk.sync
                        def _(s, k=k, dc=dc):
                            if dc >= 2:
                                s.wait_ge(SEM["sPE"], V[f"dn_{k}_{dc-2}"])
                            off = dc * WSH + OFF_WD
                            src = bass.AP(ag_w_out, off,
                                          [[128, 128], [128 * 128, NC2], [1, 128]])
                            qn = "sWDa" if dc % 2 == 0 else "sWDb"
                            dmx(s, qn, s_wdt[dc % 2][:], src)
                        _qd = "sWDa" if dc % 2 == 0 else "sWDb"
                        V[f"q_wd_{k}_{dc}"] = (_qd, C[_qd])

                        @blk.tensor
                        def _(t, k=k, dc=dc, dcl=dcl, half=half):
                            qs, qv = V[f"q_wd_{k}_{dc}"]
                            t.wait_ge(SEM[qs], qv)
                            if dc == 0:
                                t.wait_ge(SEM["sV"], V[f"hq01_{k}"])
                            if half == 1 and dcl == 0:
                                t.wait_ge(SEM["sV"], V[f"x3ep_{k}_0"])
                            last = None
                            for kc in range(NC2):
                                if dc == 0 and kc == 17:
                                    t.wait_ge(SEM["sV"], V[f"hq_{k}"])
                                last = t.matmul(PACC[dcl][:, 0:512],
                                                s_wdt[dc % 2][:, kc, :],
                                                s_hq[:, kc, :], start=(kc == 0),
                                                stop=(kc == NC2 - 1))
                            inc(last, "sPE")
                        V[f"dn_{k}_{dc}"] = C["sPE"]

                        weave_scan(k + 2, 2 * (dc + 1))

                    @blk.vector
                    def _(v, k=k, half=half):
                        v.wait_ge(SEM["sPE"], V[f"dn_{k}_{half*4+3}"])
                        for dcl in range(4):
                            dc = half * 4 + dcl
                            v.scalar_tensor_tensor(s_x3[:, dc, :],
                                                   PACC[dcl][:, 0:512],
                                                   s_gb[:, dc:dc + 1],
                                                   s_x2[:, dc, :],
                                                   Alu.mult, Alu.add)
                            v.drain()
                        inc(v.drain(), "sV")
                    V[f"x3ep_{k}_{half}"] = C["sV"]

                @blk.vector
                def _(v, k=k):
                    v.wait_ge(SEM["sV"], V[f"x3ep_{k}_1"])
                    vnop(v)
                V[f"x3r_{k}"] = C["sV"]

        for r in range(4):
            for db in range(DB):
                @blk.tensor
                def _(t, k=k, r=r, db=db):
                    if r == 0 and db == 0:
                        t.wait_ge(SEM["sV"], V[f"x3r_{k}"])
                    oidx = r * DB + db
                    if oidx >= 2:
                        pr, pdb = divmod(oidx - 2, DB)
                        t.wait_ge(SEM["sV"], V[f"oev_{k}_{pr}_{pdb}"])
                    inc(t.transpose((PX if db % 2 == 0 else PY)[:, 0:128],
                                    s_x3[:, db, 128 * r:128 * r + 128], s_id32[:]),
                        "sPE")
                V[f"oT_{k}_{r}_{db}"] = C["sPE"]

                @blk.vector
                def _(v, k=k, r=r, db=db):
                    v.wait_ge(SEM["sPE"], V[f"oT_{k}_{r}_{db}"])
                    gidx = 4 * k + r
                    if db == 0 and gidx >= 2:
                        pk, pr = divmod(gidx - 2, 4)
                        qs, qv = V[f"q_out_{pk}_{pr}"]
                        v.wait_ge(SEM[qs], qv)
                    inc(v.tensor_copy(s_otm[r % 2][:, 128 * db:128 * db + 128],
                                      (PX if db % 2 == 0 else PY)[:, 0:128]), "sV")
                V[f"oev_{k}_{r}_{db}"] = C["sV"]

            weave_scan(k + 2, 16 + (r + 1))

            @blk.sync
            def _(sy, k=k, r=r):
                sy.wait_ge(SEM["sV"], V[f"oev_{k}_{r}_{DB-1}"])
                dmx(sy, "sOa" if r % 2 == 0 else "sOb",
                    d_out[512 * k + 128 * r:512 * k + 128 * r + 128, :],
                    s_otm[r % 2][:])
            _qo = "sOa" if r % 2 == 0 else "sOb"
            V[f"q_out_{k}_{r}"] = (_qo, C[_qo])

        @blk.vector
        def _(v, k=k):
            vnop(v)
        V[f"pass_done_{k}"] = C["sV"]

    @blk.sync
    def _(sy):
        for n in ["sQ", "sQS", "sU", "sY", "sOa", "sOb", "sQ2", "sQ2b", "sQ3", "sQW",
              "sQP", "sQXa", "sQXb",
                  "sW1a", "sW1b", "sW2a", "sW2b", "sW2c", "sW2d",
                  "sGUa", "sGUb", "sWDa", "sWDb", "sXC"]:
            if C[n]:
                sy.wait_ge(SEM[n], C[n])

    es.close()
    return nc


_NC_CACHE = None


def _get_nc():
    global _NC_CACHE
    if _NC_CACHE is None:
        _NC_CACHE = build_nc()
    return _NC_CACHE


def make_in_maps(inputs):
    f32 = np.float32
    x = np.asarray(inputs["L0_attn_x"], f32)
    A = np.asarray(inputs["L0_attn_A_ssm"], f32)
    Bm = np.asarray(inputs["L0_attn_B_ssm"], f32)
    Cm = np.asarray(inputs["L0_attn_C_ssm"], f32)
    Dv = np.asarray(inputs["L0_attn_D_ssm"], f32)
    s1 = np.asarray(inputs["L0_attn_scale1"], f32)
    s2 = np.asarray(inputs["L0_attn_scale2"], f32)
    w1 = np.asarray(inputs["L0_attn_w1"], f32)
    b1 = np.asarray(inputs["L0_attn_b1"], f32)
    w2 = np.asarray(inputs["L0_attn_w2"], f32)
    b2 = np.asarray(inputs["L0_attn_b2"], f32)
    wg = np.asarray(inputs["L0_ffn_w_gate"], f32)
    wu = np.asarray(inputs["L0_ffn_w_up"], f32)
    wd = np.asarray(inputs["L0_ffn_w_down"], f32)
    cc = np.asarray(inputs["L0_c"], f32)
    wmod = np.asarray(inputs["L0_wmod"], f32)

    dt = np.exp(np.linspace(np.log(0.001), np.log(0.1), N16)).astype(f32)
    dt_mat = np.ascontiguousarray(np.broadcast_to(dt[None, :], (128, N16)))

    def fmvec(v):
        return np.ascontiguousarray(v.reshape(DB, 128).T)

    wg_pad = np.zeros((384 * NCORE, D), f32)
    wg_pad[:DFF2] = wg
    wu_pad = np.zeros((384 * NCORE, D), f32)
    wu_pad[:DFF2] = wu

    ident = np.eye(128, dtype=f32)
    identb = np.eye(128, dtype=ml_dtypes.bfloat16)
    ones_col = np.ones((128, 1), f32)
    ones_row = np.ones((1, 128), f32)

    in_maps = []
    for c in range(NCORE):
        b = c // 2
        half = c % 2
        rows = np.concatenate([np.arange(1024 * kk + 512 * half,
                                         1024 * kk + 512 * half + 512)
                               for kk in range(NSTR)])
        sel = np.zeros((128, B), f32)
        sel[:, b] = 1.0
        offs = np.zeros((1, 8), np.int32)
        for kk in range(NSTR):
            offs[0, kk] = kk * (NCORE * DB * 128 * STRIPE) + c * (128 * STRIPE)
            offs[0, 4 + kk] = kk * (NCORE * 128 * B * 1024) + b * 1024 + half * 512
        in_maps.append({
            "x_own": np.ascontiguousarray(x[b][rows]),
            "A_sl": np.ascontiguousarray(A[128 * c:128 * c + 128]),
            "B_sl": np.ascontiguousarray(Bm[128 * c:128 * c + 128]),
            "C_sl": np.ascontiguousarray(Cm[128 * c:128 * c + 128]),
            "dt_mat": dt_mat,
            "D_fm": fmvec(Dv), "s1_fm": fmvec(s1), "s2_fm": fmvec(s2),
            "b2_fm": fmvec(b2),
            "b1_fm": np.ascontiguousarray(b1.reshape(MC1, 128).T),
            "cT_fm": np.ascontiguousarray(
                cc.T.reshape(DB, 128, B).transpose(1, 0, 2)),
            "wmod_sl": np.ascontiguousarray(wmod[:, 768 * c:768 * c + 768]),
            "sel": sel,
            "w1_sh": np.ascontiguousarray(w1[:, 512 * c:512 * c + 512]),
            "w2_sh": np.ascontiguousarray(w2[:, 128 * c:128 * c + 128]),
            "wg_sh": np.ascontiguousarray(wg_pad[384 * c:384 * c + 384]),
            "wu_sh": np.ascontiguousarray(wu_pad[384 * c:384 * c + 384]),
            "wd_sh": np.ascontiguousarray(wd[128 * c:128 * c + 128]),
            "ident": ident, "identb": identb,
            "ones_col": ones_col, "ones_row": ones_row,
            "offs": offs,
        })
    return in_maps


def assemble(results):
    out = np.zeros((B, T, D), np.float32)
    for c in range(NCORE):
        b = c // 2
        half = c % 2
        rows = np.concatenate([np.arange(1024 * kk + 512 * half,
                                         1024 * kk + 512 * half + 512)
                               for kk in range(NSTR)])
        out[b][rows] = np.asarray(results[c]["out_own"], np.float32)
    return out


def kernel(**inputs):
    nc = _get_nc()
    in_maps = make_in_maps(inputs)
    res = run_bass_kernel_spmd(nc, in_maps, core_ids=list(range(NCORE)))
    return assemble(res.results)


# version 17
# speedup vs baseline: 1.1395x; 1.0134x over previous
import numpy as np
import ml_dtypes
from contextlib import ExitStack

import concourse.bass as bass
import concourse.mybir as mybir
from concourse.bass_utils import run_bass_kernel_spmd

F32 = mybir.dt.float32
BF16 = mybir.dt.bfloat16
I32 = mybir.dt.int32
Alu = mybir.AluOpType
Act = mybir.ActivationFunctionType
AX = mybir.AxisListType.X

B, T, D = 4, 4096, 1024
N16 = 16
DFF1 = 4096
DFF2 = 2816
NCORE = 8
TOK = 2048
STRIPE = 512
NSTR = 4
DB = 8
MC1 = 32
NC2 = 22
LNEPS = 1e-5
RMSEPS = 1e-6

OFF_W1 = 0
SZ_W1 = 1024 * 512
OFF_W2 = OFF_W1 + SZ_W1
SZ_W2 = 4096 * 128
OFF_WG = OFF_W2 + SZ_W2
SZ_WG = 1024 * 384
OFF_WU = OFF_WG + SZ_WG
SZ_WU = 1024 * 384
OFF_WD = OFF_WU + SZ_WU
SZ_WD = 2816 * 128
WSH = OFF_WD + SZ_WD

STAGE = 3


def build_nc():
    nc = bass.Bass("TRN2", target_bir_lowering=False)

    d_x = nc.dram_tensor("x_own", [TOK, D], F32, kind="ExternalInput")
    d_A = nc.dram_tensor("A_sl", [128, N16], F32, kind="ExternalInput")
    d_B = nc.dram_tensor("B_sl", [128, N16], F32, kind="ExternalInput")
    d_C = nc.dram_tensor("C_sl", [128, N16], F32, kind="ExternalInput")
    d_dt = nc.dram_tensor("dt_mat", [128, N16], F32, kind="ExternalInput")
    d_Dfm = nc.dram_tensor("D_fm", [128, DB], F32, kind="ExternalInput")
    d_s1 = nc.dram_tensor("s1_fm", [128, DB], F32, kind="ExternalInput")
    d_s2 = nc.dram_tensor("s2_fm", [128, DB], F32, kind="ExternalInput")
    d_b2 = nc.dram_tensor("b2_fm", [128, DB], F32, kind="ExternalInput")
    d_b1 = nc.dram_tensor("b1_fm", [128, MC1], F32, kind="ExternalInput")
    d_cT = nc.dram_tensor("cT_fm", [128, DB, B], F32, kind="ExternalInput")
    d_wmod = nc.dram_tensor("wmod_sl", [D, 768], F32, kind="ExternalInput")
    d_sel = nc.dram_tensor("sel", [128, B], F32, kind="ExternalInput")
    d_w1s = nc.dram_tensor("w1_sh", [D, 512], F32, kind="ExternalInput")
    d_w2s = nc.dram_tensor("w2_sh", [DFF1, 128], F32, kind="ExternalInput")
    d_wgs = nc.dram_tensor("wg_sh", [384, D], F32, kind="ExternalInput")
    d_wus = nc.dram_tensor("wu_sh", [384, D], F32, kind="ExternalInput")
    d_wds = nc.dram_tensor("wd_sh", [128, DFF2], F32, kind="ExternalInput")
    d_id32 = nc.dram_tensor("ident", [128, 128], F32, kind="ExternalInput")
    d_idbf = nc.dram_tensor("identb", [128, 128], BF16, kind="ExternalInput")
    d_onc = nc.dram_tensor("ones_col", [128, 1], F32, kind="ExternalInput")
    d_onr = nc.dram_tensor("ones_row", [1, 128], F32, kind="ExternalInput")
    d_offs = nc.dram_tensor("offs", [1, 8], I32, kind="ExternalInput")
    d_out = nc.dram_tensor("out_own", [TOK, D], F32, kind="ExternalOutput")

    h_xfm = nc.dram_tensor("h_xfm", [DB, 128, TOK], F32, kind="Internal")
    h_p = nc.dram_tensor("h_p", [DB, 128, TOK], BF16, kind="Internal")
    ag_u_in = nc.dram_tensor("ag_u_in", [NSTR, DB, 128, STRIPE], BF16, kind="Internal")
    ag_u_out = nc.dram_tensor("ag_u_out", [NSTR, NCORE, DB, 128, STRIPE], BF16, kind="Internal", addr_space="Shared")
    ag_y_in = nc.dram_tensor("ag_y_in", [NSTR, 128, B, 1024], BF16, kind="Internal")
    ag_y_out = nc.dram_tensor("ag_y_out", [NSTR, NCORE, 128, B, 1024], BF16, kind="Internal", addr_space="Shared")
    ag_w_in = nc.dram_tensor("ag_w_in", [WSH], BF16, kind="Internal")
    ag_w_out = nc.dram_tensor("ag_w_out", [NCORE, WSH], BF16, kind="Internal", addr_space="Shared")
    ag_m_in = nc.dram_tensor("ag_m_in", [6, 128, B], F32, kind="Internal")
    ag_m_out = nc.dram_tensor("ag_m_out", [NCORE, 6, 128, B], F32, kind="Internal", addr_space="Shared")

    RG = [list(range(NCORE))]
    es = ExitStack()

    def sbuf(name, shape, dtype):
        return es.enter_context(nc.sbuf_tensor(name, shape, dtype))

    C, SEM = {}, {}
    for n in ["sQ", "sQS", "sU", "sY", "sOa", "sOb", "sQ2", "sQ2b", "sQ3", "sQW",
              "sQP", "sQXa", "sQXb",
              "sW1a", "sW1b", "sW2a", "sW2b", "sW2c", "sW2d",
              "sGUa", "sGUb", "sWDa", "sWDb", "sXC", "sQ4",
              "sPE", "sV", "sS", "sAGu", "sAGy", "sAGw", "sAGm"]:
        SEM[n] = es.enter_context(nc.semaphore(n))
        C[n] = 0

    def inc(instr, name, k=1):
        instr.then_inc(SEM[name], k)
        C[name] += k
        return C[name]

    def dmx(eng, name, out_ap, in_ap):
        ins = eng.dma_start(out=out_ap, in_=in_ap)
        ins.then_inc(SEM[name], 16)
        C[name] += 16
        return C[name]

    V = {}

    s_id32 = sbuf("s_id32", [128, 128], F32)
    s_idbf = sbuf("s_idbf", [128, 128], BF16)
    s_onc = sbuf("s_onc", [128, 1], F32)
    s_onr = sbuf("s_onr", [1, 128], F32)
    s_oncb = sbuf("s_oncb", [128, 1], BF16)
    s_A = sbuf("s_A", [128, N16], F32)
    s_Bm = sbuf("s_Bm", [128, N16], F32)
    s_Cm = sbuf("s_Cm", [128, N16], F32)
    s_dt = sbuf("s_dt", [128, N16], F32)
    s_dA = sbuf("s_dA", [128, N16], F32)
    s_CdB = sbuf("s_CdB", [128, N16], F32)
    s_t16a = sbuf("s_t16a", [128, N16], F32)
    s_t16b = sbuf("s_t16b", [128, N16], F32)
    s_Dfm = sbuf("s_Dfm", [128, DB], F32)
    s_s1 = sbuf("s_s1", [128, DB], F32)
    s_s2 = sbuf("s_s2", [128, DB], F32)
    s_b2 = sbuf("s_b2", [128, DB], F32)
    s_b1 = sbuf("s_b1", [128, MC1], F32)
    s_sel = sbuf("s_sel", [128, B], F32)
    s_cT = sbuf("s_cT", [128, DB, B], F32)
    s_cTb = sbuf("s_cTb", [128, DB, B], F32)
    s_modsl = sbuf("s_modsl", [128, 6, B], F32)
    s_modal = sbuf("s_modal", [128, 48, B], F32)
    s_mod = sbuf("s_mod", [128, 48], F32)
    s_ca1 = sbuf("s_ca1", [128, DB], F32)
    s_sa = sbuf("s_sa", [128, DB], F32)
    s_ga = sbuf("s_ga", [128, DB], F32)
    s_cb1 = sbuf("s_cb1", [128, DB], F32)
    s_sbv = sbuf("s_sbv", [128, DB], F32)
    s_gb = sbuf("s_gb", [128, DB], F32)
    s_gtmp = sbuf("s_gtmp", [128, DB], F32)
    s_gtmp2 = sbuf("s_gtmp2", [128, DB], F32)
    s_offs = sbuf("s_offs", [1, 8], I32)
    s_state = sbuf("s_state", [128, B * N16], F32)

    a1 = sbuf("a1", [128, 32, 512], BF16)
    a2 = sbuf("a2", [128, 16, 512], BF16)
    F1 = sbuf("F1", [128, 4096], F32)
    F2 = sbuf("F2", [128, 4096], F32)
    F3 = sbuf("F3", [128, 3072], F32)


    sh_w = F1[:]
    sh_b = a1[:, 0:8, :].rearrange("p a b -> p (a b)")
    sh_f1 = F2[:, 0:3072]
    sh_f2 = F3[:, 0:3072]
    sh_i1 = a1[:, 16:22, :].rearrange("p a b -> p (a b)")
    sh_t = a1[:, 24:30, :].rearrange("p a b -> p (a b)").rearrange(
        "p (i m) -> p i m", m=128)

    s_xtm = [F3[:, 0:1024], F3[:, 1024:2048]]
    s_sq = sbuf("s_sq", [128, DB, STRIPE], BF16)
    s_t1 = s_sq
    s_xfm = sbuf("s_xfm", [128, DB, STRIPE], F32)
    sh_m = F3[:, 2560:2816]
    sh_i = F3[:, 2816:3072]
    s_h1 = a2[:, 0:8, :]
    s_u = a2[:, 8:16, :]
    s_pch = a1[:, 8:16, :]
    s_row = sbuf("s_row", [1, STRIPE], F32)
    s_row2 = sbuf("s_row2", [1, STRIPE], F32)

    s_ug2 = sbuf("s_ug2", [128, 2, DB, STRIPE], BF16)
    s_ug = [s_ug2[:, 0], s_ug2[:, 1]]
    s_z = [F2[:, 2048:3072], F2[:, 3072:4096]]
    s_yac2 = sbuf("s_yac2", [128, 2, 1024], F32)
    s_yst2 = sbuf("s_yst2", [128, 4096], BF16)

    s_yin = F3[:, 0:2048].bitcast(BF16).rearrange("p (a b) -> p a b", b=STRIPE)
    s_pin = a1[:, 22:30, :]
    s_xp = sbuf("s_xp", [128, DB, STRIPE], BF16)
    s_h2 = a2[:, 0:8, :]
    s_g = a1[:]
    s_xch = s_xfm
    s_x2 = a2[:, 8:16, :]
    s_h3 = a1[:, 22:30, :]
    s_xq = a1[:, 22:30, :]
    s_hq = a1[:, 0:22, :]
    s_x3 = F1[:].rearrange("p (a b) -> p a b", b=STRIPE)
    s_otm = [F2[:, 0:1024], F2[:, 1024:2048]]
    s_sg2 = sbuf("s_sg2", [128, 2 * STRIPE], BF16)

    s_w1t = [sbuf(f"s_w1t{i}", [128, DB, 128], BF16) for i in range(2)]
    s_w2t = [sbuf(f"s_w2t{i}", [128, 4, 128], BF16) for i in range(4)]
    s_wgt = [sbuf(f"s_wgt{i}", [128, DB, 128], BF16) for i in range(2)]
    s_wut = [sbuf(f"s_wut{i}", [128, DB, 128], BF16) for i in range(2)]
    s_wdt = [sbuf(f"s_wdt{i}", [128, NC2, 128], BF16) for i in range(2)]

    QW = 3072
    q_vt = sbuf("q_vt", [128, QW], BF16)
    q_sq = sbuf("q_sq", [128, QW], BF16)
    q_q1 = s_xp[:, 0:8, :].rearrange("p a b -> p (a b)")
    q_h5 = s_sq[:, 0:8, :].rearrange("p a b -> p (a b)")
    q_u = a2[:, 0:8, :].rearrange("p a b -> p (a b)")
    q_m = F3[:, 2048:2048 + QW // 16]
    q_in = F3[:, 2304:2304 + QW // 16]

    PG1 = es.enter_context(nc.psum_tensor("pg1", [128, 2048], F32))
    PG2 = es.enter_context(nc.psum_tensor("pg2", [128, 2048], F32))
    PA = PG1[:, 0:512]
    PX = PG1[:, 512:1024]
    PBK = PG1[:, 1024:1536]
    PY = PG1[:, 1536:2048]
    PACC = [PG2[:, 512 * i:512 * i + 512] for i in range(4)]

    blk = es.enter_context(nc.Block())

    r_u = [es.enter_context(nc.sync.register(f"r_u{k}")) for k in range(NSTR)]
    r_y = [es.enter_context(nc.sync.register(f"r_y{k}")) for k in range(NSTR)]

    def vnop(v, name="sV"):
        return inc(v.drain(), name)

    @blk.sync
    def _(sy):
        for d, s in [(d_id32, s_id32), (d_idbf, s_idbf), (d_onc, s_onc),
                     (d_onr, s_onr), (d_A, s_A), (d_B, s_Bm), (d_C, s_Cm),
                     (d_dt, s_dt), (d_Dfm, s_Dfm), (d_s1, s_s1), (d_s2, s_s2),
                     (d_b2, s_b2), (d_b1, s_b1), (d_sel, s_sel), (d_cT, s_cT),
                     (d_offs, s_offs)]:
            dmx(sy, "sQ", s[:], d[:])
        sy.wait_ge(SEM["sQ"], C["sQ"])
        for k in range(NSTR):
            sy.reg_load(r_u[k], s_offs[0:1, k:k + 1])
            sy.reg_load(r_y[k], s_offs[0:1, 4 + k:5 + k])
    V["q_small"] = C["sQ"]

    @blk.scalar
    def _(s):
        s.wait_ge(SEM["sQ"], V["q_small"])
        inc(s.activation(s_t16a[:], s_A[:], Act.Exp), "sS")
    V["s_expA"] = C["sS"]

    @blk.vector
    def _(v):
        v.wait_ge(SEM["sS"], V["s_expA"])
        v.tensor_tensor(s_t16b[:], s_t16a[:], s_dt[:], Alu.mult)
        inc(v.reciprocal(s_dA[:], s_t16a[:]), "sV")
    V["v_rec"] = C["sV"]

    @blk.scalar
    def _(s):
        s.wait_ge(SEM["sV"], V["v_rec"])
        inc(s.activation(s_t16a[:], s_t16b[:], Act.Exp, scale=-1.0), "sS")
    V["s_dA"] = C["sS"]

    @blk.vector
    def _(v):
        v.wait_ge(SEM["sS"], V["s_dA"])
        v.tensor_scalar(s_t16b[:], s_t16a[:], -1.0, 1.0, Alu.mult, Alu.add)
        v.drain()
        v.tensor_tensor(s_t16b[:], s_t16b[:], s_Bm[:], Alu.mult)
        v.drain()
        v.tensor_tensor(s_t16b[:], s_t16b[:], s_dA[:], Alu.mult)
        v.drain()
        v.tensor_tensor(s_CdB[:], s_t16b[:], s_Cm[:], Alu.mult)
        v.tensor_copy(s_dA[:], s_t16a[:])
        v.tensor_copy(s_oncb[:], s_onc[:])
        v.memset(s_state[:], 0.0)
        vnop(v)
    V["v_tabs"] = C["sV"]

    for h in range(2):
        @blk.gpsimd
        def _(g, h=h):
            if h == 1:
                g.wait_ge(SEM["sPE"], V["pe_mod2"])
            dmx(g, "sQ3", sh_w[:, 0:3072].rearrange("p (db m) -> p db m", db=DB), d_wmod[:, 384 * h:384 * h + 384].rearrange(
                    "(db p) m -> p db m", p=128))
        V[f"q3_wmod{h}"] = C["sQ3"]

        @blk.scalar
        def _(s, h=h):
            if h == 0:
                s.wait_ge(SEM["sQ"], V["q_small"])
                inc(s.activation(s_cTb[:], s_cT[:], Act.Silu), "sS")
        if h == 0:
            V["s_siluc"] = C["sS"]

        for jl in range(3):
            j = 3 * h + jl

            @blk.tensor
            def _(t, h=h, jl=jl, j=j):
                if j == 0:
                    t.wait_ge(SEM["sS"], V["s_siluc"])
                if jl == 0:
                    t.wait_ge(SEM["sQ3"], V[f"q3_wmod{h}"])
                if j > 0:
                    t.wait_ge(SEM["sV"], V[f"v_modev{j-1}"])
                wv = sh_w[:, 0:3072].rearrange("p (db j m) -> p db j m",
                                               db=DB, j=3)
                last = None
                for kc in range(DB):
                    last = t.matmul(PX[:, 0:B], wv[:, kc, jl, :],
                                    s_cTb[:, kc, :],
                                    start=(kc == 0), stop=(kc == DB - 1))
                inc(last, "sPE")
            V[f"pe_mod{j}"] = C["sPE"]

            @blk.vector
            def _(v, j=j):
                v.wait_ge(SEM["sPE"], V[f"pe_mod{j}"])
                inc(v.tensor_copy(s_modsl[:, j, :], PX[:, 0:B]), "sV")
            V[f"v_modev{j}"] = C["sV"]

    @blk.sync
    def _(sy):
        sy.wait_ge(SEM["sV"], V["v_modev5"])
        dmx(sy, "sQS", ag_m_in[:].transpose([1, 0, 2]), s_modsl[:])
    V["q_modin"] = C["sQS"]

    @blk.gpsimd
    def _(g):
        g.wait_ge(SEM["sQS"], V["q_modin"])
        inc(g.collective_compute("AllGather", Alu.bypass, replica_groups=RG,
                                 ins=[ag_m_in[:]], outs=[ag_m_out[:]]), "sAGm")
    V["g_mod"] = C["sAGm"]

    @blk.sync
    def _(sy):
        sy.wait_ge(SEM["sAGm"], V["g_mod"])
        dmx(sy, "sQ", s_modal[:], ag_m_out[:].rearrange("r s p b -> p (r s) b"))
    V["q_modal"] = C["sQ"]

    @blk.vector
    def _(v):
        v.wait_ge(SEM["sQ"], V["q_modal"])
        v.scalar_tensor_tensor(s_mod[:], s_modal[:, :, 0], s_sel[:, 0:1],
                               s_modal[:, :, 0], Alu.mult, Alu.bypass)
        for b in range(1, B):
            v.drain()
            v.scalar_tensor_tensor(s_mod[:], s_modal[:, :, b], s_sel[:, b:b + 1],
                                   s_mod[:], Alu.mult, Alu.add)
        v.drain()
        mv = s_mod[:].rearrange("p (v db) -> p v db", v=6)
        v.tensor_scalar(s_ca1[:], mv[:, 1, :], 1.0, None, Alu.add)
        v.tensor_copy(s_sa[:], mv[:, 0, :])
        v.tensor_scalar(s_cb1[:], mv[:, 4, :], 1.0, None, Alu.add)
        v.tensor_copy(s_sbv[:], mv[:, 3, :])
        for src_i, dst in [(2, s_ga), (5, s_gb)]:
            v.tensor_scalar(s_gtmp[:], mv[:, src_i, :], 0.0, 2.0,
                            Alu.is_ge, Alu.mult)
            v.drain()
            v.tensor_scalar(s_gtmp[:], s_gtmp[:], -1.0, None, Alu.add)
            v.tensor_scalar(s_gtmp2[:], mv[:, src_i, :], -1.0, None, Alu.mult)
            v.drain()
            v.tensor_tensor(s_gtmp2[:], s_gtmp2[:], mv[:, src_i, :], Alu.max)
            v.drain()
            v.tensor_scalar(s_gtmp2[:], s_gtmp2[:], 0.25, None, Alu.max)
            v.drain()
            v.tensor_tensor(dst[:], s_gtmp[:], s_gtmp2[:], Alu.mult)
            v.drain()
        vnop(v)
    V["v_mod"] = C["sV"]

    def emit_stats_rows(src, tag, want_mean, wait_s_v=None, wait_pe_v=None,
                        src_f32=False):
        @blk.scalar
        def _(s):
            if wait_s_v is not None:
                s.wait_ge(SEM["sV"], wait_s_v)
            inc(s.activation(s_sq[:], src[:], Act.Square), "sS")
        V[f"sq_{tag}"] = C["sS"]

        @blk.tensor
        def _(t):
            t.wait_ge(SEM["sS"], V[f"sq_{tag}"])
            if wait_pe_v is not None:
                t.wait_ge(SEM["sV"], wait_pe_v)
            last = None
            for db in range(DB):
                last = t.matmul(PY[0:1, 0:512], s_oncb[:], s_sq[:, db, :],
                                start=(db == 0), stop=(db == DB - 1))
            if want_mean:
                ones = s_onc[:] if src_f32 else s_oncb[:]
                for db in range(DB):
                    last = t.matmul(PX[0:1, 0:512], ones, src[:, db, :],
                                    start=(db == 0), stop=(db == DB - 1))
            inc(last, "sPE")
        V[f"st_{tag}"] = C["sPE"]

        @blk.vector
        def _(v):
            v.wait_ge(SEM["sPE"], V[f"st_{tag}"])
            if want_mean:
                v.tensor_scalar(s_row2[:], PX[0:1, 0:512], 1.0 / D, None, Alu.mult)
                v.drain()
                v.tensor_tensor(s_row[:], s_row2[:], s_row2[:], Alu.mult)
                v.drain()
                v.scalar_tensor_tensor(s_row[:], PY[0:1, 0:512], 1.0 / D,
                                       s_row[:], Alu.mult, Alu.subtract)
                v.drain()
                v.tensor_scalar(s_row[:], s_row[:], LNEPS, None, Alu.add)
            else:
                v.tensor_scalar(s_row[:], PY[0:1, 0:512], 1.0 / D, RMSEPS,
                                Alu.mult, Alu.add)
            vnop(v)
        V[f"stv_{tag}"] = C["sV"]

        @blk.scalar
        def _(s):
            s.wait_ge(SEM["sV"], V[f"stv_{tag}"])
            s.activation(s_row[:], s_row[:], Act.Ln)
            s.drain()
            inc(s.activation(s_row[:], s_row[:], Act.Exp, scale=-0.5), "sS")
        V[f"str_{tag}"] = C["sS"]

        @blk.vector
        def _(v):
            v.wait_ge(SEM["sS"], V[f"str_{tag}"])
            if want_mean:
                v.tensor_tensor(s_row2[:], s_row2[:], s_row[:], Alu.mult)
            vnop(v)
        V[f"strows_{tag}"] = C["sV"]

        @blk.tensor
        def _(t):
            t.wait_ge(SEM["sV"], V[f"strows_{tag}"])
            i1 = t.matmul(PX[:, 0:512], s_onr[:], s_row[:], start=True, stop=True)
            if want_mean:
                i1 = t.matmul(PY[:, 0:512], s_onr[:], s_row2[:],
                              start=True, stop=True)
            inc(i1, "sPE")
        V[f"stexp_{tag}"] = C["sPE"]

    def quant_steps(src_slc, dst_slc, F, tag, wait_v=None):
        nbm = F // 16
        mb = q_m[:, 0:nbm].rearrange("p (a b) -> p a b", b=2) \
            .unsqueeze(3).broadcast_to([128, F // 32, 2, 16])
        ib = q_in[:, 0:nbm].rearrange("p (a b) -> p a b", b=2) \
            .unsqueeze(3).broadcast_to([128, F // 32, 2, 16])

        def g4(ap):
            return ap.rearrange("p (a b k) -> p a b k", b=2, k=16)

        def mk(fn, first=False, last=False):
            def emit(fn=fn, first=first, last=last):
                @blk.vector
                def _(v):
                    if not first:
                        v.drain()
                    r = fn(v)
                    if last:
                        inc(r, "sV")
                        V[f"q_{tag}"] = C["sV"]
            return emit

        def s1(v):
            if wait_v is not None:
                v.wait_ge(SEM["sV"], wait_v)
            v.transpose(q_vt[:, 0:F], src_slc)

        def s2(v):
            v.tensor_reduce(q_m[:, 0:nbm], g4(q_vt[:, 0:F]), AX, Alu.max,
                            apply_absolute_value=True)

        def s3(v):
            v.tensor_scalar(q_m[:, 0:nbm], q_m[:, 0:nbm], 1.0 / 6.0, 1e-12,
                            Alu.mult, Alu.max)

        def s4(v):
            v.reciprocal(q_in[:, 0:nbm], q_m[:, 0:nbm])

        def s5(v):
            v.tensor_tensor(g4(q_vt[:, 0:F]), g4(q_vt[:, 0:F]), ib, Alu.mult)

        def s6(v):
            v.tensor_tensor(q_sq[:, 0:F], q_vt[:, 0:F], q_vt[:, 0:F], Alu.mult)
            v.tensor_scalar(q_u[:, 0:F], q_vt[:, 0:F], 0.0, 2.0,
                            Alu.is_gt, Alu.mult)

        def s7(v):
            v.tensor_scalar(q_q1[:, 0:F], q_sq[:, 0:F], 0.0625, 0.5,
                            Alu.is_gt, Alu.mult)
            v.tensor_scalar(q_h5[:, 0:F], q_sq[:, 0:F], 0.5625, 0.5,
                            Alu.is_gt, Alu.mult)
            v.tensor_scalar(q_vt[:, 0:F], q_sq[:, 0:F], 1.5625, 0.5,
                            Alu.is_gt, Alu.mult)
            v.tensor_scalar(q_u[:, 0:F], q_u[:, 0:F], -1.0, None, Alu.add)

        def s8(v):
            v.tensor_tensor(q_q1[:, 0:F], q_q1[:, 0:F], q_h5[:, 0:F], Alu.add)

        def s9(v):
            v.tensor_scalar(q_h5[:, 0:F], q_sq[:, 0:F], 3.0625, 0.5,
                            Alu.is_gt, Alu.mult)

        def s10(v):
            v.tensor_tensor(q_q1[:, 0:F], q_q1[:, 0:F], q_vt[:, 0:F], Alu.add)

        def s11(v):
            v.tensor_tensor(q_q1[:, 0:F], q_q1[:, 0:F], q_h5[:, 0:F], Alu.add)
            v.tensor_scalar(q_vt[:, 0:F], q_sq[:, 0:F], 6.25, None, Alu.is_gt)

        def s12(v):
            v.tensor_tensor(q_q1[:, 0:F], q_q1[:, 0:F], q_vt[:, 0:F], Alu.add)
            v.tensor_scalar(q_h5[:, 0:F], q_sq[:, 0:F], 12.25, None, Alu.is_gt)

        def s13(v):
            v.tensor_tensor(q_q1[:, 0:F], q_q1[:, 0:F], q_h5[:, 0:F], Alu.add)
            v.tensor_scalar(q_vt[:, 0:F], q_sq[:, 0:F], 25.0, 2.0,
                            Alu.is_gt, Alu.mult)

        def s14(v):
            v.tensor_tensor(q_q1[:, 0:F], q_q1[:, 0:F], q_vt[:, 0:F], Alu.add)

        def s15(v):
            v.tensor_tensor(q_q1[:, 0:F], q_q1[:, 0:F], q_u[:, 0:F], Alu.mult)

        def s16(v):
            v.tensor_tensor(g4(q_q1[:, 0:F]), g4(q_q1[:, 0:F]), mb, Alu.mult)

        def s17(v):
            return v.transpose(dst_slc, q_q1[:, 0:F])

        fns = [s1, s2, s3, s4, s5, s6, s7, s8, s9, s10, s11, s12, s13,
               s14, s15, s16, s17]
        return [mk(f, first=(i == 0), last=(i == len(fns) - 1))
                for i, f in enumerate(fns)]

    def emit_quant(src_slc, dst_slc, F, tag, wait_v=None):
        for st in quant_steps(src_slc, dst_slc, F, tag, wait_v):
            st()

    def emit_ug_load(k):
        @blk.sync
        def _(sy, k=k):
            sy.wait_ge(SEM["sAGu"], V[f"agu_{k}"])
            if k >= 2:
                sy.wait_ge(SEM["sV"], V[f"scn_{k-2}_{B*N16-1}"])
            src = bass.AP(ag_u_out, r_u[k],
                          [[STRIPE, 128], [DB * 128 * STRIPE, NCORE], [1, STRIPE]])
            qn = "sQ2" if k % 2 == 0 else "sQ2b"
            dmx(sy, qn, s_ug[k % 2][:], src)
        V[f"q_ug_{k}"] = (("sQ2" if k % 2 == 0 else "sQ2b"),
                          C["sQ2" if k % 2 == 0 else "sQ2b"])

    def emit_scan_iter(k, i):
        b, n = divmod(i, N16)
        par = i % 2

        @blk.vector
        def _(v, k=k, b=b, n=n, i=i, par=par):
            if i == 0:
                qs, qv = V[f"q_ug_{k}"]
                v.wait_ge(SEM[qs], qv)
            st = b * N16 + n
            init = 0.0 if k == 0 else s_state[:, st:st + 1]
            uview = s_ug[k % 2][:].rearrange("p s t -> p (s t)")
            v.tensor_tensor_scan(s_z[par][:],
                                 s_dA[:, n:n + 1].broadcast_to([128, 1024]),
                                 uview[:, 1024 * b:1024 * b + 1024],
                                 init, Alu.mult, Alu.add)
            v.drain()
            v.tensor_copy(s_state[:, st:st + 1], s_z[par][:, 1023:1024])
            if n == 0:
                last = v.tensor_scalar(s_yac2[:, b % 2, :], s_z[par][:],
                                       s_CdB[:, n:n + 1], None, Alu.mult)
            else:
                last = v.scalar_tensor_tensor(s_yac2[:, b % 2, :], s_z[par][:],
                                              s_CdB[:, n:n + 1],
                                              s_yac2[:, b % 2, :],
                                              Alu.mult, Alu.add)
            inc(last, "sV")
        V[f"scn_{k}_{i}"] = C["sV"]

    def emit_scan_flush(k, half):
        @blk.vector
        def _(v, k=k, half=half):
            if half == 0 and k > 0:
                v.wait_ge(SEM["sY"], V[f"q_yst_{k-1}"])
            v.drain()
            v.tensor_copy(s_yst2[:, 2048 * half:2048 * half + 2048],
                          s_yac2[:].rearrange("p a b -> p (a b)"))
            vnop(v)
        V[f"yfl_{k}_{half}"] = C["sV"]

    def emit_scan_store(k):
        @blk.gpsimd
        def _(g, k=k):
            g.wait_ge(SEM["sV"], V[f"yfl_{k}_1"])
            dmx(g, "sY", ag_y_in[k], s_yst2[:])
        V[f"q_yst_{k}"] = C["sY"]

        @blk.gpsimd
        def _(g, k=k):
            g.wait_ge(SEM["sY"], V[f"q_yst_{k}"])
            inc(g.collective_compute("AllGather", Alu.bypass, replica_groups=RG,
                                     ins=[ag_y_in[k]], outs=[ag_y_out[k]]), "sAGy")
        V[f"agy_{k}"] = C["sAGy"]

    scan_cur = {k: 0 for k in range(NSTR)}

    def weave_scan(k, upto):
        if k >= NSTR:
            return
        while scan_cur[k] < min(upto, B * N16):
            i = scan_cur[k]
            emit_scan_iter(k, i)
            scan_cur[k] += 1
            if i == 31:
                emit_scan_flush(k, 0)
            elif i == 63:
                emit_scan_flush(k, 1)
                emit_scan_store(k)

    def emit_p1_stripe(k):
        for r in range(4):
            idx = 4 * k + r

            @blk.sync
            def _(sy, k=k, r=r, idx=idx):
                if idx == 0:
                    pass
                if idx >= 2:
                    pk, pr = divmod(idx - 2, 4)
                    sy.wait_ge(SEM["sPE"], V[f"xT_{pk}_{pr}"])
                qn = "sQXa" if idx % 2 == 0 else "sQXb"
                dmx(sy, qn, s_xtm[idx % 2][:],
                    d_x[512 * k + 128 * r:512 * k + 128 * r + 128, :])
            V[f"q_x_{k}_{r}"] = ("sQXa" if (4 * k + r) % 2 == 0 else "sQXb",
                                 C["sQXa" if (4 * k + r) % 2 == 0 else "sQXb"])

            for db in range(DB):
                @blk.tensor
                def _(t, k=k, r=r, db=db, idx=idx):
                    if db == 0:
                        qs, qv = V[f"q_x_{k}_{r}"]
                        t.wait_ge(SEM[qs], qv)
                        if idx == 0:
                            t.wait_ge(SEM["sV"], V["v_mod"])
                        elif r == 0 and k > 0:
                            t.wait_ge(SEM["sV"], V[f"up_{k-1}"])
                            if k == 1:
                                t.wait_ge(SEM["sV"], V["shTev_wd_21"])
                    if db >= 2:
                        t.wait_ge(SEM["sV"], V[f"xev_{k}_{r}_{db-2}"])
                    elif idx > 0:
                        pk, pr = divmod(idx - 1, 4)
                        t.wait_ge(SEM["sV"], V[f"xev_{pk}_{pr}_{DB-2+db}"])
                    inc(t.transpose((PX if db % 2 == 0 else PY)[:, 0:128],
                                    s_xtm[idx % 2][:, 128 * db:128 * db + 128],
                                    s_id32[:]), "sPE")
                V[f"xTp_{k}_{r}_{db}"] = C["sPE"]

                @blk.vector
                def _(v, k=k, r=r, db=db):
                    v.wait_ge(SEM["sPE"], V[f"xTp_{k}_{r}_{db}"])
                    if k > 0 and r == 0 and db == 0:
                        v.wait_ge(SEM["sQP"], V[f"q_xsp_{k-1}"])
                    ins = v.tensor_copy(s_xfm[:, db, 128 * r:128 * r + 128],
                                        (PX if db % 2 == 0 else PY)[:, 0:128])
                    inc(ins, "sV")
                V[f"xev_{k}_{r}_{db}"] = C["sV"]
            V[f"xT_{k}_{r}"] = C["sPE"]
        V[f"xev_last_{k}"] = C["sV"]

        @blk.sync
        def _(sy, k=k):
            sy.wait_ge(SEM["sV"], V[f"xev_last_{k}"])
            dmx(sy, "sQP", h_xfm[:, :, 512 * k:512 * k + 512].transpose([1, 0, 2]), s_xfm[:])
        V[f"q_xsp_{k}"] = C["sQP"]

        emit_stats_rows(s_xfm, f"ln1_{k}", True,
                        wait_s_v=V[f"xev_last_{k}"],
                        wait_pe_v=(V[f"up_{k-1}"] if k > 0 else None),
                        src_f32=True)

        @blk.vector
        def _(v, k=k):
            v.wait_ge(SEM["sPE"], V[f"stexp_ln1_{k}"])
            pxb = PX[:, 0:512].unsqueeze(1).broadcast_to([128, DB, 512])
            pyb = PY[:, 0:512].unsqueeze(1).broadcast_to([128, DB, 512])
            v.tensor_tensor(s_t1[:], s_xfm[:], pxb, Alu.mult)
            v.drain()
            v.tensor_tensor(s_t1[:], s_t1[:], pyb, Alu.subtract)
            v.drain()
            for db in range(DB):
                v.tensor_scalar(s_h1[:, db, :], s_t1[:, db, :],
                                s_ca1[:, db:db + 1], s_sa[:, db:db + 1],
                                Alu.mult, Alu.add)
            vnop(v)
        V[f"h1_{k}"] = C["sV"]

        emit_stats_rows(s_h1, f"rms1_{k}", False,
                        wait_s_v=V[f"h1_{k}"], wait_pe_v=V[f"h1_{k}"])

        @blk.vector
        def _(v, k=k):
            v.wait_ge(SEM["sPE"], V[f"stexp_rms1_{k}"])
            if k > 0:
                v.wait_ge(SEM["sU"], V[f"q_ust_{k-1}"])
            pxb = PX[:, 0:512].unsqueeze(1).broadcast_to([128, DB, 512])
            v.tensor_tensor(s_t1[:], s_h1[:], pxb, Alu.mult)
            v.drain()
            for db in range(DB):
                v.tensor_scalar(s_u[:, db, :], s_t1[:, db, :],
                                s_s1[:, db:db + 1], None, Alu.mult)
            v.drain()
            for db in range(DB):
                v.scalar_tensor_tensor(s_pch[:, db, :], s_u[:, db, :],
                                       s_Dfm[:, db:db + 1], s_h1[:, db, :],
                                       Alu.mult, Alu.add)
            vnop(v)
        V[f"up_{k}"] = C["sV"]

        @blk.sync
        def _(sy, k=k):
            sy.wait_ge(SEM["sV"], V[f"up_{k}"])
            dmx(sy, "sU", ag_u_in[k].transpose([1, 0, 2]), s_u[:])
            dmx(sy, "sU", h_p[:, :, 512 * k:512 * k + 512].transpose([1, 0, 2]), s_pch[:])
        V[f"q_ust_{k}"] = C["sU"]

        @blk.gpsimd
        def _(g, k=k):
            g.wait_ge(SEM["sU"], V[f"q_ust_{k}"])
            inc(g.collective_compute("AllGather", Alu.bypass, replica_groups=RG,
                                     ins=[ag_u_in[k]], outs=[ag_u_out[k]]), "sAGu")
        V[f"agu_{k}"] = C["sAGu"]

        if k == 1:
            emit_ug_load(0)
            weave_scan(0, 24)
        elif k == 2:
            emit_ug_load(1)
            weave_scan(0, 48)

    emit_p1_stripe(0)

    def emit_shard_quant(n, tag, wait_pe=None):
        nb16 = n // 16
        mv = sh_m[:, 0:nb16]
        iv = sh_i[:, 0:nb16]
        mb = mv.unsqueeze(2).broadcast_to([128, nb16, 16])
        ib = iv.unsqueeze(2).broadcast_to([128, nb16, 16])

        def g3(ap):
            return ap.rearrange("p (g k) -> p g k", k=16)

        sqv = sh_w[:, 0:n]
        qb = sh_b[:, 0:n]
        tmp = sh_i1[:, 0:n]
        xn = sh_f1[:, 0:n]

        @blk.vector
        def _(v):
            v.wait_ge(SEM["sQ3"], V[f"q3_{tag}"])
            bsem, bval = V[f"shbfree_{tag}"]
            v.wait_ge(SEM[bsem], bval)
            if wait_pe is not None:
                v.wait_ge(SEM["sPE"], wait_pe)
            v.tensor_reduce(mv, g3(sh_w[:, 0:n]), AX, Alu.max,
                            apply_absolute_value=True)
            v.drain()
            v.tensor_scalar(mv, mv, 1.0 / 6.0, 1e-12, Alu.mult, Alu.max)
            v.drain()
            v.reciprocal(iv, mv)
            v.drain()
            v.tensor_tensor(g3(xn), g3(sh_w[:, 0:n]), ib, Alu.mult)
            v.drain()
            v.tensor_tensor(sqv, xn, xn, Alu.mult)
            v.drain()
            v.tensor_scalar(qb, sqv, 0.0625, 0.5, Alu.is_gt, Alu.mult)
            v.tensor_scalar(tmp, sqv, 0.5625, 0.5, Alu.is_gt, Alu.mult)
            v.tensor_scalar(xn, xn, 0.0, 2.0, Alu.is_gt, Alu.mult)
            v.drain()
            v.tensor_tensor(qb, qb, tmp, Alu.add)
            v.tensor_scalar(xn, xn, -1.0, None, Alu.add)
            v.drain()
            for m2, w in [(1.5625, 0.5), (3.0625, 0.5), (6.25, None),
                          (12.25, None), (25.0, 2.0)]:
                if w is None:
                    v.tensor_scalar(tmp, sqv, m2, None, Alu.is_gt)
                else:
                    v.tensor_scalar(tmp, sqv, m2, w, Alu.is_gt, Alu.mult)
                v.drain()
                v.tensor_tensor(qb, qb, tmp, Alu.add)
                v.drain()
            v.tensor_tensor(xn, qb, xn, Alu.mult)
            v.drain()
            inc(v.tensor_tensor(g3(xn), g3(xn), mb, Alu.mult), "sV")
        V[f"shq_{tag}"] = C["sV"]

    @blk.gpsimd
    def _(g):
        dmx(g, "sQ4", F2[:].rearrange("p (kc m) -> p kc m", kc=DB), d_w1s[:].rearrange("(kc p) m -> p kc m", p=128))
    V["q3_w1s"] = C["sQ4"]

    @blk.vector
    def _(v):
        v.wait_ge(SEM["sQ4"], V["q3_w1s"])
        v.wait_ge(SEM["sPE"], V["pe_mod5"])
        inc(v.tensor_copy(sh_b[:], F2[:]), "sV")
    V["v_w1b"] = C["sV"]

    @blk.sync
    def _(sy):
        sy.wait_ge(SEM["sV"], V["v_w1b"])
        dmx(sy, "sQW", ag_w_in[OFF_W1:OFF_W1 + SZ_W1].rearrange(
                "(kc p m) -> p kc m", p=128, m=512), sh_b[:].rearrange("p (kc m) -> p kc m", kc=DB))
    V["q_w1in"] = C["sQW"]

    @blk.gpsimd
    def _(g):
        g.wait_ge(SEM["sV"], V["v_w1b"])
        dmx(g, "sQ4", F2[:].rearrange("p (kc m) -> p kc m", kc=MC1), d_w2s[:].rearrange("(kc p) m -> p kc m", p=128))
    V["q3_w2s"] = C["sQ4"]

    @blk.vector
    def _(v):
        v.wait_ge(SEM["sQ4"], V["q3_w2s"])
        v.wait_ge(SEM["sQW"], V["q_w1in"])
        inc(v.tensor_copy(sh_b[:], F2[:]), "sV")
    V["v_w2b"] = C["sV"]

    @blk.sync
    def _(sy):
        sy.wait_ge(SEM["sV"], V["v_w2b"])
        dmx(sy, "sQW", ag_w_in[OFF_W2:OFF_W2 + SZ_W2].rearrange(
                "(kc p m) -> p kc m", p=128, m=128), sh_b[:].rearrange("p (kc m) -> p kc m", kc=MC1))
    V["q_w2in"] = C["sQW"]

    def emit_ffn_shard(d_src, off, nblk, kdim, tag, prev_w_free, prev_b_free,
                       prev_pe=None):
        n = nblk * kdim
        n_kc = kdim // 128

        @blk.gpsimd
        def _(g):
            if isinstance(prev_w_free, tuple):
                g.wait_ge(SEM[prev_w_free[0]], prev_w_free[1])
            else:
                g.wait_ge(SEM["sV"], prev_w_free)
            dmx(g, "sQ3", sh_w[:, 0:n].rearrange("p (nb k) -> p nb k", nb=nblk), d_src[:].rearrange("(nb p) k -> p nb k", p=128))
        V[f"q3_{tag}"] = C["sQ3"]
        V[f"shbfree_{tag}"] = prev_b_free
        emit_shard_quant(n, tag, wait_pe=prev_pe)

        ntile = nblk * n_kc
        for idx in range(ntile):
            nb, kc = idx // n_kc, idx % n_kc

            @blk.tensor
            def _(t, nb=nb, kc=kc, idx=idx):
                if idx == 0:
                    t.wait_ge(SEM["sV"], V[f"shq_{tag}"])
                if idx >= 2:
                    t.wait_ge(SEM["sV"], V[f"shTev_{tag}_{idx-2}"])
                src = sh_f1[:, 0:n].rearrange("p (nb k) -> p nb k", nb=nblk)
                inc(t.transpose((PX if idx % 2 == 0 else PY)[:, 0:128],
                                src[:, nb, 128 * kc:128 * kc + 128], s_id32[:]),
                    "sPE")
            V[f"shTpe_{tag}_{idx}"] = C["sPE"]

            @blk.vector
            def _(v, idx=idx):
                v.wait_ge(SEM["sPE"], V[f"shTpe_{tag}_{idx}"])
                inc(v.tensor_copy(sh_t[:, idx, :],
                                  (PX if idx % 2 == 0 else PY)[:, 0:128]), "sV")
            V[f"shTev_{tag}_{idx}"] = C["sV"]

        @blk.sync
        def _(sy):
            sy.wait_ge(SEM["sV"], V[f"shTev_{tag}_{ntile-1}"])
            shp = sh_t[:, 0:ntile, :].rearrange("p (nb kc) m -> p kc nb m",
                                                nb=nblk)
            hv = ag_w_in[off:off + n_kc * 128 * nblk * 128].rearrange(
                "(kc p nb m) -> p kc nb m", p=128, nb=nblk, m=128)
            if nblk == 1:
                dmx(sy, "sQW", hv[:, :, 0, :], shp[:, :, 0, :])
            else:
                for kc in range(n_kc):
                    dmx(sy, "sQW", hv[:, kc, :, :], shp[:, kc, :, :])
        V[f"q_{tag}in"] = C["sQW"]
        return V[f"shq_{tag}"], V[f"shq_{tag}"]

    wfree, bfree = emit_ffn_shard(d_wgs, OFF_WG, 3, 1024, "wg",
                                  ("sPE", V["pe_mod5"]), ("sQW", V["q_w2in"]))
    wfree, bfree = emit_ffn_shard(d_wus, OFF_WU, 3, 1024, "wu", wfree,
                                  ("sQW", V["q_wgin"]),
                                  prev_pe=V["shTpe_wg_23"])
    wfree, bfree = emit_ffn_shard(d_wds, OFF_WD, 1, 2816, "wd", wfree,
                                  ("sQW", V["q_wuin"]),
                                  prev_pe=V["shTpe_wu_23"])

    @blk.gpsimd
    def _(g):
        g.wait_ge(SEM["sQW"], V["q_wdin"])
        inc(g.collective_compute("AllGather", Alu.bypass, replica_groups=RG,
                                 ins=[ag_w_in[:]], outs=[ag_w_out[:]]), "sAGw")
    V["g_w"] = C["sAGw"]



    for _k1 in range(1, NSTR):
        if _k1 == 3:
            weave_scan(0, 64)
        emit_p1_stripe(_k1)

    def emit_pass_front(k):
        @blk.sync
        def _(sy, k=k):
            sy.wait_ge(SEM["sAGy"], V[f"agy_{k}"])
            if k == 0:
                sy.wait_ge(SEM["sU"], V[f"q_ust_{NSTR-1}"])
                sy.wait_ge(SEM["sPE"], V[f"xT_{NSTR-1}_3"])
                sy.wait_ge(SEM["sQW"], V["q_wdin"])
            else:
                sy.wait_ge(SEM["sPE"], V[f"gu_{k-1}_{NC2-1}"])
            src = bass.AP(ag_y_out, r_y[k],
                          [[B * 1024, 128], [128 * B * 1024, DB], [1, STRIPE]])
            dmx(sy, "sQ", s_yin[:], src)
            dmx(sy, "sQ", s_pin[:], h_p[:, :, 512 * k:512 * k + 512].transpose([1, 0, 2]))
        V[f"q_yin_{k}"] = C["sQ"]

        @blk.sync
        def _(sy, k=k):
            if k == 0:
                sy.wait_ge(SEM["sQP"], V[f"q_xsp_{NSTR-1}"])
            else:
                sy.wait_ge(SEM["sV"], V[f"ep_{k-1}_1"])
            dmx(sy, "sXC", s_xch[:], h_xfm[:, :, 512 * k:512 * k + 512].transpose([1, 0, 2]))
        V[f"q_xch_{k}"] = C["sXC"]

        @blk.vector
        def _(v, k=k):
            v.wait_ge(SEM["sQ"], V[f"q_yin_{k}"])
            v.drain()
            inc(v.tensor_tensor(s_xp[:], s_pin[:], s_yin[:], Alu.add), "sV")
        V[f"xp_{k}"] = C["sV"]

        if STAGE >= 2:
            emit_stats_rows(s_xp, f"rms2_{k}", False,
                            wait_s_v=V[f"xp_{k}"], wait_pe_v=V[f"xp_{k}"])

            @blk.vector
            def _(v, k=k):
                v.wait_ge(SEM["sPE"], V[f"stexp_rms2_{k}"])
                pxb = PX[:, 0:512].unsqueeze(1).broadcast_to([128, DB, 512])
                v.tensor_tensor(s_t1[:], s_xp[:], pxb, Alu.mult)
                v.drain()
                for db in range(DB):
                    v.tensor_scalar(s_h2[:, db, :], s_t1[:, db, :],
                                    s_s2[:, db:db + 1], None, Alu.mult)
                vnop(v)
            V[f"h2_{k}"] = C["sV"]

        if k + 2 < NSTR:
            emit_ug_load(k + 2)

    for k in range(NSTR):
        if k == 0:
            emit_pass_front(0)

        if STAGE == 1:
            @blk.vector
            def _(v, k=k):
                v.wait_ge(SEM["sV"], V[f"xp_{k}"])
                v.tensor_copy(s_x3[:], s_xp[:])
                vnop(v)
            V[f"x3r_{k}"] = C["sV"]
        else:
            w1_scan_base = scan_cur.get(k + 1, 0) if k + 1 < NSTR else 0
            for mc in range(MC1):
                @blk.sync
                def _(s, k=k, mc=mc):
                    if k == 0 and mc == 0:
                        s.wait_ge(SEM["sAGw"], V["g_w"])
                    if mc >= 2:
                        s.wait_ge(SEM["sPE"], V[f"w1mm_{k}_{mc-2}"])
                    elif k > 0:
                        s.wait_ge(SEM["sPE"], V[f"w1mm_{k-1}_{MC1-1}"])
                    rank, col = mc // 4, mc % 4
                    off = rank * WSH + OFF_W1 + col * 128
                    src = bass.AP(ag_w_out, off,
                                  [[512, 128], [128 * 512, DB], [1, 128]])
                    dmx(s, "sW1a" if mc % 2 == 0 else "sW1b", s_w1t[mc % 2][:], src)
                V[f"q_w1t_{k}_{mc}"] = ("sW1a" if mc % 2 == 0 else "sW1b",
                                        C["sW1a" if mc % 2 == 0 else "sW1b"])

                @blk.tensor
                def _(t, k=k, mc=mc):
                    qs, qv = V[f"q_w1t_{k}_{mc}"]
                    t.wait_ge(SEM[qs], qv)
                    if mc == 0:
                        t.wait_ge(SEM["sV"], V[f"h2_{k}"])
                        if k > 0 and STAGE >= 3:
                            t.wait_ge(SEM["sS"], V[f"sil_{k-1}_{NC2//2-1}"])
                            t.wait_ge(SEM["sV"], V[f"hmul_{k-1}_{NC2//2-1}"])
                    if mc >= 4:
                        t.wait_ge(SEM["sS"], V[f"gel_{k}_{mc-4}"])
                    pdst = [PA, PX, PBK, PY][mc % 4][:, 0:512]
                    last = None
                    for kc in range(DB):
                        last = t.matmul(pdst, s_w1t[mc % 2][:, kc, :],
                                        s_h2[:, kc, :], start=(kc == 0),
                                        stop=(kc == DB - 1))
                    inc(last, "sPE")
                V[f"w1mm_{k}_{mc}"] = C["sPE"]

                @blk.scalar
                def _(s, k=k, mc=mc):
                    s.wait_ge(SEM["sPE"], V[f"w1mm_{k}_{mc}"])
                    inc(s.activation(s_g[:, mc, :],
                                     [PA, PX, PBK, PY][mc % 4][:, 0:512],
                                     Act.Gelu, bias=s_b1[:, mc:mc + 1]), "sS")
                V[f"gel_{k}_{mc}"] = C["sS"]

                if k + 1 < NSTR and w1_scan_base < 56:
                    base = w1_scan_base
                    weave_scan(k + 1, base + ((56 - base) * (mc + 1)) // MC1)

            for half in range(2):
                for mc in range(MC1):
                    gidx = half * MC1 + mc

                    @blk.sync
                    def _(s, k=k, half=half, mc=mc, gidx=gidx):
                        if gidx >= 4:
                            ph, pm = divmod(gidx - 4, MC1)
                            s.wait_ge(SEM["sPE"], V[f"w2mm_{k}_{ph}_{pm}"])
                        elif k > 0 and gidx < 4:
                            s.wait_ge(SEM["sPE"], V[f"w2mm_{k-1}_{1}_{MC1-4+gidx}"])
                        qn = ["sW2a", "sW2b", "sW2c", "sW2d"][gidx % 4]
                        off = (half * 4) * WSH + OFF_W2 + mc * 128 * 128
                        src = bass.AP(ag_w_out, off,
                                      [[128, 128], [WSH, 4], [1, 128]])
                        dmx(s, qn, s_w2t[gidx % 4][:], src)
                    _q2 = ["sW2a", "sW2b", "sW2c", "sW2d"][gidx % 4]
                    V[f"q_w2t_{k}_{half}_{mc}"] = (_q2, C[_q2])

                    @blk.tensor
                    def _(t, k=k, half=half, mc=mc):
                        qs, qv = V[f"q_w2t_{k}_{half}_{mc}"]
                        t.wait_ge(SEM[qs], qv)
                        t.wait_ge(SEM["sS"], V[f"gel_{k}_{mc}"])
                        if mc == 0:
                            if half == 1:
                                t.wait_ge(SEM["sV"], V[f"ep_{k}_0"])
                            elif k > 0:
                                t.wait_ge(SEM["sV"],
                                          V[f"x3ep_{k-1}_1"] if STAGE >= 3
                                          else V[f"ep_{k-1}_1"])
                        last = None
                        gidx = half * MC1 + mc
                        for dcl in range(4):
                            last = t.matmul(PACC[dcl][:, 0:512],
                                            s_w2t[gidx % 4][:, dcl, :],
                                            s_g[:, mc, :], start=(mc == 0),
                                            stop=(mc == MC1 - 1))
                        inc(last, "sPE")
                    V[f"w2mm_{k}_{half}_{mc}"] = C["sPE"]

                @blk.vector
                def _(v, k=k, half=half):
                    v.wait_ge(SEM["sPE"], V[f"w2mm_{k}_{half}_{MC1-1}"])
                    if half == 0:
                        v.wait_ge(SEM["sXC"], V[f"q_xch_{k}"])
                    for dcl in range(4):
                        dc = half * 4 + dcl
                        v.scalar_tensor_tensor(s_t1[:, dc, :], PACC[dcl][:, 0:512],
                                               s_b2[:, dc:dc + 1], s_xp[:, dc, :],
                                               Alu.add, Alu.add)
                        v.drain()
                        v.scalar_tensor_tensor(s_x2[:, dc, :], s_t1[:, dc, :],
                                               s_ga[:, dc:dc + 1], s_xch[:, dc, :],
                                               Alu.mult, Alu.add)
                        v.drain()
                    inc(v.drain(), "sV")
                V[f"ep_{k}_{half}"] = C["sV"]

                if half == 0:
                    weave_scan(k + 1, 64)

            if STAGE == 2:
                @blk.vector
                def _(v, k=k):
                    v.wait_ge(SEM["sV"], V[f"ep_{k}_1"])
                    v.tensor_copy(s_x3[:], s_x2[:])
                    vnop(v)
                V[f"x3r_{k}"] = C["sV"]
            else:
                emit_stats_rows(s_x2, f"ln2_{k}", True,
                                wait_s_v=V[f"ep_{k}_1"], wait_pe_v=V[f"ep_{k}_1"])

                @blk.vector
                def _(v, k=k):
                    v.wait_ge(SEM["sPE"], V[f"stexp_ln2_{k}"])
                    pxb = PX[:, 0:512].unsqueeze(1).broadcast_to([128, DB, 512])
                    pyb = PY[:, 0:512].unsqueeze(1).broadcast_to([128, DB, 512])
                    v.tensor_tensor(s_t1[:], s_x2[:], pxb, Alu.mult)
                    v.drain()
                    v.tensor_tensor(s_t1[:], s_t1[:], pyb, Alu.subtract)
                    v.drain()
                    for db in range(DB):
                        v.tensor_scalar(s_h3[:, db, :], s_t1[:, db, :],
                                        s_cb1[:, db:db + 1], s_sbv[:, db:db + 1],
                                        Alu.mult, Alu.add)
                    vnop(v)
                V[f"h3_{k}"] = C["sV"]

                prev = V[f"h3_{k}"]
                for ch in range(4):
                    emit_quant(s_h3[:, 2 * ch:2 * ch + 2, :]
                               .rearrange("p a b -> p (a b)"),
                               s_xq[:, 2 * ch:2 * ch + 2, :]
                               .rearrange("p a b -> p (a b)"),
                               2 * STRIPE, f"xq_{k}_{ch}", wait_v=prev)
                    prev = V[f"q_xq_{k}_{ch}"]
                V[f"xq_{k}"] = prev

                hq_queue = []
                for nc_ in range(NC2):
                    @blk.sync
                    def _(s, k=k, nc_=nc_):
                        if nc_ >= 2:
                            s.wait_ge(SEM["sPE"], V[f"gu_{k}_{nc_-2}"])
                        rank, col = nc_ // 3, nc_ % 3
                        offg = rank * WSH + OFF_WG + col * 128
                        offu = rank * WSH + OFF_WU + col * 128
                        srcg = bass.AP(ag_w_out, offg,
                                       [[384, 128], [128 * 384, DB], [1, 128]])
                        srcu = bass.AP(ag_w_out, offu,
                                       [[384, 128], [128 * 384, DB], [1, 128]])
                        qn = "sGUa" if nc_ % 2 == 0 else "sGUb"
                        dmx(s, qn, s_wgt[nc_ % 2][:], srcg)
                        dmx(s, qn, s_wut[nc_ % 2][:], srcu)
                    _qg = "sGUa" if nc_ % 2 == 0 else "sGUb"
                    V[f"q_gu_{k}_{nc_}"] = (_qg, C[_qg])

                    @blk.tensor
                    def _(t, k=k, nc_=nc_):
                        qs, qv = V[f"q_gu_{k}_{nc_}"]
                        t.wait_ge(SEM[qs], qv)
                        p = nc_ // 2
                        if nc_ == 0:
                            t.wait_ge(SEM["sV"], V[f"q_xq_{k}_0"])
                            t.wait_ge(SEM["sS"], V[f"gel_{k}_{MC1-1}"])
                        if nc_ % 2 == 0 and p >= 2:
                            t.wait_ge(SEM["sV"], V[f"hmul_{k}_{p-2}"])
                            t.wait_ge(SEM["sS"], V[f"sil_{k}_{p-2}"])
                        grp = PG1 if p % 2 == 0 else PG2
                        go = 512 * (nc_ % 2)
                        last = None
                        for kc in range(DB):
                            if nc_ == 0 and kc in (2, 4, 6):
                                t.wait_ge(SEM["sV"], V[f"q_xq_{k}_{kc//2}"])
                            last = t.matmul(grp[:, go:go + 512],
                                            s_wgt[nc_ % 2][:, kc, :],
                                            s_xq[:, kc, :], start=(kc == 0),
                                            stop=(kc == DB - 1))
                        for kc in range(DB):
                            last = t.matmul(grp[:, 1024 + go:1536 + go],
                                            s_wut[nc_ % 2][:, kc, :],
                                            s_xq[:, kc, :], start=(kc == 0),
                                            stop=(kc == DB - 1))
                        inc(last, "sPE")
                    V[f"gu_{k}_{nc_}"] = C["sPE"]

                    if nc_ % 2 == 1:
                        @blk.scalar
                        def _(s, k=k, nc_=nc_):
                            s.wait_ge(SEM["sPE"], V[f"gu_{k}_{nc_}"])
                            p = nc_ // 2
                            if p >= 1:
                                s.wait_ge(SEM["sV"], V[f"hmul_{k}_{p-1}"])
                            elif k > 0:
                                s.wait_ge(SEM["sV"], V[f"hmul_{k-1}_{NC2//2-1}"])
                            grp = PG1 if (nc_ // 2) % 2 == 0 else PG2
                            inc(s.activation(s_sg2[:], grp[:, 0:1024],
                                             Act.Silu), "sS")
                        V[f"sil_{k}_{nc_//2}"] = C["sS"]

                        @blk.vector
                        def _(v, k=k, nc_=nc_):
                            v.wait_ge(SEM["sS"], V[f"sil_{k}_{nc_//2}"])
                            grp = PG1 if (nc_ // 2) % 2 == 0 else PG2
                            inc(v.tensor_tensor(s_hq[:, nc_ - 1:nc_ + 1, :]
                                                .rearrange("p a b -> p (a b)"),
                                                s_sg2[:], grp[:, 1024:2048],
                                                Alu.mult), "sV")
                        V[f"hmul_{k}_{nc_//2}"] = C["sV"]

                        p = nc_ // 2
                        if p == 3:
                            hs = s_hq[:, 0:6, :].rearrange("p a b -> p (a b)")
                            hq_queue.extend(quant_steps(
                                hs, hs, 6 * STRIPE, f"hq_{k}_0",
                                wait_v=V[f"hmul_{k}_{2}"]))
                        elif p == 6:
                            hs = s_hq[:, 6:12, :].rearrange("p a b -> p (a b)")
                            hq_queue.extend(quant_steps(
                                hs, hs, 6 * STRIPE, f"hq_{k}_1",
                                wait_v=V[f"hmul_{k}_{5}"]))
                        elif p == 9:
                            hs = s_hq[:, 12:17, :].rearrange("p a b -> p (a b)")
                            hq_queue.extend(quant_steps(
                                hs, hs, 5 * STRIPE, f"hq_{k}_2",
                                wait_v=V[f"hmul_{k}_{8}"]))
                        if p >= 4:
                            for _ in range(5):
                                if hq_queue:
                                    hq_queue.pop(0)()

                while hq_queue:
                    hq_queue.pop(0)()
                V[f"hq01_{k}"] = V[f"q_hq_{k}_2"]
                hs = s_hq[:, 17:22, :].rearrange("p a b -> p (a b)")
                for st in quant_steps(hs, hs, 5 * STRIPE, f"hq_{k}_3",
                                      wait_v=V[f"hmul_{k}_{NC2//2-1}"]):
                    st()
                V[f"hq_{k}"] = V[f"q_hq_{k}_3"]

                if k + 1 < NSTR:
                    emit_pass_front(k + 1)

                for half in range(2):
                    for dcl in range(4):
                        dc = half * 4 + dcl

                        @blk.sync
                        def _(s, k=k, dc=dc):
                            if dc >= 2:
                                s.wait_ge(SEM["sPE"], V[f"dn_{k}_{dc-2}"])
                            off = dc * WSH + OFF_WD
                            src = bass.AP(ag_w_out, off,
                                          [[128, 128], [128 * 128, NC2], [1, 128]])
                            qn = "sWDa" if dc % 2 == 0 else "sWDb"
                            dmx(s, qn, s_wdt[dc % 2][:], src)
                        _qd = "sWDa" if dc % 2 == 0 else "sWDb"
                        V[f"q_wd_{k}_{dc}"] = (_qd, C[_qd])

                        @blk.tensor
                        def _(t, k=k, dc=dc, dcl=dcl, half=half):
                            qs, qv = V[f"q_wd_{k}_{dc}"]
                            t.wait_ge(SEM[qs], qv)
                            if dc == 0:
                                t.wait_ge(SEM["sV"], V[f"hq01_{k}"])
                            if half == 1 and dcl == 0:
                                t.wait_ge(SEM["sV"], V[f"x3ep_{k}_0"])
                            last = None
                            for kc in range(NC2):
                                if dc == 0 and kc == 17:
                                    t.wait_ge(SEM["sV"], V[f"hq_{k}"])
                                last = t.matmul(PACC[dcl][:, 0:512],
                                                s_wdt[dc % 2][:, kc, :],
                                                s_hq[:, kc, :], start=(kc == 0),
                                                stop=(kc == NC2 - 1))
                            inc(last, "sPE")
                        V[f"dn_{k}_{dc}"] = C["sPE"]

                        weave_scan(k + 2, 2 * (dc + 1))

                    @blk.vector
                    def _(v, k=k, half=half):
                        v.wait_ge(SEM["sPE"], V[f"dn_{k}_{half*4+3}"])
                        for dcl in range(4):
                            dc = half * 4 + dcl
                            v.scalar_tensor_tensor(s_x3[:, dc, :],
                                                   PACC[dcl][:, 0:512],
                                                   s_gb[:, dc:dc + 1],
                                                   s_x2[:, dc, :],
                                                   Alu.mult, Alu.add)
                            v.drain()
                        inc(v.drain(), "sV")
                    V[f"x3ep_{k}_{half}"] = C["sV"]

                @blk.vector
                def _(v, k=k):
                    v.wait_ge(SEM["sV"], V[f"x3ep_{k}_1"])
                    vnop(v)
                V[f"x3r_{k}"] = C["sV"]

        for r in range(4):
            for db in range(DB):
                @blk.tensor
                def _(t, k=k, r=r, db=db):
                    if r == 0 and db == 0:
                        t.wait_ge(SEM["sV"], V[f"x3r_{k}"])
                    oidx = r * DB + db
                    if oidx >= 2:
                        pr, pdb = divmod(oidx - 2, DB)
                        t.wait_ge(SEM["sV"], V[f"oev_{k}_{pr}_{pdb}"])
                    inc(t.transpose((PX if db % 2 == 0 else PY)[:, 0:128],
                                    s_x3[:, db, 128 * r:128 * r + 128], s_id32[:]),
                        "sPE")
                V[f"oT_{k}_{r}_{db}"] = C["sPE"]

                @blk.vector
                def _(v, k=k, r=r, db=db):
                    v.wait_ge(SEM["sPE"], V[f"oT_{k}_{r}_{db}"])
                    gidx = 4 * k + r
                    if db == 0 and gidx >= 2:
                        pk, pr = divmod(gidx - 2, 4)
                        qs, qv = V[f"q_out_{pk}_{pr}"]
                        v.wait_ge(SEM[qs], qv)
                    inc(v.tensor_copy(s_otm[r % 2][:, 128 * db:128 * db + 128],
                                      (PX if db % 2 == 0 else PY)[:, 0:128]), "sV")
                V[f"oev_{k}_{r}_{db}"] = C["sV"]

            weave_scan(k + 2, 16 + (r + 1))

            @blk.sync
            def _(sy, k=k, r=r):
                sy.wait_ge(SEM["sV"], V[f"oev_{k}_{r}_{DB-1}"])
                dmx(sy, "sOa" if r % 2 == 0 else "sOb",
                    d_out[512 * k + 128 * r:512 * k + 128 * r + 128, :],
                    s_otm[r % 2][:])
            _qo = "sOa" if r % 2 == 0 else "sOb"
            V[f"q_out_{k}_{r}"] = (_qo, C[_qo])

        @blk.vector
        def _(v, k=k):
            vnop(v)
        V[f"pass_done_{k}"] = C["sV"]

    @blk.sync
    def _(sy):
        for n in ["sQ", "sQS", "sU", "sY", "sOa", "sOb", "sQ2", "sQ2b", "sQ3", "sQW",
              "sQP", "sQXa", "sQXb",
                  "sW1a", "sW1b", "sW2a", "sW2b", "sW2c", "sW2d",
                  "sGUa", "sGUb", "sWDa", "sWDb", "sXC", "sQ4"]:
            if C[n]:
                sy.wait_ge(SEM[n], C[n])

    es.close()
    return nc


_NC_CACHE = None


def _get_nc():
    global _NC_CACHE
    if _NC_CACHE is None:
        _NC_CACHE = build_nc()
    return _NC_CACHE


def make_in_maps(inputs):
    f32 = np.float32
    x = np.asarray(inputs["L0_attn_x"], f32)
    A = np.asarray(inputs["L0_attn_A_ssm"], f32)
    Bm = np.asarray(inputs["L0_attn_B_ssm"], f32)
    Cm = np.asarray(inputs["L0_attn_C_ssm"], f32)
    Dv = np.asarray(inputs["L0_attn_D_ssm"], f32)
    s1 = np.asarray(inputs["L0_attn_scale1"], f32)
    s2 = np.asarray(inputs["L0_attn_scale2"], f32)
    w1 = np.asarray(inputs["L0_attn_w1"], f32)
    b1 = np.asarray(inputs["L0_attn_b1"], f32)
    w2 = np.asarray(inputs["L0_attn_w2"], f32)
    b2 = np.asarray(inputs["L0_attn_b2"], f32)
    wg = np.asarray(inputs["L0_ffn_w_gate"], f32)
    wu = np.asarray(inputs["L0_ffn_w_up"], f32)
    wd = np.asarray(inputs["L0_ffn_w_down"], f32)
    cc = np.asarray(inputs["L0_c"], f32)
    wmod = np.asarray(inputs["L0_wmod"], f32)

    dt = np.exp(np.linspace(np.log(0.001), np.log(0.1), N16)).astype(f32)
    dt_mat = np.ascontiguousarray(np.broadcast_to(dt[None, :], (128, N16)))

    def fmvec(v):
        return np.ascontiguousarray(v.reshape(DB, 128).T)

    wg_pad = np.zeros((384 * NCORE, D), f32)
    wg_pad[:DFF2] = wg
    wu_pad = np.zeros((384 * NCORE, D), f32)
    wu_pad[:DFF2] = wu

    ident = np.eye(128, dtype=f32)
    identb = np.eye(128, dtype=ml_dtypes.bfloat16)
    ones_col = np.ones((128, 1), f32)
    ones_row = np.ones((1, 128), f32)

    in_maps = []
    for c in range(NCORE):
        b = c // 2
        half = c % 2
        rows = np.concatenate([np.arange(1024 * kk + 512 * half,
                                         1024 * kk + 512 * half + 512)
                               for kk in range(NSTR)])
        sel = np.zeros((128, B), f32)
        sel[:, b] = 1.0
        offs = np.zeros((1, 8), np.int32)
        for kk in range(NSTR):
            offs[0, kk] = kk * (NCORE * DB * 128 * STRIPE) + c * (128 * STRIPE)
            offs[0, 4 + kk] = kk * (NCORE * 128 * B * 1024) + b * 1024 + half * 512
        in_maps.append({
            "x_own": np.ascontiguousarray(x[b][rows]),
            "A_sl": np.ascontiguousarray(A[128 * c:128 * c + 128]),
            "B_sl": np.ascontiguousarray(Bm[128 * c:128 * c + 128]),
            "C_sl": np.ascontiguousarray(Cm[128 * c:128 * c + 128]),
            "dt_mat": dt_mat,
            "D_fm": fmvec(Dv), "s1_fm": fmvec(s1), "s2_fm": fmvec(s2),
            "b2_fm": fmvec(b2),
            "b1_fm": np.ascontiguousarray(b1.reshape(MC1, 128).T),
            "cT_fm": np.ascontiguousarray(
                cc.T.reshape(DB, 128, B).transpose(1, 0, 2)),
            "wmod_sl": np.ascontiguousarray(wmod[:, 768 * c:768 * c + 768]),
            "sel": sel,
            "w1_sh": np.ascontiguousarray(w1[:, 512 * c:512 * c + 512]),
            "w2_sh": np.ascontiguousarray(w2[:, 128 * c:128 * c + 128]),
            "wg_sh": np.ascontiguousarray(wg_pad[384 * c:384 * c + 384]),
            "wu_sh": np.ascontiguousarray(wu_pad[384 * c:384 * c + 384]),
            "wd_sh": np.ascontiguousarray(wd[128 * c:128 * c + 128]),
            "ident": ident, "identb": identb,
            "ones_col": ones_col, "ones_row": ones_row,
            "offs": offs,
        })
    return in_maps


def assemble(results):
    out = np.zeros((B, T, D), np.float32)
    for c in range(NCORE):
        b = c // 2
        half = c % 2
        rows = np.concatenate([np.arange(1024 * kk + 512 * half,
                                         1024 * kk + 512 * half + 512)
                               for kk in range(NSTR)])
        out[b][rows] = np.asarray(results[c]["out_own"], np.float32)
    return out


def kernel(**inputs):
    nc = _get_nc()
    in_maps = make_in_maps(inputs)
    res = run_bass_kernel_spmd(nc, in_maps, core_ids=list(range(NCORE)))
    return assemble(res.results)


# version 18
# speedup vs baseline: 1.1516x; 1.0106x over previous
import numpy as np
import ml_dtypes
from contextlib import ExitStack

import concourse.bass as bass
import concourse.mybir as mybir
from concourse.bass_utils import run_bass_kernel_spmd

F32 = mybir.dt.float32
BF16 = mybir.dt.bfloat16
I32 = mybir.dt.int32
Alu = mybir.AluOpType
Act = mybir.ActivationFunctionType
AX = mybir.AxisListType.X

B, T, D = 4, 4096, 1024
N16 = 16
DFF1 = 4096
DFF2 = 2816
NCORE = 8
TOK = 2048
STRIPE = 512
NSTR = 4
DB = 8
MC1 = 32
NC2 = 22
LNEPS = 1e-5
RMSEPS = 1e-6

OFF_W1 = 0
SZ_W1 = 1024 * 512
OFF_W2 = OFF_W1 + SZ_W1
SZ_W2 = 4096 * 128
OFF_WG = OFF_W2 + SZ_W2
SZ_WG = 1024 * 384
OFF_WU = OFF_WG + SZ_WG
SZ_WU = 1024 * 384
OFF_WD = OFF_WU + SZ_WU
SZ_WD = 2816 * 128
WSH = OFF_WD + SZ_WD

STAGE = 3


def build_nc():
    nc = bass.Bass("TRN2", target_bir_lowering=False)

    d_x = nc.dram_tensor("x_own", [TOK, D], F32, kind="ExternalInput")
    d_A = nc.dram_tensor("A_sl", [128, N16], F32, kind="ExternalInput")
    d_B = nc.dram_tensor("B_sl", [128, N16], F32, kind="ExternalInput")
    d_C = nc.dram_tensor("C_sl", [128, N16], F32, kind="ExternalInput")
    d_dt = nc.dram_tensor("dt_mat", [128, N16], F32, kind="ExternalInput")
    d_Dfm = nc.dram_tensor("D_fm", [128, DB], F32, kind="ExternalInput")
    d_s1 = nc.dram_tensor("s1_fm", [128, DB], F32, kind="ExternalInput")
    d_s2 = nc.dram_tensor("s2_fm", [128, DB], F32, kind="ExternalInput")
    d_b2 = nc.dram_tensor("b2_fm", [128, DB], F32, kind="ExternalInput")
    d_b1 = nc.dram_tensor("b1_fm", [128, MC1], F32, kind="ExternalInput")
    d_cT = nc.dram_tensor("cT_fm", [128, DB, B], F32, kind="ExternalInput")
    d_wmod = nc.dram_tensor("wmod_sl", [D, 768], F32, kind="ExternalInput")
    d_sel = nc.dram_tensor("sel", [128, B], F32, kind="ExternalInput")
    d_w1s = nc.dram_tensor("w1_sh", [D, 512], F32, kind="ExternalInput")
    d_w2s = nc.dram_tensor("w2_sh", [DFF1, 128], F32, kind="ExternalInput")
    d_wgs = nc.dram_tensor("wg_sh", [384, D], F32, kind="ExternalInput")
    d_wus = nc.dram_tensor("wu_sh", [384, D], F32, kind="ExternalInput")
    d_wds = nc.dram_tensor("wd_sh", [128, DFF2], F32, kind="ExternalInput")
    d_id32 = nc.dram_tensor("ident", [128, 128], F32, kind="ExternalInput")
    d_idbf = nc.dram_tensor("identb", [128, 128], BF16, kind="ExternalInput")
    d_onc = nc.dram_tensor("ones_col", [128, 1], F32, kind="ExternalInput")
    d_onr = nc.dram_tensor("ones_row", [1, 128], F32, kind="ExternalInput")
    d_offs = nc.dram_tensor("offs", [1, 8], I32, kind="ExternalInput")
    d_out = nc.dram_tensor("out_own", [TOK, D], F32, kind="ExternalOutput")

    h_xfm = nc.dram_tensor("h_xfm", [DB, 128, TOK], F32, kind="Internal")
    h_p = nc.dram_tensor("h_p", [DB, 128, TOK], BF16, kind="Internal")
    ag_u_in = nc.dram_tensor("ag_u_in", [NSTR, DB, 128, STRIPE], BF16, kind="Internal")
    ag_u_out = nc.dram_tensor("ag_u_out", [NSTR, NCORE, DB, 128, STRIPE], BF16, kind="Internal", addr_space="Shared")
    ag_y_in = nc.dram_tensor("ag_y_in", [NSTR, 128, B, 1024], BF16, kind="Internal")
    ag_y_out = nc.dram_tensor("ag_y_out", [NSTR, NCORE, 128, B, 1024], BF16, kind="Internal", addr_space="Shared")
    ag_w_in = nc.dram_tensor("ag_w_in", [WSH], BF16, kind="Internal")
    ag_w_out = nc.dram_tensor("ag_w_out", [NCORE, WSH], BF16, kind="Internal", addr_space="Shared")
    ag_m_in = nc.dram_tensor("ag_m_in", [6, 128, B], F32, kind="Internal")
    ag_m_out = nc.dram_tensor("ag_m_out", [NCORE, 6, 128, B], F32, kind="Internal", addr_space="Shared")

    RG = [list(range(NCORE))]
    es = ExitStack()

    def sbuf(name, shape, dtype):
        return es.enter_context(nc.sbuf_tensor(name, shape, dtype))

    C, SEM = {}, {}
    for n in ["sQ", "sQS", "sU", "sY", "sOa", "sOb", "sQ2", "sQ2b", "sQ3", "sQW",
              "sQP", "sQXa", "sQXb",
              "sW1a", "sW1b", "sW2a", "sW2b", "sW2c", "sW2d",
              "sGUa", "sGUb", "sWDa", "sWDb", "sXC", "sQ4",
              "sPE", "sV", "sS", "sAGu", "sAGy", "sAGw", "sAGm"]:
        SEM[n] = es.enter_context(nc.semaphore(n))
        C[n] = 0

    def inc(instr, name, k=1):
        instr.then_inc(SEM[name], k)
        C[name] += k
        return C[name]

    def dmx(eng, name, out_ap, in_ap):
        ins = eng.dma_start(out=out_ap, in_=in_ap)
        ins.then_inc(SEM[name], 16)
        C[name] += 16
        return C[name]

    V = {}

    s_id32 = sbuf("s_id32", [128, 128], F32)
    s_idbf = sbuf("s_idbf", [128, 128], BF16)
    s_onc = sbuf("s_onc", [128, 1], F32)
    s_onr = sbuf("s_onr", [1, 128], F32)
    s_oncb = sbuf("s_oncb", [128, 1], BF16)
    s_A = sbuf("s_A", [128, N16], F32)
    s_Bm = sbuf("s_Bm", [128, N16], F32)
    s_Cm = sbuf("s_Cm", [128, N16], F32)
    s_dt = sbuf("s_dt", [128, N16], F32)
    s_dA = sbuf("s_dA", [128, N16], F32)
    s_CdB = sbuf("s_CdB", [128, N16], F32)
    s_t16a = sbuf("s_t16a", [128, N16], F32)
    s_t16b = sbuf("s_t16b", [128, N16], F32)
    s_Dfm = sbuf("s_Dfm", [128, DB], F32)
    s_s1 = sbuf("s_s1", [128, DB], F32)
    s_s2 = sbuf("s_s2", [128, DB], F32)
    s_b2 = sbuf("s_b2", [128, DB], F32)
    s_b1 = sbuf("s_b1", [128, MC1], F32)
    s_sel = sbuf("s_sel", [128, B], F32)
    s_cT = sbuf("s_cT", [128, DB, B], F32)
    s_cTb = sbuf("s_cTb", [128, DB, B], F32)
    s_modsl = sbuf("s_modsl", [128, 6, B], F32)
    s_modal = sbuf("s_modal", [128, 48, B], F32)
    s_mod = sbuf("s_mod", [128, 48], F32)
    s_ca1 = sbuf("s_ca1", [128, DB], F32)
    s_sa = sbuf("s_sa", [128, DB], F32)
    s_ga = sbuf("s_ga", [128, DB], F32)
    s_cb1 = sbuf("s_cb1", [128, DB], F32)
    s_sbv = sbuf("s_sbv", [128, DB], F32)
    s_gb = sbuf("s_gb", [128, DB], F32)
    s_gtmp = sbuf("s_gtmp", [128, DB], F32)
    s_gtmp2 = sbuf("s_gtmp2", [128, DB], F32)
    s_offs = sbuf("s_offs", [1, 8], I32)
    s_state = sbuf("s_state", [128, B * N16], F32)

    a1 = sbuf("a1", [128, 32, 512], BF16)
    a2 = sbuf("a2", [128, 16, 512], BF16)
    F1 = sbuf("F1", [128, 4096], F32)
    F2 = sbuf("F2", [128, 4096], F32)
    F3 = sbuf("F3", [128, 3072], F32)


    sh_w = F1[:]
    sh_b = a1[:, 0:8, :].rearrange("p a b -> p (a b)")
    sh_f1 = F2[:, 0:3072]
    sh_f2 = F3[:, 0:3072]
    sh_i1 = a1[:, 16:22, :].rearrange("p a b -> p (a b)")
    sh_t = a1[:, 24:30, :].rearrange("p a b -> p (a b)").rearrange(
        "p (i m) -> p i m", m=128)

    s_xtm = [F3[:, 0:1024], F3[:, 1024:2048]]
    s_sq = sbuf("s_sq", [128, DB, STRIPE], BF16)
    s_t1 = s_sq
    s_xfm = sbuf("s_xfm", [128, DB, STRIPE], F32)
    sh_m = F3[:, 2560:2816]
    sh_i = F3[:, 2816:3072]
    s_h1 = a2[:, 0:8, :]
    s_u = a2[:, 8:16, :]
    s_pch = a1[:, 8:16, :]
    s_row = sbuf("s_row", [1, STRIPE], F32)
    s_row2 = sbuf("s_row2", [1, STRIPE], F32)

    s_ug2 = sbuf("s_ug2", [128, 2, DB, STRIPE], BF16)
    s_ug = [s_ug2[:, 0], s_ug2[:, 1]]
    s_z = [F2[:, 2048:3072], F2[:, 3072:4096]]
    s_yac2 = sbuf("s_yac2", [128, 2, 1024], F32)
    s_yst2 = sbuf("s_yst2", [128, 4096], BF16)

    s_yin = F3[:, 0:2048].bitcast(BF16).rearrange("p (a b) -> p a b", b=STRIPE)
    s_pin = a1[:, 22:30, :]
    s_xp = sbuf("s_xp", [128, DB, STRIPE], BF16)
    s_h2 = a2[:, 0:8, :]
    s_g = a1[:]
    s_xch = s_xfm
    s_x2 = a2[:, 8:16, :]
    s_h3 = a1[:, 22:30, :]
    s_xq = a1[:, 22:30, :]
    s_hq = a1[:, 0:22, :]
    s_x3 = F1[:].rearrange("p (a b) -> p a b", b=STRIPE)
    s_otm = [F2[:, 0:1024], F2[:, 1024:2048]]
    s_sg2 = sbuf("s_sg2", [128, 2 * STRIPE], BF16)

    s_w1t = [sbuf(f"s_w1t{i}", [128, DB, 128], BF16) for i in range(2)]
    s_w2t = [sbuf(f"s_w2t{i}", [128, 4, 128], BF16) for i in range(4)]
    s_wgt = [sbuf(f"s_wgt{i}", [128, DB, 128], BF16) for i in range(2)]
    s_wut = [sbuf(f"s_wut{i}", [128, DB, 128], BF16) for i in range(2)]
    s_wdt = [sbuf(f"s_wdt{i}", [128, NC2, 128], BF16) for i in range(2)]

    QW = 3072
    q_vt = sbuf("q_vt", [128, QW], BF16)
    q_sq = sbuf("q_sq", [128, QW], BF16)
    q_q1 = s_xp[:, 0:8, :].rearrange("p a b -> p (a b)")
    q_h5 = s_sq[:, 0:8, :].rearrange("p a b -> p (a b)")
    q_u = a2[:, 0:8, :].rearrange("p a b -> p (a b)")
    q_m = F3[:, 2048:2048 + QW // 16]
    q_in = F3[:, 2304:2304 + QW // 16]

    PG1 = es.enter_context(nc.psum_tensor("pg1", [128, 2048], F32))
    PG2 = es.enter_context(nc.psum_tensor("pg2", [128, 2048], F32))
    PA = PG1[:, 0:512]
    PX = PG1[:, 512:1024]
    PBK = PG1[:, 1024:1536]
    PY = PG1[:, 1536:2048]
    PACC = [PG2[:, 512 * i:512 * i + 512] for i in range(4)]

    blk = es.enter_context(nc.Block())

    r_u = [es.enter_context(nc.sync.register(f"r_u{k}")) for k in range(NSTR)]
    r_y = [es.enter_context(nc.sync.register(f"r_y{k}")) for k in range(NSTR)]

    def vnop(v, name="sV"):
        return inc(v.drain(), name)

    @blk.sync
    def _(sy):
        for d, s in [(d_id32, s_id32), (d_idbf, s_idbf), (d_onc, s_onc),
                     (d_onr, s_onr), (d_A, s_A), (d_B, s_Bm), (d_C, s_Cm),
                     (d_dt, s_dt), (d_Dfm, s_Dfm), (d_s1, s_s1), (d_s2, s_s2),
                     (d_b2, s_b2), (d_b1, s_b1), (d_sel, s_sel), (d_cT, s_cT),
                     (d_offs, s_offs)]:
            dmx(sy, "sQ", s[:], d[:])
        sy.wait_ge(SEM["sQ"], C["sQ"])
        for k in range(NSTR):
            sy.reg_load(r_u[k], s_offs[0:1, k:k + 1])
            sy.reg_load(r_y[k], s_offs[0:1, 4 + k:5 + k])
    V["q_small"] = C["sQ"]

    @blk.scalar
    def _(s):
        s.wait_ge(SEM["sQ"], V["q_small"])
        inc(s.activation(s_t16a[:], s_A[:], Act.Exp), "sS")
    V["s_expA"] = C["sS"]

    @blk.vector
    def _(v):
        v.wait_ge(SEM["sS"], V["s_expA"])
        v.tensor_tensor(s_t16b[:], s_t16a[:], s_dt[:], Alu.mult)
        inc(v.reciprocal(s_dA[:], s_t16a[:]), "sV")
    V["v_rec"] = C["sV"]

    @blk.scalar
    def _(s):
        s.wait_ge(SEM["sV"], V["v_rec"])
        inc(s.activation(s_t16a[:], s_t16b[:], Act.Exp, scale=-1.0), "sS")
    V["s_dA"] = C["sS"]

    @blk.vector
    def _(v):
        v.wait_ge(SEM["sS"], V["s_dA"])
        v.tensor_scalar(s_t16b[:], s_t16a[:], -1.0, 1.0, Alu.mult, Alu.add)
        v.drain()
        v.tensor_tensor(s_t16b[:], s_t16b[:], s_Bm[:], Alu.mult)
        v.drain()
        v.tensor_tensor(s_t16b[:], s_t16b[:], s_dA[:], Alu.mult)
        v.drain()
        v.tensor_tensor(s_CdB[:], s_t16b[:], s_Cm[:], Alu.mult)
        v.tensor_copy(s_dA[:], s_t16a[:])
        v.tensor_copy(s_oncb[:], s_onc[:])
        v.memset(s_state[:], 0.0)
        vnop(v)
    V["v_tabs"] = C["sV"]

    for h in range(2):
        @blk.gpsimd
        def _(g, h=h):
            if h == 1:
                g.wait_ge(SEM["sPE"], V["pe_mod2"])
            dmx(g, "sQ3", sh_w[:, 0:3072].rearrange("p (db m) -> p db m", db=DB), d_wmod[:, 384 * h:384 * h + 384].rearrange(
                    "(db p) m -> p db m", p=128))
        V[f"q3_wmod{h}"] = C["sQ3"]

        @blk.scalar
        def _(s, h=h):
            if h == 0:
                s.wait_ge(SEM["sQ"], V["q_small"])
                inc(s.activation(s_cTb[:], s_cT[:], Act.Silu), "sS")
        if h == 0:
            V["s_siluc"] = C["sS"]

        for jl in range(3):
            j = 3 * h + jl

            @blk.tensor
            def _(t, h=h, jl=jl, j=j):
                if j == 0:
                    t.wait_ge(SEM["sS"], V["s_siluc"])
                if jl == 0:
                    t.wait_ge(SEM["sQ3"], V[f"q3_wmod{h}"])
                if j > 0:
                    t.wait_ge(SEM["sV"], V[f"v_modev{j-1}"])
                wv = sh_w[:, 0:3072].rearrange("p (db j m) -> p db j m",
                                               db=DB, j=3)
                last = None
                for kc in range(DB):
                    last = t.matmul(PX[:, 0:B], wv[:, kc, jl, :],
                                    s_cTb[:, kc, :],
                                    start=(kc == 0), stop=(kc == DB - 1))
                inc(last, "sPE")
            V[f"pe_mod{j}"] = C["sPE"]

            @blk.vector
            def _(v, j=j):
                v.wait_ge(SEM["sPE"], V[f"pe_mod{j}"])
                inc(v.tensor_copy(s_modsl[:, j, :], PX[:, 0:B]), "sV")
            V[f"v_modev{j}"] = C["sV"]

    @blk.sync
    def _(sy):
        sy.wait_ge(SEM["sV"], V["v_modev5"])
        dmx(sy, "sQS", ag_m_in[:].transpose([1, 0, 2]), s_modsl[:])
    V["q_modin"] = C["sQS"]

    @blk.gpsimd
    def _(g):
        g.wait_ge(SEM["sQS"], V["q_modin"])
        inc(g.collective_compute("AllGather", Alu.bypass, replica_groups=RG,
                                 ins=[ag_m_in[:]], outs=[ag_m_out[:]]), "sAGm")
    V["g_mod"] = C["sAGm"]

    @blk.sync
    def _(sy):
        sy.wait_ge(SEM["sAGm"], V["g_mod"])
        dmx(sy, "sQ", s_modal[:], ag_m_out[:].rearrange("r s p b -> p (r s) b"))
    V["q_modal"] = C["sQ"]

    @blk.vector
    def _(v):
        v.wait_ge(SEM["sQ"], V["q_modal"])
        v.scalar_tensor_tensor(s_mod[:], s_modal[:, :, 0], s_sel[:, 0:1],
                               s_modal[:, :, 0], Alu.mult, Alu.bypass)
        for b in range(1, B):
            v.drain()
            v.scalar_tensor_tensor(s_mod[:], s_modal[:, :, b], s_sel[:, b:b + 1],
                                   s_mod[:], Alu.mult, Alu.add)
        v.drain()
        mv = s_mod[:].rearrange("p (v db) -> p v db", v=6)
        v.tensor_scalar(s_ca1[:], mv[:, 1, :], 1.0, None, Alu.add)
        v.tensor_copy(s_sa[:], mv[:, 0, :])
        v.tensor_scalar(s_cb1[:], mv[:, 4, :], 1.0, None, Alu.add)
        v.tensor_copy(s_sbv[:], mv[:, 3, :])
        for src_i, dst in [(2, s_ga), (5, s_gb)]:
            v.tensor_scalar(s_gtmp[:], mv[:, src_i, :], 0.0, 2.0,
                            Alu.is_ge, Alu.mult)
            v.drain()
            v.tensor_scalar(s_gtmp[:], s_gtmp[:], -1.0, None, Alu.add)
            v.tensor_scalar(s_gtmp2[:], mv[:, src_i, :], -1.0, None, Alu.mult)
            v.drain()
            v.tensor_tensor(s_gtmp2[:], s_gtmp2[:], mv[:, src_i, :], Alu.max)
            v.drain()
            v.tensor_scalar(s_gtmp2[:], s_gtmp2[:], 0.25, None, Alu.max)
            v.drain()
            v.tensor_tensor(dst[:], s_gtmp[:], s_gtmp2[:], Alu.mult)
            v.drain()
        vnop(v)
    V["v_mod"] = C["sV"]

    def emit_stats_rows(src, tag, want_mean, wait_s_v=None, wait_pe_v=None,
                        src_f32=False, wait_s_half=None):
        if wait_s_half is not None:
            @blk.scalar
            def _(s):
                s.wait_ge(SEM["sV"], wait_s_half)
                inc(s.activation(s_sq[:, 0:4, :], src[:, 0:4, :],
                                 Act.Square), "sS")
            V[f"sqh_{tag}"] = C["sS"]

        @blk.scalar
        def _(s):
            if wait_s_v is not None:
                s.wait_ge(SEM["sV"], wait_s_v)
            if wait_s_half is None:
                inc(s.activation(s_sq[:], src[:], Act.Square), "sS")
            else:
                inc(s.activation(s_sq[:, 4:8, :], src[:, 4:8, :],
                                 Act.Square), "sS")
        V[f"sq_{tag}"] = C["sS"]

        @blk.tensor
        def _(t):
            ones = s_onc[:] if src_f32 else s_oncb[:]
            if wait_s_half is not None:
                t.wait_ge(SEM["sS"], V[f"sqh_{tag}"])
                t.wait_ge(SEM["sV"], wait_s_half)
                for db in range(4):
                    t.matmul(PY[0:1, 0:512], s_oncb[:], s_sq[:, db, :],
                             start=(db == 0), stop=False)
                if want_mean:
                    for db in range(4):
                        t.matmul(PX[0:1, 0:512], ones, src[:, db, :],
                                 start=(db == 0), stop=False)
                t.wait_ge(SEM["sS"], V[f"sq_{tag}"])
                last = None
                for db in range(4, DB):
                    last = t.matmul(PY[0:1, 0:512], s_oncb[:], s_sq[:, db, :],
                                    start=False, stop=(db == DB - 1))
                if want_mean:
                    for db in range(4, DB):
                        last = t.matmul(PX[0:1, 0:512], ones, src[:, db, :],
                                        start=False, stop=(db == DB - 1))
                inc(last, "sPE")
            else:
                t.wait_ge(SEM["sS"], V[f"sq_{tag}"])
                if wait_pe_v is not None:
                    t.wait_ge(SEM["sV"], wait_pe_v)
                last = None
                for db in range(DB):
                    last = t.matmul(PY[0:1, 0:512], s_oncb[:], s_sq[:, db, :],
                                    start=(db == 0), stop=(db == DB - 1))
                if want_mean:
                    for db in range(DB):
                        last = t.matmul(PX[0:1, 0:512], ones, src[:, db, :],
                                        start=(db == 0), stop=(db == DB - 1))
                inc(last, "sPE")
        V[f"st_{tag}"] = C["sPE"]

        @blk.vector
        def _(v):
            v.wait_ge(SEM["sPE"], V[f"st_{tag}"])
            if want_mean:
                v.tensor_scalar(s_row2[:], PX[0:1, 0:512], 1.0 / D, None, Alu.mult)
                v.drain()
                v.tensor_tensor(s_row[:], s_row2[:], s_row2[:], Alu.mult)
                v.drain()
                v.scalar_tensor_tensor(s_row[:], PY[0:1, 0:512], 1.0 / D,
                                       s_row[:], Alu.mult, Alu.subtract)
                v.drain()
                v.tensor_scalar(s_row[:], s_row[:], LNEPS, None, Alu.add)
            else:
                v.tensor_scalar(s_row[:], PY[0:1, 0:512], 1.0 / D, RMSEPS,
                                Alu.mult, Alu.add)
            vnop(v)
        V[f"stv_{tag}"] = C["sV"]

        @blk.scalar
        def _(s):
            s.wait_ge(SEM["sV"], V[f"stv_{tag}"])
            s.activation(s_row[:], s_row[:], Act.Ln)
            s.drain()
            inc(s.activation(s_row[:], s_row[:], Act.Exp, scale=-0.5), "sS")
        V[f"str_{tag}"] = C["sS"]

        @blk.vector
        def _(v):
            v.wait_ge(SEM["sS"], V[f"str_{tag}"])
            if want_mean:
                v.tensor_tensor(s_row2[:], s_row2[:], s_row[:], Alu.mult)
            vnop(v)
        V[f"strows_{tag}"] = C["sV"]

        @blk.tensor
        def _(t):
            t.wait_ge(SEM["sV"], V[f"strows_{tag}"])
            i1 = t.matmul(PX[:, 0:512], s_onr[:], s_row[:], start=True, stop=True)
            if want_mean:
                i1 = t.matmul(PY[:, 0:512], s_onr[:], s_row2[:],
                              start=True, stop=True)
            inc(i1, "sPE")
        V[f"stexp_{tag}"] = C["sPE"]

    def quant_steps(src_slc, dst_slc, F, tag, wait_v=None):
        nbm = F // 16
        mb = q_m[:, 0:nbm].rearrange("p (a b) -> p a b", b=2) \
            .unsqueeze(3).broadcast_to([128, F // 32, 2, 16])
        ib = q_in[:, 0:nbm].rearrange("p (a b) -> p a b", b=2) \
            .unsqueeze(3).broadcast_to([128, F // 32, 2, 16])

        def g4(ap):
            return ap.rearrange("p (a b k) -> p a b k", b=2, k=16)

        def mk(fn, first=False, last=False):
            def emit(fn=fn, first=first, last=last):
                @blk.vector
                def _(v):
                    if not first:
                        v.drain()
                    r = fn(v)
                    if last:
                        inc(r, "sV")
                        V[f"q_{tag}"] = C["sV"]
            return emit

        def s1(v):
            if wait_v is not None:
                v.wait_ge(SEM["sV"], wait_v)
            v.transpose(q_vt[:, 0:F], src_slc)

        def s2(v):
            v.tensor_reduce(q_m[:, 0:nbm], g4(q_vt[:, 0:F]), AX, Alu.max,
                            apply_absolute_value=True)

        def s3(v):
            v.tensor_scalar(q_m[:, 0:nbm], q_m[:, 0:nbm], 1.0 / 6.0, 1e-12,
                            Alu.mult, Alu.max)

        def s4(v):
            v.reciprocal(q_in[:, 0:nbm], q_m[:, 0:nbm])

        def s5(v):
            v.tensor_tensor(g4(q_vt[:, 0:F]), g4(q_vt[:, 0:F]), ib, Alu.mult)

        def s6(v):
            v.tensor_tensor(q_sq[:, 0:F], q_vt[:, 0:F], q_vt[:, 0:F], Alu.mult)
            v.tensor_scalar(q_u[:, 0:F], q_vt[:, 0:F], 0.0, 2.0,
                            Alu.is_gt, Alu.mult)

        def s7(v):
            v.tensor_scalar(q_q1[:, 0:F], q_sq[:, 0:F], 0.0625, 0.5,
                            Alu.is_gt, Alu.mult)
            v.tensor_scalar(q_h5[:, 0:F], q_sq[:, 0:F], 0.5625, 0.5,
                            Alu.is_gt, Alu.mult)
            v.tensor_scalar(q_vt[:, 0:F], q_sq[:, 0:F], 1.5625, 0.5,
                            Alu.is_gt, Alu.mult)
            v.tensor_scalar(q_u[:, 0:F], q_u[:, 0:F], -1.0, None, Alu.add)

        def s8(v):
            v.tensor_tensor(q_q1[:, 0:F], q_q1[:, 0:F], q_h5[:, 0:F], Alu.add)

        def s9(v):
            v.tensor_scalar(q_h5[:, 0:F], q_sq[:, 0:F], 3.0625, 0.5,
                            Alu.is_gt, Alu.mult)

        def s10(v):
            v.tensor_tensor(q_q1[:, 0:F], q_q1[:, 0:F], q_vt[:, 0:F], Alu.add)

        def s11(v):
            v.tensor_tensor(q_q1[:, 0:F], q_q1[:, 0:F], q_h5[:, 0:F], Alu.add)
            v.tensor_scalar(q_vt[:, 0:F], q_sq[:, 0:F], 6.25, None, Alu.is_gt)

        def s12(v):
            v.tensor_tensor(q_q1[:, 0:F], q_q1[:, 0:F], q_vt[:, 0:F], Alu.add)
            v.tensor_scalar(q_h5[:, 0:F], q_sq[:, 0:F], 12.25, None, Alu.is_gt)

        def s13(v):
            v.tensor_tensor(q_q1[:, 0:F], q_q1[:, 0:F], q_h5[:, 0:F], Alu.add)
            v.tensor_scalar(q_vt[:, 0:F], q_sq[:, 0:F], 25.0, 2.0,
                            Alu.is_gt, Alu.mult)

        def s14(v):
            v.tensor_tensor(q_q1[:, 0:F], q_q1[:, 0:F], q_vt[:, 0:F], Alu.add)

        def s15(v):
            v.tensor_tensor(q_q1[:, 0:F], q_q1[:, 0:F], q_u[:, 0:F], Alu.mult)

        def s16(v):
            v.tensor_tensor(g4(q_q1[:, 0:F]), g4(q_q1[:, 0:F]), mb, Alu.mult)

        def s17(v):
            return v.transpose(dst_slc, q_q1[:, 0:F])

        fns = [s1, s2, s3, s4, s5, s6, s7, s8, s9, s10, s11, s12, s13,
               s14, s15, s16, s17]
        return [mk(f, first=(i == 0), last=(i == len(fns) - 1))
                for i, f in enumerate(fns)]

    def emit_quant(src_slc, dst_slc, F, tag, wait_v=None):
        for st in quant_steps(src_slc, dst_slc, F, tag, wait_v):
            st()

    def emit_ug_load(k):
        @blk.sync
        def _(sy, k=k):
            sy.wait_ge(SEM["sAGu"], V[f"agu_{k}"])
            if k >= 2:
                sy.wait_ge(SEM["sV"], V[f"scn_{k-2}_{B*N16-1}"])
            src = bass.AP(ag_u_out, r_u[k],
                          [[STRIPE, 128], [DB * 128 * STRIPE, NCORE], [1, STRIPE]])
            qn = "sQ2" if k % 2 == 0 else "sQ2b"
            dmx(sy, qn, s_ug[k % 2][:], src)
        V[f"q_ug_{k}"] = (("sQ2" if k % 2 == 0 else "sQ2b"),
                          C["sQ2" if k % 2 == 0 else "sQ2b"])

    def emit_scan_iter(k, i):
        b, n = divmod(i, N16)
        par = i % 2

        @blk.vector
        def _(v, k=k, b=b, n=n, i=i, par=par):
            if i == 0:
                qs, qv = V[f"q_ug_{k}"]
                v.wait_ge(SEM[qs], qv)
            st = b * N16 + n
            init = 0.0 if k == 0 else s_state[:, st:st + 1]
            uview = s_ug[k % 2][:].rearrange("p s t -> p (s t)")
            v.tensor_tensor_scan(s_z[par][:],
                                 s_dA[:, n:n + 1].broadcast_to([128, 1024]),
                                 uview[:, 1024 * b:1024 * b + 1024],
                                 init, Alu.mult, Alu.add)
            v.drain()
            v.tensor_copy(s_state[:, st:st + 1], s_z[par][:, 1023:1024])
            if n == 0:
                last = v.tensor_scalar(s_yac2[:, b % 2, :], s_z[par][:],
                                       s_CdB[:, n:n + 1], None, Alu.mult)
            else:
                last = v.scalar_tensor_tensor(s_yac2[:, b % 2, :], s_z[par][:],
                                              s_CdB[:, n:n + 1],
                                              s_yac2[:, b % 2, :],
                                              Alu.mult, Alu.add)
            inc(last, "sV")
        V[f"scn_{k}_{i}"] = C["sV"]

    def emit_scan_flush(k, half):
        @blk.vector
        def _(v, k=k, half=half):
            if half == 0 and k > 0:
                v.wait_ge(SEM["sY"], V[f"q_yst_{k-1}"])
            v.drain()
            v.tensor_copy(s_yst2[:, 2048 * half:2048 * half + 2048],
                          s_yac2[:].rearrange("p a b -> p (a b)"))
            vnop(v)
        V[f"yfl_{k}_{half}"] = C["sV"]

    def emit_scan_store(k):
        @blk.gpsimd
        def _(g, k=k):
            g.wait_ge(SEM["sV"], V[f"yfl_{k}_1"])
            dmx(g, "sY", ag_y_in[k], s_yst2[:])
        V[f"q_yst_{k}"] = C["sY"]

        @blk.gpsimd
        def _(g, k=k):
            g.wait_ge(SEM["sY"], V[f"q_yst_{k}"])
            inc(g.collective_compute("AllGather", Alu.bypass, replica_groups=RG,
                                     ins=[ag_y_in[k]], outs=[ag_y_out[k]]), "sAGy")
        V[f"agy_{k}"] = C["sAGy"]

    scan_cur = {k: 0 for k in range(NSTR)}

    def weave_scan(k, upto):
        if k >= NSTR:
            return
        while scan_cur[k] < min(upto, B * N16):
            i = scan_cur[k]
            emit_scan_iter(k, i)
            scan_cur[k] += 1
            if i == 31:
                emit_scan_flush(k, 0)
            elif i == 63:
                emit_scan_flush(k, 1)
                emit_scan_store(k)

    def emit_p1_stripe(k):
        for r in range(4):
            idx = 4 * k + r

            @blk.sync
            def _(sy, k=k, r=r, idx=idx):
                if idx == 0:
                    pass
                if idx >= 2:
                    pk, pr = divmod(idx - 2, 4)
                    sy.wait_ge(SEM["sPE"], V[f"xT_{pk}_{pr}"])
                qn = "sQXa" if idx % 2 == 0 else "sQXb"
                dmx(sy, qn, s_xtm[idx % 2][:],
                    d_x[512 * k + 128 * r:512 * k + 128 * r + 128, :])
            V[f"q_x_{k}_{r}"] = ("sQXa" if (4 * k + r) % 2 == 0 else "sQXb",
                                 C["sQXa" if (4 * k + r) % 2 == 0 else "sQXb"])

            for db in range(DB):
                @blk.tensor
                def _(t, k=k, r=r, db=db, idx=idx):
                    if db == 0:
                        qs, qv = V[f"q_x_{k}_{r}"]
                        t.wait_ge(SEM[qs], qv)
                        if idx == 0:
                            t.wait_ge(SEM["sV"], V["v_mod"])
                        elif r == 0 and k > 0:
                            t.wait_ge(SEM["sV"], V[f"up_{k-1}"])
                            if k == 1:
                                t.wait_ge(SEM["sV"], V["shTev_wd_21"])
                    if db >= 2:
                        t.wait_ge(SEM["sV"], V[f"xev_{k}_{r}_{db-2}"])
                    elif idx > 0:
                        pk, pr = divmod(idx - 1, 4)
                        t.wait_ge(SEM["sV"], V[f"xev_{pk}_{pr}_{DB-2+db}"])
                    inc(t.transpose((PX if db % 2 == 0 else PY)[:, 0:128],
                                    s_xtm[idx % 2][:, 128 * db:128 * db + 128],
                                    s_id32[:]), "sPE")
                V[f"xTp_{k}_{r}_{db}"] = C["sPE"]

                @blk.vector
                def _(v, k=k, r=r, db=db):
                    v.wait_ge(SEM["sPE"], V[f"xTp_{k}_{r}_{db}"])
                    if k > 0 and r == 0 and db == 0:
                        v.wait_ge(SEM["sQP"], V[f"q_xsp_{k-1}"])
                    ins = v.tensor_copy(s_xfm[:, db, 128 * r:128 * r + 128],
                                        (PX if db % 2 == 0 else PY)[:, 0:128])
                    inc(ins, "sV")
                V[f"xev_{k}_{r}_{db}"] = C["sV"]
            V[f"xT_{k}_{r}"] = C["sPE"]
        V[f"xev_last_{k}"] = C["sV"]

        @blk.sync
        def _(sy, k=k):
            sy.wait_ge(SEM["sV"], V[f"xev_last_{k}"])
            dmx(sy, "sQP", h_xfm[:, :, 512 * k:512 * k + 512].transpose([1, 0, 2]), s_xfm[:])
        V[f"q_xsp_{k}"] = C["sQP"]

        emit_stats_rows(s_xfm, f"ln1_{k}", True,
                        wait_s_v=V[f"xev_last_{k}"],
                        wait_pe_v=(V[f"up_{k-1}"] if k > 0 else None),
                        src_f32=True)

        @blk.vector
        def _(v, k=k):
            v.wait_ge(SEM["sPE"], V[f"stexp_ln1_{k}"])
            pxb = PX[:, 0:512].unsqueeze(1).broadcast_to([128, DB, 512])
            pyb = PY[:, 0:512].unsqueeze(1).broadcast_to([128, DB, 512])
            v.tensor_tensor(s_t1[:], s_xfm[:], pxb, Alu.mult)
            v.drain()
            v.tensor_tensor(s_t1[:], s_t1[:], pyb, Alu.subtract)
            v.drain()
            for db in range(DB):
                v.tensor_scalar(s_h1[:, db, :], s_t1[:, db, :],
                                s_ca1[:, db:db + 1], s_sa[:, db:db + 1],
                                Alu.mult, Alu.add)
            vnop(v)
        V[f"h1_{k}"] = C["sV"]

        emit_stats_rows(s_h1, f"rms1_{k}", False,
                        wait_s_v=V[f"h1_{k}"], wait_pe_v=V[f"h1_{k}"])

        @blk.vector
        def _(v, k=k):
            v.wait_ge(SEM["sPE"], V[f"stexp_rms1_{k}"])
            if k > 0:
                v.wait_ge(SEM["sU"], V[f"q_ust_{k-1}"])
            pxb = PX[:, 0:512].unsqueeze(1).broadcast_to([128, DB, 512])
            v.tensor_tensor(s_t1[:], s_h1[:], pxb, Alu.mult)
            v.drain()
            for db in range(DB):
                v.tensor_scalar(s_u[:, db, :], s_t1[:, db, :],
                                s_s1[:, db:db + 1], None, Alu.mult)
            v.drain()
            for db in range(DB):
                v.scalar_tensor_tensor(s_pch[:, db, :], s_u[:, db, :],
                                       s_Dfm[:, db:db + 1], s_h1[:, db, :],
                                       Alu.mult, Alu.add)
            vnop(v)
        V[f"up_{k}"] = C["sV"]

        @blk.sync
        def _(sy, k=k):
            sy.wait_ge(SEM["sV"], V[f"up_{k}"])
            dmx(sy, "sU", ag_u_in[k].transpose([1, 0, 2]), s_u[:])
            dmx(sy, "sU", h_p[:, :, 512 * k:512 * k + 512].transpose([1, 0, 2]), s_pch[:])
        V[f"q_ust_{k}"] = C["sU"]

        @blk.gpsimd
        def _(g, k=k):
            g.wait_ge(SEM["sU"], V[f"q_ust_{k}"])
            inc(g.collective_compute("AllGather", Alu.bypass, replica_groups=RG,
                                     ins=[ag_u_in[k]], outs=[ag_u_out[k]]), "sAGu")
        V[f"agu_{k}"] = C["sAGu"]

        if k == 1:
            emit_ug_load(0)
            weave_scan(0, 24)
        elif k == 2:
            emit_ug_load(1)
            weave_scan(0, 48)

    emit_p1_stripe(0)

    def emit_shard_quant(n, tag, wait_pe=None):
        nb16 = n // 16
        mv = sh_m[:, 0:nb16]
        iv = sh_i[:, 0:nb16]
        mb = mv.unsqueeze(2).broadcast_to([128, nb16, 16])
        ib = iv.unsqueeze(2).broadcast_to([128, nb16, 16])

        def g3(ap):
            return ap.rearrange("p (g k) -> p g k", k=16)

        sqv = sh_w[:, 0:n]
        qb = sh_b[:, 0:n]
        tmp = sh_i1[:, 0:n]
        xn = sh_f1[:, 0:n]

        @blk.vector
        def _(v):
            v.wait_ge(SEM["sQ3"], V[f"q3_{tag}"])
            bsem, bval = V[f"shbfree_{tag}"]
            v.wait_ge(SEM[bsem], bval)
            if wait_pe is not None:
                v.wait_ge(SEM["sPE"], wait_pe)
            v.tensor_reduce(mv, g3(sh_w[:, 0:n]), AX, Alu.max,
                            apply_absolute_value=True)
            v.drain()
            v.tensor_scalar(mv, mv, 1.0 / 6.0, 1e-12, Alu.mult, Alu.max)
            v.drain()
            v.reciprocal(iv, mv)
            v.drain()
            v.tensor_tensor(g3(xn), g3(sh_w[:, 0:n]), ib, Alu.mult)
            v.drain()
            v.tensor_tensor(sqv, xn, xn, Alu.mult)
            v.drain()
            v.tensor_scalar(qb, sqv, 0.0625, 0.5, Alu.is_gt, Alu.mult)
            v.tensor_scalar(tmp, sqv, 0.5625, 0.5, Alu.is_gt, Alu.mult)
            v.tensor_scalar(xn, xn, 0.0, 2.0, Alu.is_gt, Alu.mult)
            v.drain()
            v.tensor_tensor(qb, qb, tmp, Alu.add)
            v.tensor_scalar(xn, xn, -1.0, None, Alu.add)
            v.drain()
            for m2, w in [(1.5625, 0.5), (3.0625, 0.5), (6.25, None),
                          (12.25, None), (25.0, 2.0)]:
                if w is None:
                    v.tensor_scalar(tmp, sqv, m2, None, Alu.is_gt)
                else:
                    v.tensor_scalar(tmp, sqv, m2, w, Alu.is_gt, Alu.mult)
                v.drain()
                v.tensor_tensor(qb, qb, tmp, Alu.add)
                v.drain()
            v.tensor_tensor(xn, qb, xn, Alu.mult)
            v.drain()
            inc(v.tensor_tensor(g3(xn), g3(xn), mb, Alu.mult), "sV")
        V[f"shq_{tag}"] = C["sV"]

    @blk.gpsimd
    def _(g):
        dmx(g, "sQ4", F2[:].rearrange("p (kc m) -> p kc m", kc=DB), d_w1s[:].rearrange("(kc p) m -> p kc m", p=128))
    V["q3_w1s"] = C["sQ4"]

    @blk.vector
    def _(v):
        v.wait_ge(SEM["sQ4"], V["q3_w1s"])
        v.wait_ge(SEM["sPE"], V["pe_mod5"])
        inc(v.tensor_copy(sh_b[:], F2[:]), "sV")
    V["v_w1b"] = C["sV"]

    @blk.sync
    def _(sy):
        sy.wait_ge(SEM["sV"], V["v_w1b"])
        dmx(sy, "sQW", ag_w_in[OFF_W1:OFF_W1 + SZ_W1].rearrange(
                "(kc p m) -> p kc m", p=128, m=512), sh_b[:].rearrange("p (kc m) -> p kc m", kc=DB))
    V["q_w1in"] = C["sQW"]

    @blk.gpsimd
    def _(g):
        g.wait_ge(SEM["sV"], V["v_w1b"])
        dmx(g, "sQ4", F2[:].rearrange("p (kc m) -> p kc m", kc=MC1), d_w2s[:].rearrange("(kc p) m -> p kc m", p=128))
    V["q3_w2s"] = C["sQ4"]

    @blk.vector
    def _(v):
        v.wait_ge(SEM["sQ4"], V["q3_w2s"])
        v.wait_ge(SEM["sQW"], V["q_w1in"])
        inc(v.tensor_copy(sh_b[:], F2[:]), "sV")
    V["v_w2b"] = C["sV"]

    @blk.sync
    def _(sy):
        sy.wait_ge(SEM["sV"], V["v_w2b"])
        dmx(sy, "sQW", ag_w_in[OFF_W2:OFF_W2 + SZ_W2].rearrange(
                "(kc p m) -> p kc m", p=128, m=128), sh_b[:].rearrange("p (kc m) -> p kc m", kc=MC1))
    V["q_w2in"] = C["sQW"]

    def emit_ffn_shard(d_src, off, nblk, kdim, tag, prev_w_free, prev_b_free,
                       prev_pe=None):
        n = nblk * kdim
        n_kc = kdim // 128

        @blk.gpsimd
        def _(g):
            if isinstance(prev_w_free, tuple):
                g.wait_ge(SEM[prev_w_free[0]], prev_w_free[1])
            else:
                g.wait_ge(SEM["sV"], prev_w_free)
            dmx(g, "sQ3", sh_w[:, 0:n].rearrange("p (nb k) -> p nb k", nb=nblk), d_src[:].rearrange("(nb p) k -> p nb k", p=128))
        V[f"q3_{tag}"] = C["sQ3"]
        V[f"shbfree_{tag}"] = prev_b_free
        emit_shard_quant(n, tag, wait_pe=prev_pe)

        ntile = nblk * n_kc
        for idx in range(ntile):
            nb, kc = idx // n_kc, idx % n_kc

            @blk.tensor
            def _(t, nb=nb, kc=kc, idx=idx):
                if idx == 0:
                    t.wait_ge(SEM["sV"], V[f"shq_{tag}"])
                if idx >= 2:
                    t.wait_ge(SEM["sV"], V[f"shTev_{tag}_{idx-2}"])
                src = sh_f1[:, 0:n].rearrange("p (nb k) -> p nb k", nb=nblk)
                inc(t.transpose((PX if idx % 2 == 0 else PY)[:, 0:128],
                                src[:, nb, 128 * kc:128 * kc + 128], s_id32[:]),
                    "sPE")
            V[f"shTpe_{tag}_{idx}"] = C["sPE"]

            @blk.vector
            def _(v, idx=idx):
                v.wait_ge(SEM["sPE"], V[f"shTpe_{tag}_{idx}"])
                inc(v.tensor_copy(sh_t[:, idx, :],
                                  (PX if idx % 2 == 0 else PY)[:, 0:128]), "sV")
            V[f"shTev_{tag}_{idx}"] = C["sV"]

        @blk.sync
        def _(sy):
            sy.wait_ge(SEM["sV"], V[f"shTev_{tag}_{ntile-1}"])
            shp = sh_t[:, 0:ntile, :].rearrange("p (nb kc) m -> p kc nb m",
                                                nb=nblk)
            hv = ag_w_in[off:off + n_kc * 128 * nblk * 128].rearrange(
                "(kc p nb m) -> p kc nb m", p=128, nb=nblk, m=128)
            if nblk == 1:
                dmx(sy, "sQW", hv[:, :, 0, :], shp[:, :, 0, :])
            else:
                for kc in range(n_kc):
                    dmx(sy, "sQW", hv[:, kc, :, :], shp[:, kc, :, :])
        V[f"q_{tag}in"] = C["sQW"]
        return V[f"shq_{tag}"], V[f"shq_{tag}"]

    wfree, bfree = emit_ffn_shard(d_wgs, OFF_WG, 3, 1024, "wg",
                                  ("sPE", V["pe_mod5"]), ("sQW", V["q_w2in"]))
    wfree, bfree = emit_ffn_shard(d_wus, OFF_WU, 3, 1024, "wu", wfree,
                                  ("sQW", V["q_wgin"]),
                                  prev_pe=V["shTpe_wg_23"])
    wfree, bfree = emit_ffn_shard(d_wds, OFF_WD, 1, 2816, "wd", wfree,
                                  ("sQW", V["q_wuin"]),
                                  prev_pe=V["shTpe_wu_23"])

    @blk.gpsimd
    def _(g):
        g.wait_ge(SEM["sQW"], V["q_wdin"])
        inc(g.collective_compute("AllGather", Alu.bypass, replica_groups=RG,
                                 ins=[ag_w_in[:]], outs=[ag_w_out[:]]), "sAGw")
    V["g_w"] = C["sAGw"]



    for _k1 in range(1, NSTR):
        if _k1 == 3:
            weave_scan(0, 64)
        emit_p1_stripe(_k1)

    def emit_pass_front(k):
        @blk.sync
        def _(sy, k=k):
            sy.wait_ge(SEM["sAGy"], V[f"agy_{k}"])
            if k == 0:
                sy.wait_ge(SEM["sU"], V[f"q_ust_{NSTR-1}"])
                sy.wait_ge(SEM["sPE"], V[f"xT_{NSTR-1}_3"])
                sy.wait_ge(SEM["sQW"], V["q_wdin"])
            else:
                sy.wait_ge(SEM["sPE"], V[f"gu_{k-1}_{NC2-1}"])
            src = bass.AP(ag_y_out, r_y[k],
                          [[B * 1024, 128], [128 * B * 1024, DB], [1, STRIPE]])
            dmx(sy, "sQ", s_yin[:], src)
            dmx(sy, "sQ", s_pin[:], h_p[:, :, 512 * k:512 * k + 512].transpose([1, 0, 2]))
        V[f"q_yin_{k}"] = C["sQ"]

        @blk.sync
        def _(sy, k=k):
            if k == 0:
                sy.wait_ge(SEM["sQP"], V[f"q_xsp_{NSTR-1}"])
            else:
                sy.wait_ge(SEM["sV"], V[f"ep_{k-1}_1"])
            dmx(sy, "sXC", s_xch[:], h_xfm[:, :, 512 * k:512 * k + 512].transpose([1, 0, 2]))
        V[f"q_xch_{k}"] = C["sXC"]

        @blk.vector
        def _(v, k=k):
            v.wait_ge(SEM["sQ"], V[f"q_yin_{k}"])
            v.drain()
            inc(v.tensor_tensor(s_xp[:], s_pin[:], s_yin[:], Alu.add), "sV")
        V[f"xp_{k}"] = C["sV"]

        if STAGE >= 2:
            emit_stats_rows(s_xp, f"rms2_{k}", False,
                            wait_s_v=V[f"xp_{k}"], wait_pe_v=V[f"xp_{k}"])

            @blk.vector
            def _(v, k=k):
                v.wait_ge(SEM["sPE"], V[f"stexp_rms2_{k}"])
                pxb = PX[:, 0:512].unsqueeze(1).broadcast_to([128, DB, 512])
                v.tensor_tensor(s_t1[:], s_xp[:], pxb, Alu.mult)
                v.drain()
                for db in range(DB):
                    v.tensor_scalar(s_h2[:, db, :], s_t1[:, db, :],
                                    s_s2[:, db:db + 1], None, Alu.mult)
                vnop(v)
            V[f"h2_{k}"] = C["sV"]

        if k + 2 < NSTR:
            emit_ug_load(k + 2)

    for k in range(NSTR):
        if k == 0:
            emit_pass_front(0)

        if STAGE == 1:
            @blk.vector
            def _(v, k=k):
                v.wait_ge(SEM["sV"], V[f"xp_{k}"])
                v.tensor_copy(s_x3[:], s_xp[:])
                vnop(v)
            V[f"x3r_{k}"] = C["sV"]
        else:
            w1_scan_base = scan_cur.get(k + 1, 0) if k + 1 < NSTR else 0
            for mc in range(MC1):
                @blk.sync
                def _(s, k=k, mc=mc):
                    if k == 0 and mc == 0:
                        s.wait_ge(SEM["sAGw"], V["g_w"])
                    if mc >= 2:
                        s.wait_ge(SEM["sPE"], V[f"w1mm_{k}_{mc-2}"])
                    elif k > 0:
                        s.wait_ge(SEM["sPE"], V[f"w1mm_{k-1}_{MC1-1}"])
                    rank, col = mc // 4, mc % 4
                    off = rank * WSH + OFF_W1 + col * 128
                    src = bass.AP(ag_w_out, off,
                                  [[512, 128], [128 * 512, DB], [1, 128]])
                    dmx(s, "sW1a" if mc % 2 == 0 else "sW1b", s_w1t[mc % 2][:], src)
                V[f"q_w1t_{k}_{mc}"] = ("sW1a" if mc % 2 == 0 else "sW1b",
                                        C["sW1a" if mc % 2 == 0 else "sW1b"])

                @blk.tensor
                def _(t, k=k, mc=mc):
                    qs, qv = V[f"q_w1t_{k}_{mc}"]
                    t.wait_ge(SEM[qs], qv)
                    if mc == 0:
                        t.wait_ge(SEM["sV"], V[f"h2_{k}"])
                        if k > 0 and STAGE >= 3:
                            t.wait_ge(SEM["sS"], V[f"sil_{k-1}_{NC2//2-1}"])
                            t.wait_ge(SEM["sV"], V[f"hmul_{k-1}_{NC2//2-1}"])
                    if mc >= 4:
                        t.wait_ge(SEM["sS"], V[f"gel_{k}_{mc-4}"])
                    pdst = [PA, PX, PBK, PY][mc % 4][:, 0:512]
                    last = None
                    for kc in range(DB):
                        last = t.matmul(pdst, s_w1t[mc % 2][:, kc, :],
                                        s_h2[:, kc, :], start=(kc == 0),
                                        stop=(kc == DB - 1))
                    inc(last, "sPE")
                V[f"w1mm_{k}_{mc}"] = C["sPE"]

                @blk.scalar
                def _(s, k=k, mc=mc):
                    s.wait_ge(SEM["sPE"], V[f"w1mm_{k}_{mc}"])
                    inc(s.activation(s_g[:, mc, :],
                                     [PA, PX, PBK, PY][mc % 4][:, 0:512],
                                     Act.Gelu, bias=s_b1[:, mc:mc + 1]), "sS")
                V[f"gel_{k}_{mc}"] = C["sS"]

                if k + 1 < NSTR and w1_scan_base < 56:
                    base = w1_scan_base
                    weave_scan(k + 1, base + ((56 - base) * (mc + 1)) // MC1)

            for half in range(2):
                for mc in range(MC1):
                    gidx = half * MC1 + mc

                    @blk.sync
                    def _(s, k=k, half=half, mc=mc, gidx=gidx):
                        if gidx >= 4:
                            ph, pm = divmod(gidx - 4, MC1)
                            s.wait_ge(SEM["sPE"], V[f"w2mm_{k}_{ph}_{pm}"])
                        elif k > 0 and gidx < 4:
                            s.wait_ge(SEM["sPE"], V[f"w2mm_{k-1}_{1}_{MC1-4+gidx}"])
                        qn = ["sW2a", "sW2b", "sW2c", "sW2d"][gidx % 4]
                        off = (half * 4) * WSH + OFF_W2 + mc * 128 * 128
                        src = bass.AP(ag_w_out, off,
                                      [[128, 128], [WSH, 4], [1, 128]])
                        dmx(s, qn, s_w2t[gidx % 4][:], src)
                    _q2 = ["sW2a", "sW2b", "sW2c", "sW2d"][gidx % 4]
                    V[f"q_w2t_{k}_{half}_{mc}"] = (_q2, C[_q2])

                    @blk.tensor
                    def _(t, k=k, half=half, mc=mc):
                        qs, qv = V[f"q_w2t_{k}_{half}_{mc}"]
                        t.wait_ge(SEM[qs], qv)
                        t.wait_ge(SEM["sS"], V[f"gel_{k}_{mc}"])
                        if mc == 0:
                            if half == 1:
                                t.wait_ge(SEM["sV"], V[f"ep_{k}_0"])
                            elif k > 0:
                                t.wait_ge(SEM["sV"],
                                          V[f"x3ep_{k-1}_1"] if STAGE >= 3
                                          else V[f"ep_{k-1}_1"])
                        last = None
                        gidx = half * MC1 + mc
                        for dcl in range(4):
                            last = t.matmul(PACC[dcl][:, 0:512],
                                            s_w2t[gidx % 4][:, dcl, :],
                                            s_g[:, mc, :], start=(mc == 0),
                                            stop=(mc == MC1 - 1))
                        inc(last, "sPE")
                    V[f"w2mm_{k}_{half}_{mc}"] = C["sPE"]

                @blk.vector
                def _(v, k=k, half=half):
                    v.wait_ge(SEM["sPE"], V[f"w2mm_{k}_{half}_{MC1-1}"])
                    if half == 0:
                        v.wait_ge(SEM["sXC"], V[f"q_xch_{k}"])
                    for dcl in range(4):
                        dc = half * 4 + dcl
                        v.scalar_tensor_tensor(s_t1[:, dc, :], PACC[dcl][:, 0:512],
                                               s_b2[:, dc:dc + 1], s_xp[:, dc, :],
                                               Alu.add, Alu.add)
                        v.drain()
                        v.scalar_tensor_tensor(s_x2[:, dc, :], s_t1[:, dc, :],
                                               s_ga[:, dc:dc + 1], s_xch[:, dc, :],
                                               Alu.mult, Alu.add)
                        v.drain()
                    inc(v.drain(), "sV")
                V[f"ep_{k}_{half}"] = C["sV"]

                if half == 0:
                    weave_scan(k + 1, 64)

            if STAGE == 2:
                @blk.vector
                def _(v, k=k):
                    v.wait_ge(SEM["sV"], V[f"ep_{k}_1"])
                    v.tensor_copy(s_x3[:], s_x2[:])
                    vnop(v)
                V[f"x3r_{k}"] = C["sV"]
            else:
                emit_stats_rows(s_x2, f"ln2_{k}", True,
                                wait_s_v=V[f"ep_{k}_1"], wait_pe_v=V[f"ep_{k}_1"],
                                wait_s_half=V[f"ep_{k}_0"])

                @blk.vector
                def _(v, k=k):
                    v.wait_ge(SEM["sPE"], V[f"stexp_ln2_{k}"])
                    pxb = PX[:, 0:512].unsqueeze(1).broadcast_to([128, DB, 512])
                    pyb = PY[:, 0:512].unsqueeze(1).broadcast_to([128, DB, 512])
                    v.tensor_tensor(s_t1[:], s_x2[:], pxb, Alu.mult)
                    v.drain()
                    v.tensor_tensor(s_t1[:], s_t1[:], pyb, Alu.subtract)
                    v.drain()
                    for db in range(DB):
                        v.tensor_scalar(s_h3[:, db, :], s_t1[:, db, :],
                                        s_cb1[:, db:db + 1], s_sbv[:, db:db + 1],
                                        Alu.mult, Alu.add)
                    vnop(v)
                V[f"h3_{k}"] = C["sV"]

                prev = V[f"h3_{k}"]
                for ch in range(4):
                    emit_quant(s_h3[:, 2 * ch:2 * ch + 2, :]
                               .rearrange("p a b -> p (a b)"),
                               s_xq[:, 2 * ch:2 * ch + 2, :]
                               .rearrange("p a b -> p (a b)"),
                               2 * STRIPE, f"xq_{k}_{ch}", wait_v=prev)
                    prev = V[f"q_xq_{k}_{ch}"]
                V[f"xq_{k}"] = prev

                hq_queue = []
                for nc_ in range(NC2):
                    @blk.sync
                    def _(s, k=k, nc_=nc_):
                        if nc_ >= 2:
                            s.wait_ge(SEM["sPE"], V[f"gu_{k}_{nc_-2}"])
                        rank, col = nc_ // 3, nc_ % 3
                        offg = rank * WSH + OFF_WG + col * 128
                        offu = rank * WSH + OFF_WU + col * 128
                        srcg = bass.AP(ag_w_out, offg,
                                       [[384, 128], [128 * 384, DB], [1, 128]])
                        srcu = bass.AP(ag_w_out, offu,
                                       [[384, 128], [128 * 384, DB], [1, 128]])
                        qn = "sGUa" if nc_ % 2 == 0 else "sGUb"
                        dmx(s, qn, s_wgt[nc_ % 2][:], srcg)
                        dmx(s, qn, s_wut[nc_ % 2][:], srcu)
                    _qg = "sGUa" if nc_ % 2 == 0 else "sGUb"
                    V[f"q_gu_{k}_{nc_}"] = (_qg, C[_qg])

                    @blk.tensor
                    def _(t, k=k, nc_=nc_):
                        qs, qv = V[f"q_gu_{k}_{nc_}"]
                        t.wait_ge(SEM[qs], qv)
                        p = nc_ // 2
                        if nc_ == 0:
                            t.wait_ge(SEM["sV"], V[f"q_xq_{k}_0"])
                            t.wait_ge(SEM["sS"], V[f"gel_{k}_{MC1-1}"])
                        if nc_ % 2 == 0 and p >= 2:
                            t.wait_ge(SEM["sV"], V[f"hmul_{k}_{p-2}"])
                            t.wait_ge(SEM["sS"], V[f"sil_{k}_{p-2}"])
                        grp = PG1 if p % 2 == 0 else PG2
                        go = 512 * (nc_ % 2)
                        last = None
                        for kc in range(DB):
                            if nc_ == 0 and kc in (2, 4, 6):
                                t.wait_ge(SEM["sV"], V[f"q_xq_{k}_{kc//2}"])
                            last = t.matmul(grp[:, go:go + 512],
                                            s_wgt[nc_ % 2][:, kc, :],
                                            s_xq[:, kc, :], start=(kc == 0),
                                            stop=(kc == DB - 1))
                        for kc in range(DB):
                            last = t.matmul(grp[:, 1024 + go:1536 + go],
                                            s_wut[nc_ % 2][:, kc, :],
                                            s_xq[:, kc, :], start=(kc == 0),
                                            stop=(kc == DB - 1))
                        inc(last, "sPE")
                    V[f"gu_{k}_{nc_}"] = C["sPE"]

                    if nc_ % 2 == 1:
                        @blk.scalar
                        def _(s, k=k, nc_=nc_):
                            s.wait_ge(SEM["sPE"], V[f"gu_{k}_{nc_}"])
                            p = nc_ // 2
                            if p >= 1:
                                s.wait_ge(SEM["sV"], V[f"hmul_{k}_{p-1}"])
                            elif k > 0:
                                s.wait_ge(SEM["sV"], V[f"hmul_{k-1}_{NC2//2-1}"])
                            grp = PG1 if (nc_ // 2) % 2 == 0 else PG2
                            inc(s.activation(s_sg2[:], grp[:, 0:1024],
                                             Act.Silu), "sS")
                        V[f"sil_{k}_{nc_//2}"] = C["sS"]

                        @blk.vector
                        def _(v, k=k, nc_=nc_):
                            v.wait_ge(SEM["sS"], V[f"sil_{k}_{nc_//2}"])
                            grp = PG1 if (nc_ // 2) % 2 == 0 else PG2
                            inc(v.tensor_tensor(s_hq[:, nc_ - 1:nc_ + 1, :]
                                                .rearrange("p a b -> p (a b)"),
                                                s_sg2[:], grp[:, 1024:2048],
                                                Alu.mult), "sV")
                        V[f"hmul_{k}_{nc_//2}"] = C["sV"]

                        p = nc_ // 2
                        if p == 3:
                            hs = s_hq[:, 0:6, :].rearrange("p a b -> p (a b)")
                            hq_queue.extend(quant_steps(
                                hs, hs, 6 * STRIPE, f"hq_{k}_0",
                                wait_v=V[f"hmul_{k}_{2}"]))
                        elif p == 6:
                            hs = s_hq[:, 6:12, :].rearrange("p a b -> p (a b)")
                            hq_queue.extend(quant_steps(
                                hs, hs, 6 * STRIPE, f"hq_{k}_1",
                                wait_v=V[f"hmul_{k}_{5}"]))
                        elif p == 9:
                            hs = s_hq[:, 12:17, :].rearrange("p a b -> p (a b)")
                            hq_queue.extend(quant_steps(
                                hs, hs, 5 * STRIPE, f"hq_{k}_2",
                                wait_v=V[f"hmul_{k}_{8}"]))
                        if p >= 4:
                            for _ in range(5):
                                if hq_queue:
                                    hq_queue.pop(0)()

                while hq_queue:
                    hq_queue.pop(0)()
                V[f"hq01_{k}"] = V[f"q_hq_{k}_2"]
                hs = s_hq[:, 17:22, :].rearrange("p a b -> p (a b)")
                for st in quant_steps(hs, hs, 5 * STRIPE, f"hq_{k}_3",
                                      wait_v=V[f"hmul_{k}_{NC2//2-1}"]):
                    st()
                V[f"hq_{k}"] = V[f"q_hq_{k}_3"]

                if k + 1 < NSTR:
                    emit_pass_front(k + 1)

                for half in range(2):
                    for dcl in range(4):
                        dc = half * 4 + dcl

                        @blk.sync
                        def _(s, k=k, dc=dc):
                            if dc >= 2:
                                s.wait_ge(SEM["sPE"], V[f"dn_{k}_{dc-2}"])
                            off = dc * WSH + OFF_WD
                            src = bass.AP(ag_w_out, off,
                                          [[128, 128], [128 * 128, NC2], [1, 128]])
                            qn = "sWDa" if dc % 2 == 0 else "sWDb"
                            dmx(s, qn, s_wdt[dc % 2][:], src)
                        _qd = "sWDa" if dc % 2 == 0 else "sWDb"
                        V[f"q_wd_{k}_{dc}"] = (_qd, C[_qd])

                        @blk.tensor
                        def _(t, k=k, dc=dc, dcl=dcl, half=half):
                            qs, qv = V[f"q_wd_{k}_{dc}"]
                            t.wait_ge(SEM[qs], qv)
                            if dc == 0:
                                t.wait_ge(SEM["sV"], V[f"hq01_{k}"])
                            if half == 1 and dcl == 0:
                                t.wait_ge(SEM["sV"], V[f"x3ep_{k}_0"])
                            last = None
                            for kc in range(NC2):
                                if dc == 0 and kc == 17:
                                    t.wait_ge(SEM["sV"], V[f"hq_{k}"])
                                last = t.matmul(PACC[dcl][:, 0:512],
                                                s_wdt[dc % 2][:, kc, :],
                                                s_hq[:, kc, :], start=(kc == 0),
                                                stop=(kc == NC2 - 1))
                            inc(last, "sPE")
                        V[f"dn_{k}_{dc}"] = C["sPE"]

                        weave_scan(k + 2, 2 * (dc + 1))

                    @blk.vector
                    def _(v, k=k, half=half):
                        v.wait_ge(SEM["sPE"], V[f"dn_{k}_{half*4+3}"])
                        for dcl in range(4):
                            dc = half * 4 + dcl
                            v.scalar_tensor_tensor(s_x3[:, dc, :],
                                                   PACC[dcl][:, 0:512],
                                                   s_gb[:, dc:dc + 1],
                                                   s_x2[:, dc, :],
                                                   Alu.mult, Alu.add)
                            v.drain()
                        inc(v.drain(), "sV")
                    V[f"x3ep_{k}_{half}"] = C["sV"]

                @blk.vector
                def _(v, k=k):
                    v.wait_ge(SEM["sV"], V[f"x3ep_{k}_1"])
                    vnop(v)
                V[f"x3r_{k}"] = C["sV"]

        for r in range(4):
            for db in range(DB):
                @blk.tensor
                def _(t, k=k, r=r, db=db):
                    if r == 0 and db == 0:
                        t.wait_ge(SEM["sV"], V[f"x3r_{k}"])
                    oidx = r * DB + db
                    if oidx >= 2:
                        pr, pdb = divmod(oidx - 2, DB)
                        t.wait_ge(SEM["sV"], V[f"oev_{k}_{pr}_{pdb}"])
                    inc(t.transpose((PX if db % 2 == 0 else PY)[:, 0:128],
                                    s_x3[:, db, 128 * r:128 * r + 128], s_id32[:]),
                        "sPE")
                V[f"oT_{k}_{r}_{db}"] = C["sPE"]

                @blk.vector
                def _(v, k=k, r=r, db=db):
                    v.wait_ge(SEM["sPE"], V[f"oT_{k}_{r}_{db}"])
                    gidx = 4 * k + r
                    if db == 0 and gidx >= 2:
                        pk, pr = divmod(gidx - 2, 4)
                        qs, qv = V[f"q_out_{pk}_{pr}"]
                        v.wait_ge(SEM[qs], qv)
                    inc(v.tensor_copy(s_otm[r % 2][:, 128 * db:128 * db + 128],
                                      (PX if db % 2 == 0 else PY)[:, 0:128]), "sV")
                V[f"oev_{k}_{r}_{db}"] = C["sV"]

            weave_scan(k + 2, 16 + (r + 1))

            @blk.sync
            def _(sy, k=k, r=r):
                sy.wait_ge(SEM["sV"], V[f"oev_{k}_{r}_{DB-1}"])
                dmx(sy, "sOa" if r % 2 == 0 else "sOb",
                    d_out[512 * k + 128 * r:512 * k + 128 * r + 128, :],
                    s_otm[r % 2][:])
            _qo = "sOa" if r % 2 == 0 else "sOb"
            V[f"q_out_{k}_{r}"] = (_qo, C[_qo])

        @blk.vector
        def _(v, k=k):
            vnop(v)
        V[f"pass_done_{k}"] = C["sV"]

    @blk.sync
    def _(sy):
        for n in ["sQ", "sQS", "sU", "sY", "sOa", "sOb", "sQ2", "sQ2b", "sQ3", "sQW",
              "sQP", "sQXa", "sQXb",
                  "sW1a", "sW1b", "sW2a", "sW2b", "sW2c", "sW2d",
                  "sGUa", "sGUb", "sWDa", "sWDb", "sXC", "sQ4"]:
            if C[n]:
                sy.wait_ge(SEM[n], C[n])

    es.close()
    return nc


_NC_CACHE = None


def _get_nc():
    global _NC_CACHE
    if _NC_CACHE is None:
        _NC_CACHE = build_nc()
    return _NC_CACHE


def make_in_maps(inputs):
    f32 = np.float32
    x = np.asarray(inputs["L0_attn_x"], f32)
    A = np.asarray(inputs["L0_attn_A_ssm"], f32)
    Bm = np.asarray(inputs["L0_attn_B_ssm"], f32)
    Cm = np.asarray(inputs["L0_attn_C_ssm"], f32)
    Dv = np.asarray(inputs["L0_attn_D_ssm"], f32)
    s1 = np.asarray(inputs["L0_attn_scale1"], f32)
    s2 = np.asarray(inputs["L0_attn_scale2"], f32)
    w1 = np.asarray(inputs["L0_attn_w1"], f32)
    b1 = np.asarray(inputs["L0_attn_b1"], f32)
    w2 = np.asarray(inputs["L0_attn_w2"], f32)
    b2 = np.asarray(inputs["L0_attn_b2"], f32)
    wg = np.asarray(inputs["L0_ffn_w_gate"], f32)
    wu = np.asarray(inputs["L0_ffn_w_up"], f32)
    wd = np.asarray(inputs["L0_ffn_w_down"], f32)
    cc = np.asarray(inputs["L0_c"], f32)
    wmod = np.asarray(inputs["L0_wmod"], f32)

    dt = np.exp(np.linspace(np.log(0.001), np.log(0.1), N16)).astype(f32)
    dt_mat = np.ascontiguousarray(np.broadcast_to(dt[None, :], (128, N16)))

    def fmvec(v):
        return np.ascontiguousarray(v.reshape(DB, 128).T)

    wg_pad = np.zeros((384 * NCORE, D), f32)
    wg_pad[:DFF2] = wg
    wu_pad = np.zeros((384 * NCORE, D), f32)
    wu_pad[:DFF2] = wu

    ident = np.eye(128, dtype=f32)
    identb = np.eye(128, dtype=ml_dtypes.bfloat16)
    ones_col = np.ones((128, 1), f32)
    ones_row = np.ones((1, 128), f32)

    in_maps = []
    for c in range(NCORE):
        b = c // 2
        half = c % 2
        rows = np.concatenate([np.arange(1024 * kk + 512 * half,
                                         1024 * kk + 512 * half + 512)
                               for kk in range(NSTR)])
        sel = np.zeros((128, B), f32)
        sel[:, b] = 1.0
        offs = np.zeros((1, 8), np.int32)
        for kk in range(NSTR):
            offs[0, kk] = kk * (NCORE * DB * 128 * STRIPE) + c * (128 * STRIPE)
            offs[0, 4 + kk] = kk * (NCORE * 128 * B * 1024) + b * 1024 + half * 512
        in_maps.append({
            "x_own": np.ascontiguousarray(x[b][rows]),
            "A_sl": np.ascontiguousarray(A[128 * c:128 * c + 128]),
            "B_sl": np.ascontiguousarray(Bm[128 * c:128 * c + 128]),
            "C_sl": np.ascontiguousarray(Cm[128 * c:128 * c + 128]),
            "dt_mat": dt_mat,
            "D_fm": fmvec(Dv), "s1_fm": fmvec(s1), "s2_fm": fmvec(s2),
            "b2_fm": fmvec(b2),
            "b1_fm": np.ascontiguousarray(b1.reshape(MC1, 128).T),
            "cT_fm": np.ascontiguousarray(
                cc.T.reshape(DB, 128, B).transpose(1, 0, 2)),
            "wmod_sl": np.ascontiguousarray(wmod[:, 768 * c:768 * c + 768]),
            "sel": sel,
            "w1_sh": np.ascontiguousarray(w1[:, 512 * c:512 * c + 512]),
            "w2_sh": np.ascontiguousarray(w2[:, 128 * c:128 * c + 128]),
            "wg_sh": np.ascontiguousarray(wg_pad[384 * c:384 * c + 384]),
            "wu_sh": np.ascontiguousarray(wu_pad[384 * c:384 * c + 384]),
            "wd_sh": np.ascontiguousarray(wd[128 * c:128 * c + 128]),
            "ident": ident, "identb": identb,
            "ones_col": ones_col, "ones_row": ones_row,
            "offs": offs,
        })
    return in_maps


def assemble(results):
    out = np.zeros((B, T, D), np.float32)
    for c in range(NCORE):
        b = c // 2
        half = c % 2
        rows = np.concatenate([np.arange(1024 * kk + 512 * half,
                                         1024 * kk + 512 * half + 512)
                               for kk in range(NSTR)])
        out[b][rows] = np.asarray(results[c]["out_own"], np.float32)
    return out


def kernel(**inputs):
    nc = _get_nc()
    in_maps = make_in_maps(inputs)
    res = run_bass_kernel_spmd(nc, in_maps, core_ids=list(range(NCORE)))
    return assemble(res.results)
